# version 06
# speedup vs baseline: 2.0265x; 1.4240x over previous
import numpy as np
from contextlib import ExitStack
import concourse.bass as bass
import concourse.mybir as mybir
from concourse.bass_utils import run_bass_kernel_spmd

F32 = mybir.dt.float32
BF16 = mybir.dt.bfloat16
AF = mybir.ActivationFunctionType
ALU = mybir.AluOpType
AX = mybir.AxisListType

D = 1024
S = 4096
B = 4
MY = 2048
NT = S // 128
ND = D // 128
FH = D // 2 + 1
FPAD = 1152
NFK = FPAD // 128
LN_EPS = 1e-5



def _consts_stage1():
    e = np.arange(D, dtype=np.float64)[:, None]
    f = np.arange(FH, dtype=np.float64)[None, :]
    ang = 2.0 * np.pi * e * f / D
    Fm = np.zeros((D, FPAD), dtype=np.float64)
    Fm[:, :FH] = np.cos(ang)
    Fm[:, FH:2 * FH] = -np.sin(ang)
    c = np.full((FH,), 2.0); c[0] = 1.0; c[FH - 1] = 1.0
    f2 = np.arange(FH, dtype=np.float64)[:, None]
    d2 = np.arange(D, dtype=np.float64)[None, :]
    ang2 = 2.0 * np.pi * f2 * d2 / D
    Gm = np.zeros((FPAD, D), dtype=np.float64)
    Gm[:FH] = c[:, None] * np.cos(ang2) / D
    Gm[FH:2 * FH] = -c[:, None] * np.sin(ang2) / D
    LTT = np.triu(np.ones((128, 128), dtype=np.float64))
    return {
        "cF": Fm.astype(np.float32),
        "cG": Gm.astype(np.float32),
        "cLTT": LTT.astype(np.float32),
        "cIdent": np.eye(128, dtype=np.float32),
        "cOnesRow": np.ones((1, 128), dtype=np.float32),
        "cOnesCol": np.ones((128, 1), dtype=np.float32),
    }



ENGINES = ("sync", "gpsimd", "vector", "tensor", "scalar")

_NO_DRAIN = {"wait_ge", "drain", "dma_start", "indirect_dma_start", "sem_inc",
             "memzero"}


class AutoDrain:

    def __init__(self, e):
        self._e = e

    @property
    def raw(self):
        return self._e

    def __getattr__(self, name):
        attr = getattr(self._e, name)
        if name.startswith("_") or name in _NO_DRAIN or not callable(attr):
            return attr
        def wrapped(*a, **k):
            ins = attr(*a, **k)
            self._e.drain()
            return ins
        return wrapped


def bt_wait(E, bev, key, eng="tensor"):
    if key in bev:
        cn, v = bev[key]
        E.wait(eng, cn, v)


def bt_set(bev, key, cn, v):
    bev[key] = (cn, v)


class Prog:
    def __init__(self):
        self.ops = {k: [] for k in ENGINES}

    def add(self, eng, fn):
        self.ops[eng].append(fn)

    def play(self, blk):
        @blk.sync
        def _(e):
            for f in self.ops["sync"]:
                f(e)

        @blk.gpsimd
        def _(e):
            w = AutoDrain(e)
            for f in self.ops["gpsimd"]:
                f(w)

        @blk.vector
        def _(e):
            w = AutoDrain(e)
            for f in self.ops["vector"]:
                f(w)

        @blk.tensor
        def _(e):
            for f in self.ops["tensor"]:
                f(e)

        @blk.scalar
        def _(e):
            w = AutoDrain(e)
            for f in self.ops["scalar"]:
                f(w)


class Cnt:
    def __init__(self, sem):
        self.sem = sem
        self.val = 0


class Emit:

    def __init__(self, prog, sems, nc=None, ctx=None):
        self.p = prog
        self.c = {k: Cnt(v) for k, v in sems.items()}
        self.nc = nc
        self.ctx = ctx

    def dma_group(self, name):
        if name not in self.c:
            sem = self.ctx.enter_context(self.nc.semaphore("sem_" + name))
            self.c[name] = Cnt(sem)
        return name

    def on(self, eng, fn):
        self.p.add(eng, fn)

    def wait(self, eng, cname, val=None):
        cnt = self.c[cname]
        v = cnt.val if val is None else val
        if v > 0:
            self.p.add(eng, lambda e, s=cnt.sem, v=v: e.wait_ge(s, v))
        return v

    def milestone(self, eng, cname):
        cnt = self.c[cname]
        cnt.val += 1
        v = cnt.val
        self.p.add(eng, lambda e, s=cnt.sem: e.drain().then_inc(s, 1))
        return v

    def dma(self, eng, cname, fn):
        self.dma_group(cname)
        cnt = self.c[cname]
        cnt.val += 16
        def run(e, fn=fn, s=cnt.sem):
            fn(e).then_inc(s, 16)
        self.p.add(eng, run)
        return cnt.val



def build_stage1(dbg=False, F=None):
    nc = F["nc"] if F else bass.Bass("TRN2", target_bir_lowering=False)
    x_h = F["x"] if F else nc.dram_tensor("x", [S, D], F32, kind="ExternalInput").ap()
    wq_h = F["Wq"] if F else nc.dram_tensor("Wq", [D, D], F32, kind="ExternalInput").ap()
    wk_h = F["Wk"] if F else nc.dram_tensor("Wk", [D, D], F32, kind="ExternalInput").ap()
    wv_h = F["Wv"] if F else nc.dram_tensor("Wv", [D, D], F32, kind="ExternalInput").ap()
    cF_h = F["cF"] if F else nc.dram_tensor("cF", [D, FPAD], F32, kind="ExternalInput").ap()
    cG_h = F["cG"] if F else nc.dram_tensor("cG", [FPAD, D], F32, kind="ExternalInput").ap()
    cLTT_h = F["cLTT"] if F else nc.dram_tensor("cLTT", [128, 128], F32, kind="ExternalInput").ap()
    cId_h = F["cIdent"] if F else nc.dram_tensor("cIdent", [128, 128], F32, kind="ExternalInput").ap()
    cOC_h = F["cOnesCol"] if F else nc.dram_tensor("cOnesCol", [128, 1], F32, kind="ExternalInput").ap()
    cOR_h = F["cOnesRow"] if F else nc.dram_tensor("cOnesRow", [1, 128], F32, kind="ExternalInput").ap()
    out_h = F["out"] if F else nc.dram_tensor("out", [S, D], F32, kind="ExternalOutput").ap()
    if dbg:
        dbg_h = {k: nc.dram_tensor("dbg_" + k, [128, FPAD], F32,
                                   kind="ExternalOutput").ap()
                 for k in ["fk", "fv", "fkn", "prod", "fkv", "fq", "z", "acc0", "xT0",
                           "vh", "zT0", "cG0", "xin0"]}

    with ExitStack() as ctx:
        def sb(name, shape, dtype):
            return ctx.enter_context(nc.sbuf_tensor(name, shape, dtype))
        def psum(name, shape, dtype):
            return ctx.enter_context(nc.psum_tensor(name, shape, dtype))
        def sem(name):
            return ctx.enter_context(nc.semaphore(name))
        xT = sb("xT", [128, ND * S], BF16)
        wF = sb("wF", [128, 3 * ND * FPAD], BF16)
        cGs = sb("cGs", [128, NFK * D], BF16)
        ident = sb("ident", [128, 128], BF16)
        ltt = sb("ltt", [128, 128], BF16)
        onesc = sb("onesc", [128, 1], BF16)
        onesr = sb("onesr", [1, 128], BF16)
        fk = sb("fk", [128, FPAD], F32)
        fv = sb("fv", [128, FPAD], F32)
        fq = sb("fq", [128, FPAD], F32)
        sqb = sb("sq", [128, FH], F32)
        mag = sb("mag", [128, FH], F32)
        prodb = sb("prod", [128, FPAD], BF16)
        fkv = sb("fkv", [128, FPAD], F32)
        carry = sb("carry", [1, FPAD], BF16)
        zt = sb("zt", [128, FPAD], BF16)
        zT = sb("zT", [128, NFK * 128], BF16)
        acc = sb("acc", [128, D], F32)
        m1 = sb("m1", [128, 1], F32)
        m2 = sb("m2", [128, 1], F32)
        epscol = sb("epscol", [128, 1], F32)
        fold_ctx = ExitStack()
        cFs = fold_ctx.enter_context(nc.sbuf_tensor("cFs", [128, ND * FPAD], BF16))
        wtmp = fold_ctx.enter_context(nc.sbuf_tensor("wtmp", [128, ND * D], BF16))
        pA = psum("pA", [128, 512], F32)
        pB = psum("pB", [128, 512], F32)
        pC = psum("pC", [128, 512], F32)
        pD = psum("pD", [128, 512], F32)
        pE = psum("pE", [128, 512], F32)
        pF = psum("pF", [128, 512], F32)
        pT1 = psum("pT1", [128, 128], BF16)
        pT2 = psum("pT2", [128, 128], BF16)
        if F:
            E = F["E"]; p = F["p"]
        else:
            s_t0 = sem("s_t"); s_v0 = sem("s_v"); s_s0 = sem("s_s"); s_g0 = sem("s_g")
            p = Prog()
            E = Emit(p, {"t": s_t0, "v": s_v0, "s": s_s0, "g": s_g0}, nc=nc, ctx=ctx)
        s_t = E.c["t"].sem; s_v = E.c["v"].sem; s_s = E.c["s"].sem; s_g = E.c["g"].sem
        E.dma("gpsimd", "gC", lambda e: e.dma_start(out=ident[:], in_=cId_h[:]))
        E.dma("gpsimd", "gC", lambda e: e.dma_start(out=ltt[:], in_=cLTT_h[:]))
        E.dma("gpsimd", "gC", lambda e: e.dma_start(out=onesc[:], in_=cOC_h[:]))
        E.dma("gpsimd", "gC", lambda e: e.dma_start(out=onesr[:], in_=cOR_h[:]))
        E.dma("gpsimd", "gC", lambda e: e.dma_start(
            out=cFs[:].rearrange("p (n m) -> p n m", n=ND),
            in_=cF_h.rearrange("(n p) m -> p n m", p=128)))
        E.dma("gpsimd", "gC", lambda e: e.dma_start(
            out=cGs[:].rearrange("p (n m) -> p n m", n=NFK),
            in_=cG_h.rearrange("(n p) m -> p n m", p=128)))
        ld_consts = E.c["gC"].val

        def zinit(e):
            e.memset(prodb[:, 2 * FH:FPAD], 0.0)
            e.memset(carry[:, :], 0.0)
            e.memset(zt[:, 2 * FH:FPAD], 0.0)
            e.memset(fkv[:, 2 * FH:FPAD], 0.0)
            e.memset(epscol[:], LN_EPS)
        E.on("vector", zinit)
        E.milestone("vector", "v")

        bev1 = {}
        for wi, w_h in enumerate((wq_h, wk_h, wv_h)):
            ld_w = E.dma("gpsimd", "gC", lambda e, w_h=w_h: e.dma_start(
                out=wtmp[:].rearrange("p (n m) -> p n m", n=ND),
                in_=w_h.rearrange("(n p) m -> p n m", p=128)))
            for dt in range(ND):
                psg = ((pA, pB, pC), (pD, pE, pF))[dt % 2]
                bt_wait(E, bev1, "f%d" % (dt % 2))
                for ch in range(3):
                    n0 = ch * 512
                    n1 = min(n0 + 512, FPAD)
                    ps = psg[ch]
                    def mm_fold(e, dt=dt, n0=n0, n1=n1, ps=ps, ld_w=ld_w):
                        e.wait_ge(E.c["gC"].sem, max(ld_w, ld_consts))
                        for kt in range(ND):
                            e.matmul(
                                ps[:, 0:n1 - n0],
                                wtmp[:, kt * D + dt * 128:kt * D + (dt + 1) * 128],
                                cFs[:, kt * FPAD + n0:kt * FPAD + n1],
                                start=(kt == 0), stop=(kt == ND - 1))
                    E.on("tensor", mm_fold)
                tms = E.milestone("tensor", "t")
                E.wait("vector", "t", tms)
                def ev_fold(e, wi=wi, dt=dt, psg=psg):
                    base = (wi * ND + dt) * FPAD
                    e.tensor_copy(wF[:, base:base + 512], psg[0][:, :])
                    e.tensor_copy(wF[:, base + 512:base + 1024], psg[1][:, :])
                    e.tensor_copy(wF[:, base + 1024:base + FPAD],
                                  psg[2][:, 0:FPAD - 1024])
                E.on("vector", ev_fold)
                bt_set(bev1, "f%d" % (dt % 2), "v", E.milestone("vector", "v"))
            if wi < 2:
                E.wait("gpsimd", "t", E.c["t"].val)
        folds_v = E.c["v"].val
        folds_t = E.c["t"].val
        fold_ctx.close()
        xin = sb("xin", [128, 2 * D], F32)
        xinb = sb("xinb", [128, 2 * D], BF16)

        E.wait("sync", "t", folds_t)
        cast_v_ms = {}
        trans_t_ms = {}
        for tt in range(NT):
            bo = (tt % 2) * D
            if tt >= 2:
                E.wait("sync", "v", cast_v_ms[tt - 2])
            grpn = ("xA", "xB")[tt % 2]
            ldv = E.dma("sync", grpn, lambda e, tt=tt, bo=bo: e.dma_start(
                out=xin[:, bo:bo + D], in_=x_h[tt * 128:(tt + 1) * 128, :]))
            E.wait("vector", grpn, ldv)
            if tt >= 2:
                E.wait("vector", "t", trans_t_ms[tt - 2])
            E.on("vector", lambda e, bo=bo: e.tensor_copy(xinb[:, bo:bo + D],
                                                          xin[:, bo:bo + D]))
            cast_v_ms[tt] = E.milestone("vector", "v")
            E.wait("tensor", "v", max(cast_v_ms[tt], folds_v))
            for dt in range(ND):
                pt = (pT1, pT2)[dt % 2]
                bt_wait(E, bev1, "pT%d" % (dt % 2))
                def tr1(e, bo=bo, dt=dt, pt=pt):
                    e.transpose(pt[:], xinb[:, bo + dt * 128:bo + (dt + 1) * 128],
                                ident[:])
                E.on("tensor", tr1)
                tms = E.milestone("tensor", "t")
                E.wait("vector", "t", tms)
                def ev1(e, tt=tt, dt=dt, pt=pt):
                    e.tensor_copy(xT[:, dt * S + tt * 128:dt * S + (tt + 1) * 128],
                                  pt[:])
                E.on("vector", ev1)
                bt_set(bev1, "pT%d" % (dt % 2), "v", E.milestone("vector", "v"))
            trans_t_ms[tt] = E.c["t"].val
        xT_done_v = E.c["v"].val

        def emit_proj(dst_ps, w_idx, tt):
            for ch in range(3):
                n0 = ch * 512
                n1 = min(n0 + 512, FPAD)
                ps = dst_ps[ch]
                def mm(e, w_idx=w_idx, tt=tt, n0=n0, n1=n1, ps=ps):
                    for kt in range(ND):
                        e.matmul(
                            ps[:, 0:n1 - n0],
                            xT[:, kt * S + tt * 128:kt * S + (tt + 1) * 128],
                            wF[:, (w_idx * ND + kt) * FPAD + n0:
                               (w_idx * ND + kt) * FPAD + n1],
                            start=(kt == 0), stop=(kt == ND - 1))
                E.on("tensor", mm)

        def emit_normalize(dst):
            def v1(e, dst=dst):
                e.tensor_mul(mag[:, 0:FH], dst[:, 0:FH], dst[:, 0:FH])
                e.tensor_mul(sqb[:, 0:FH], dst[:, FH:2 * FH], dst[:, FH:2 * FH])
                e.tensor_add(mag[:, 0:FH], mag[:, 0:FH], sqb[:, 0:FH])
            E.on("vector", v1)
            vm = E.milestone("vector", "v")
            E.wait("scalar", "v", vm)
            E.on("scalar", lambda e: e.activation(mag[:, 0:FH], mag[:, 0:FH],
                                                  AF.Sqrt))
            sm = E.milestone("scalar", "s")
            E.wait("vector", "s", sm)
            def v2(e, dst=dst):
                e.tensor_scalar_add(mag[:, 0:FH], mag[:, 0:FH], 1e-8)
                e.reciprocal(mag[:, 0:FH], mag[:, 0:FH])
                e.tensor_mul(dst[:, 0:FH], dst[:, 0:FH], mag[:, 0:FH])
                e.tensor_mul(dst[:, FH:2 * FH], dst[:, FH:2 * FH], mag[:, 0:FH])
            E.on("vector", v2)

        def dump(name, src_ap, ms_eng="v"):
            if not dbg:
                return
            ms = E.c[ms_eng].val
            E.on("gpsimd", lambda e, ms=ms: e.wait_ge(E.c[ms_eng].sem, ms))
            if "db" in E.c and E.c["db"].val:
                E.on("gpsimd", lambda e, v=E.c["db"].val: e.wait_ge(E.c["db"].sem, v))
            E.dma("gpsimd", "db", lambda e, name=name, src_ap=src_ap:
                  e.dma_start(out=dbg_h[name][:, 0:src_ap.shape[-1]], in_=src_ap))
            for eng in ("vector", "scalar", "sync"):
                E.wait(eng, "db", E.c["db"].val)

        if dbg:
            E.wait("sync", "v", xT_done_v)
            E.on("gpsimd", lambda e: e.wait_ge(s_v, xT_done_v))
            E.dma("gpsimd", "db", lambda e: e.dma_start(out=dbg_h["xT0"][:, 0:128],
                                                        in_=xT[:, 0:128]))
        E.wait("tensor", "v", xT_done_v)
        res_v_ms = {}
        for tt in range(NT):
            emit_proj((pA, pB, pC), 1, tt)
            emit_proj((pD, pE, pF), 2, tt)
            t_kv = E.milestone("tensor", "t")
            E.wait("vector", "t", t_kv)
            def ev_kv(e):
                e.tensor_copy(fk[:, 0:512], pA[:, :])
                e.tensor_copy(fk[:, 512:1024], pB[:, :])
                e.tensor_copy(fk[:, 1024:FPAD], pC[:, 0:FPAD - 1024])
                e.tensor_copy(fv[:, 0:512], pD[:, :])
                e.tensor_copy(fv[:, 512:1024], pE[:, :])
                e.tensor_copy(fv[:, 1024:FPAD], pF[:, 0:FPAD - 1024])
            E.on("vector", ev_kv)
            v_kv = E.milestone("vector", "v")
            if tt == 0:
                dump("fk", fk[:, 0:FPAD])
                dump("fv", fv[:, 0:FPAD])
            E.wait("tensor", "v", v_kv)
            emit_proj((pA, pB, pC), 0, tt)
            t_q = E.milestone("tensor", "t")

            emit_normalize(fk)
            def v_prod(e):
                e.tensor_mul(sqb[:, 0:FH], fk[:, FH:2 * FH], fv[:, FH:2 * FH])
                e.tensor_mul(mag[:, 0:FH], fk[:, 0:FH], fv[:, 0:FH])
                e.tensor_sub(prodb[:, 0:FH], mag[:, 0:FH], sqb[:, 0:FH])
                e.tensor_mul(sqb[:, 0:FH], fk[:, 0:FH], fv[:, FH:2 * FH])
                e.tensor_mul(mag[:, 0:FH], fk[:, FH:2 * FH], fv[:, 0:FH])
                e.tensor_add(prodb[:, FH:2 * FH], sqb[:, 0:FH], mag[:, 0:FH])
            E.on("vector", v_prod)
            v_prod_ms = E.milestone("vector", "v")
            if tt == 0:
                dump("fkn", fk[:, 0:FPAD])
                dump("prod", prodb[:, 0:FPAD])

            E.wait("vector", "t", t_q)
            def ev_q(e):
                e.tensor_copy(fq[:, 0:512], pA[:, :])
                e.tensor_copy(fq[:, 512:1024], pB[:, :])
                e.tensor_copy(fq[:, 1024:FPAD], pC[:, 0:FPAD - 1024])
            E.on("vector", ev_q)
            v_q = E.milestone("vector", "v")

            E.wait("tensor", "v", v_prod_ms)
            for ch in range(3):
                n0 = ch * 512
                n1 = min(n0 + 512, FPAD)
                ps = (pD, pE, pF)[ch]
                def mm_cs(e, n0=n0, n1=n1, ps=ps):
                    e.matmul(ps[:, 0:n1 - n0], ltt[:], prodb[:, n0:n1],
                             start=True, stop=False)
                    e.matmul(ps[:, 0:n1 - n0], onesr[:], carry[:, n0:n1],
                             start=False, stop=True)
                E.on("tensor", mm_cs)
            t_cs = E.milestone("tensor", "t")
            E.wait("tensor", "v", v_q)
            def mm_carry(e):
                e.matmul(pA[0:1, 0:512], onesc[:], prodb[:, 0:512],
                         start=True, stop=False)
                e.matmul(pA[0:1, 0:512], onesr[:, 0:1], carry[:, 0:512],
                         start=False, stop=True)
                e.matmul(pB[0:1, 0:512], onesc[:], prodb[:, 512:1024],
                         start=True, stop=False)
                e.matmul(pB[0:1, 0:512], onesr[:, 0:1], carry[:, 512:1024],
                         start=False, stop=True)
                e.matmul(pC[0:1, 0:128], onesc[:], prodb[:, 1024:FPAD],
                         start=True, stop=False)
                e.matmul(pC[0:1, 0:128], onesr[:, 0:1], carry[:, 1024:FPAD],
                         start=False, stop=True)
            E.on("tensor", mm_carry)
            t_carry = E.milestone("tensor", "t")

            E.wait("vector", "t", t_carry)
            def ev_cs(e):
                e.tensor_copy(fkv[:, 0:512], pD[:, :])
                e.tensor_copy(fkv[:, 512:1024], pE[:, :])
                e.tensor_copy(fkv[:, 1024:2 * FH], pF[:, 0:2 * FH - 1024])
                e.tensor_copy(carry[:, 0:512], pA[0:1, 0:512])
                e.tensor_copy(carry[:, 512:1024], pB[0:1, 0:512])
                e.tensor_copy(carry[:, 1024:FPAD], pC[0:1, 0:128])
            E.on("vector", ev_cs)
            v_cs = E.milestone("vector", "v")
            if tt == 0:
                dump("fkv", fkv[:, 0:FPAD])
            E.wait("tensor", "v", v_cs)

            emit_normalize(fq)
            def v_z(e):
                e.tensor_mul(sqb[:, 0:FH], fkv[:, FH:2 * FH], fq[:, FH:2 * FH])
                e.tensor_mul(mag[:, 0:FH], fkv[:, 0:FH], fq[:, 0:FH])
                e.tensor_add(zt[:, 0:FH], mag[:, 0:FH], sqb[:, 0:FH])
                e.tensor_mul(sqb[:, 0:FH], fkv[:, 0:FH], fq[:, FH:2 * FH])
                e.tensor_mul(mag[:, 0:FH], fkv[:, FH:2 * FH], fq[:, 0:FH])
                e.tensor_sub(zt[:, FH:2 * FH], mag[:, 0:FH], sqb[:, 0:FH])
            E.on("vector", v_z)
            v_zms = E.milestone("vector", "v")
            if tt == 0:
                dump("fq", fq[:, 0:FPAD])
                dump("z", zt[:, 0:FPAD])

            E.wait("tensor", "v", v_zms)
            for ft in range(NFK):
                pt = (pT1, pT2)[ft % 2]
                bt_wait(E, bev1, "pT%d" % (ft % 2))
                def trz(e, ft=ft, pt=pt):
                    e.transpose(pt[:], zt[:, ft * 128:(ft + 1) * 128], ident[:])
                E.on("tensor", trz)
                tms = E.milestone("tensor", "t")
                E.wait("vector", "t", tms)
                def evz(e, ft=ft, pt=pt):
                    e.tensor_copy(zT[:, ft * 128:(ft + 1) * 128], pt[:])
                E.on("vector", evz)
                bt_set(bev1, "pT%d" % (ft % 2), "v", E.milestone("vector", "v"))
            E.wait("tensor", "v", E.c["v"].val)
            for ch in range(2):
                ps = (pD, pE)[ch]
                def mm_ir(e, ch=ch, ps=ps):
                    for kt in range(NFK):
                        e.matmul(ps[:, :],
                                 zT[:, kt * 128:(kt + 1) * 128],
                                 cGs[:, kt * D + ch * 512:kt * D + (ch + 1) * 512],
                                 start=(kt == 0), stop=(kt == NFK - 1))
                E.on("tensor", mm_ir)
            t_ir = E.milestone("tensor", "t")

            bo = (tt % 2) * D
            if tt >= 2:
                E.wait("sync", "v", res_v_ms[tt - 2])
            else:
                E.wait("sync", "v", cast_v_ms[NT - 2 + tt])
            grpn = ("xA", "xB")[tt % 2]
            ldx = E.dma("sync", grpn, lambda e, tt=tt, bo=bo: e.dma_start(
                out=xin[:, bo:bo + D], in_=x_h[tt * 128:(tt + 1) * 128, :]))
            E.wait("vector", grpn, ldx)
            E.wait("vector", "t", t_ir)
            if "out" in E.c:
                E.wait("vector", "out", E.c["out"].val)
            if dbg and tt == 0:
                def v_dump_xin(e, bo=bo):
                    e.tensor_copy(fk[:, 0:D], xin[:, bo:bo + D])
                E.on("vector", v_dump_xin)
                E.milestone("vector", "v")
                dump("xin0", fk[:, 0:D])
                def v_dump_vh(e):
                    e.tensor_copy(fv[:, 0:512], pD[:, :])
                    e.tensor_copy(fv[:, 512:1024], pE[:, :])
                E.on("vector", v_dump_vh)
                E.milestone("vector", "v")
                dump("vh", fv[:, 0:1024])
                dump("zT0", zT[:, 0:128])
                dump("cG0", cGs[:, 0:1024])
            def v_res(e, bo=bo):
                e.scalar_tensor_tensor(acc[:, 0:512], xin[:, bo:bo + 512], 2.0,
                                       pD[:, :], ALU.mult, ALU.add)
                e.scalar_tensor_tensor(acc[:, 512:1024], xin[:, bo + 512:bo + D],
                                       2.0, pE[:, :], ALU.mult, ALU.add)
                e.reduce_sum(m1[:], acc[:, 0:D], axis=AX.X)
                e.tensor_scalar_mul(m1[:], m1[:], -1.0 / D)
                e.tensor_scalar_add(acc[:, 0:D], acc[:, 0:D], m1[:])
                e.tensor_mul(fk[:, 0:D], acc[:, 0:D], acc[:, 0:D])
                e.reduce_sum(m2[:], fk[:, 0:D], axis=AX.X)
            E.on("vector", v_res)
            v_res_ms = E.milestone("vector", "v")
            if tt == 0:
                dump("acc0", acc[:, 0:D])
            E.wait("scalar", "v", v_res_ms)
            E.on("scalar", lambda e: e.activation(m2[:], m2[:], AF.Sqrt,
                                                  bias=epscol[:], scale=1.0 / D))
            s_ln = E.milestone("scalar", "s")
            E.wait("vector", "s", s_ln)
            def v_fin(e):
                e.reciprocal(m2[:], m2[:])
                e.tensor_scalar_mul(acc[:, 0:D], acc[:, 0:D], m2[:])
            E.on("vector", v_fin)
            v_fin_ms = E.milestone("vector", "v")
            res_v_ms[tt] = v_fin_ms
            E.wait("tensor", "v", v_res_ms)
            E.wait("sync", "v", v_fin_ms)
            E.dma("sync", "out", lambda e, tt=tt: e.dma_start(
                out=out_h[tt * 128:(tt + 1) * 128, :], in_=acc[:, 0:D]))

        E.wait("sync", "out", E.c["out"].val)
        if dbg and "db" in E.c:
            E.on("gpsimd", lambda e, v=E.c["db"].val: e.wait_ge(E.c["db"].sem, v))
        if not F:
            p.play(ctx.enter_context(nc.Block()))
    return nc





def emit_barrier(E):
    vals = {k: c.val for k, c in E.c.items()}
    for eng in ENGINES:
        for cname, val in vals.items():
            E.wait(eng, cname, val)
    E.milestone("tensor", "t")
    E.milestone("vector", "v")
    E.milestone("scalar", "s")


def emit_x_to_xT(E, x_h, ntiles, xT, xin, xinb, ident, pT1, pT2, SW,
                 first_ld_guard=None, slots=("xA", "xB")):
    cast_v = {}
    trans_t = {}
    bev = {}
    v_entry = E.c["v"].val
    t_entry = E.c["t"].val
    for tt in range(ntiles):
        bo = (tt % 2) * D
        grpn = slots[tt % 2]
        if tt >= 2:
            E.wait("sync", "v", cast_v[tt - 2])
        else:
            E.wait("sync", "v", v_entry)
            E.wait("sync", "t", t_entry)
            if first_ld_guard:
                E.wait("sync", first_ld_guard[0], first_ld_guard[1])
        ldv = E.dma("sync", grpn, lambda e, tt=tt, bo=bo: e.dma_start(
            out=xin[:, bo:bo + D], in_=x_h[tt * 128:(tt + 1) * 128, :]))
        E.wait("vector", grpn, ldv)
        if tt >= 2:
            E.wait("vector", "t", trans_t[tt - 2])
        else:
            E.wait("vector", "t", t_entry)
        E.on("vector", lambda e, bo=bo: e.tensor_copy(xinb[:, bo:bo + D],
                                                      xin[:, bo:bo + D]))
        cast_v[tt] = E.milestone("vector", "v")
        E.wait("tensor", "v", cast_v[tt])
        for dt in range(ND):
            pt = (pT1, pT2)[dt % 2]
            bt_wait(E, bev, dt % 2)
            def tr1(e, bo=bo, dt=dt, pt=pt):
                e.transpose(pt[:], xinb[:, bo + dt * 128:bo + (dt + 1) * 128],
                            ident[:])
            E.on("tensor", tr1)
            tms = E.milestone("tensor", "t")
            E.wait("vector", "t", tms)
            def ev1(e, tt=tt, dt=dt, pt=pt):
                e.tensor_copy(xT[:, dt * SW + tt * 128:dt * SW + (tt + 1) * 128],
                              pt[:])
            E.on("vector", ev1)
            bt_set(bev, dt % 2, "v", E.milestone("vector", "v"))
        trans_t[tt] = E.c["t"].val
    E.wait("tensor", "v", E.c["v"].val)
    return cast_v, trans_t


def emit_w_transpose(E, w_h, rows, wtmp, wT, wT_colstride, ident, pT1, pT2,
                     half_rows=None):
    nrt = rows // 128
    hr = half_rows or nrt
    bev = {}
    for h0 in range(0, nrt, hr):
        cnt = min(hr, nrt - h0)
        ld = E.dma("gpsimd", "wT", lambda e, h0=h0, cnt=cnt: e.dma_start(
            out=wtmp[:].rearrange("p (n m) -> p n m", n=hr)[:, 0:cnt, :],
            in_=w_h.rearrange("(n p) m -> p n m", p=128)[:, h0:h0 + cnt, :]))
        E.wait("tensor", "wT", ld)
        for et in range(cnt):
            for dt in range(ND):
                pt = (pT1, pT2)[dt % 2]
                bt_wait(E, bev, dt % 2)
                def trw(e, et=et, dt=dt, pt=pt):
                    e.transpose(pt[:], wtmp[:, et * D + dt * 128:et * D + (dt + 1) * 128],
                                ident[:])
                E.on("tensor", trw)
                tms = E.milestone("tensor", "t")
                E.wait("vector", "t", tms)
                def evw(e, et=et, h0=h0, dt=dt, pt=pt):
                    col = (h0 + et) * 128
                    e.tensor_copy(
                        wT[:, dt * wT_colstride + col:dt * wT_colstride + col + 128],
                        pt[:])
                E.on("vector", evw)
                bt_set(bev, dt % 2, "v", E.milestone("vector", "v"))
        E.wait("gpsimd", "t", E.c["t"].val)
    E.wait("tensor", "v", E.c["v"].val)


def emit_ln_tile(E, acc, m1, m2, sqbuf, epscol, gcast=None, bcast=None,
                 out_bf=None, out_f32=None, width=D):
    def v1(e):
        e.reduce_sum(m1[:], acc[:, 0:width], axis=AX.X)
        e.tensor_scalar_mul(m1[:], m1[:], -1.0 / width)
        e.tensor_scalar_add(acc[:, 0:width], acc[:, 0:width], m1[:])
        e.tensor_mul(sqbuf[:, 0:width], acc[:, 0:width], acc[:, 0:width])
        e.reduce_sum(m2[:], sqbuf[:, 0:width], axis=AX.X)
    E.on("vector", v1)
    vm = E.milestone("vector", "v")
    E.wait("scalar", "v", vm)
    E.on("scalar", lambda e: e.activation(m2[:], m2[:], AF.Sqrt,
                                          bias=epscol[:], scale=1.0 / width))
    sm = E.milestone("scalar", "s")
    E.wait("vector", "s", sm)
    def v2(e):
        e.reciprocal(m2[:], m2[:])
        e.tensor_scalar_mul(acc[:, 0:width], acc[:, 0:width], m2[:])
        if gcast is not None:
            e.tensor_mul(acc[:, 0:width], acc[:, 0:width], gcast[:, 0:width])
            e.tensor_add(acc[:, 0:width], acc[:, 0:width], bcast[:, 0:width])
        if out_bf is not None:
            e.tensor_copy(out_bf, acc[:, 0:width])
        if out_f32 is not None:
            e.tensor_copy(out_f32, acc[:, 0:width])
    E.on("vector", v2)
    return E.milestone("vector", "v")


def emit_bcast_row(E, row_f32, dst, onescol_f32, ps, width=D):
    vm = 0
    for ch in range((width + 511) // 512):
        n0, n1 = ch * 512, min((ch + 1) * 512, width)
        def mm(e, n0=n0, n1=n1):
            e.matmul(ps[:, 0:n1 - n0], onescol_f32[0:1, :], row_f32[0:1, n0:n1],
                     start=True, stop=True)
        E.on("tensor", mm)
        tms = E.milestone("tensor", "t")
        E.wait("vector", "t", tms)
        def ev(e, n0=n0, n1=n1):
            e.tensor_copy(dst[:, n0:n1], ps[:, 0:n1 - n0])
        E.on("vector", ev)
        vm = E.milestone("vector", "v")
        E.wait("tensor", "v", vm)
    return vm



NH = 16
HD2 = 64
NPAIR = 8
NTM = MY // 128
FF = 2048
NE0 = 8


def build_stage2(dbg=False, F=None):
    nc = F["nc"] if F else bass.Bass("TRN2", target_bir_lowering=False)
    x1_h = F["x1"] if F else nc.dram_tensor("x1", [S, D], F32, kind="ExternalInput").ap()
    wi_h = F["wi"] if F else nc.dram_tensor("wi", [3 * D, D], F32, kind="ExternalInput").ap()
    bicols_h = F["bicols"] if F else nc.dram_tensor("bicols", [128, 16], F32, kind="ExternalInput").ap()
    bivrow_h = F["bivrow"] if F else nc.dram_tensor("bivrow", [1, D], F32, kind="ExternalInput").ap()
    wo_h = F["wo"] if F else nc.dram_tensor("wo", [D, D], F32, kind="ExternalInput").ap()
    borow_h = F["borow"] if F else nc.dram_tensor("borow", [1, D], F32, kind="ExternalInput").ap()
    g1_h = F["ln1g"] if F else nc.dram_tensor("ln1g", [1, D], F32, kind="ExternalInput").ap()
    b1_h = F["ln1b"] if F else nc.dram_tensor("ln1b", [1, D], F32, kind="ExternalInput").ap()
    g2_h = F["ln2g"] if F else nc.dram_tensor("ln2g", [1, D], F32, kind="ExternalInput").ap()
    b2_h = F["ln2b"] if F else nc.dram_tensor("ln2b", [1, D], F32, kind="ExternalInput").ap()
    wg_h = F["wg"] if F else nc.dram_tensor("wg", [D, 4], F32, kind="ExternalInput").ap()
    bg_h = F["bg"] if F else nc.dram_tensor("bg", [1, 4], F32, kind="ExternalInput").ap()
    wef_h = F["wef"] if F else nc.dram_tensor("wef", [D, 8], F32, kind="ExternalInput").ap()
    bef_h = F["bef"] if F else nc.dram_tensor("bef", [1, 8], F32, kind="ExternalInput").ap()
    w1_h = F["w1"] if F else nc.dram_tensor("w1", [NE0, D, FF], F32, kind="ExternalInput").ap()
    b1c_h = F["b1cols"] if F else nc.dram_tensor("b1cols", [128, NE0 * 16], F32, kind="ExternalInput").ap()
    w2_h = F["w2"] if F else nc.dram_tensor("w2", [NE0, FF, D], F32, kind="ExternalInput").ap()
    b2r_h = F["b2rows"] if F else nc.dram_tensor("b2rows", [1, NE0 * D], F32, kind="ExternalInput").ap()
    cId_h = F["cIdent"] if F else nc.dram_tensor("cIdent", [128, 128], F32, kind="ExternalInput").ap()
    cOR_h = F["cOnesRow"] if F else nc.dram_tensor("cOnesRow", [1, 128], F32, kind="ExternalInput").ap()
    out_h = F["out"] if F else nc.dram_tensor("out", [S, D], F32, kind="ExternalOutput").ap()
    if dbg:
        dbg_h = {k: nc.dram_tensor("dbg_" + k, [128, 4096], F32,
                                   kind="ExternalOutput").ap()
                 for k in ["v0", "kT0", "qT0", "ao0", "u0", "comb", "yacc0", "ET0"]}
    kTs = nc.dram_tensor("kTs", [NPAIR, 128, S], BF16, kind="Internal").ap()
    qTs = nc.dram_tensor("qTs", [NPAIR, 128, S], BF16, kind="Internal").ap()
    vs = nc.dram_tensor("vs", [S, D], BF16, kind="Internal").ap()
    us = nc.dram_tensor("us", [S, D], F32, kind="Internal").ap()
    aoTs = nc.dram_tensor("aoTs", [128, ND * S], BF16, kind="Internal").ap()
    UMts = nc.dram_tensor("UMts", [128, ND * S], BF16, kind="Internal").ap()

    with ExitStack() as ctx:
        def sb(name, shape, dtype, c=None):
            return (c or ctx).enter_context(nc.sbuf_tensor("b" + name, shape, dtype))
        def psum(name, shape, dtype):
            return ctx.enter_context(nc.psum_tensor("b" + name, shape, dtype))
        def sem(name):
            return ctx.enter_context(nc.semaphore("b" + name))

        ident = sb("ident", [128, 128], BF16)
        onesrB = sb("onesrB", [1, 128], BF16)
        onesrF = sb("onesrF", [1, 128], F32)
        epscol = sb("epscol", [128, 1], F32)
        bicols = sb("bicols_s", [128, 16], F32)
        bivrow = sb("bivrow_s", [1, D], BF16)
        borow = sb("borow_s", [1, D], BF16)
        b1cols = sb("b1cols_s", [128, NE0 * 16], F32)
        wgsb = sb("wgsb", [128, ND * 4], BF16)
        wesb = sb("wesb", [128, ND * 8], BF16)
        bgrow = sb("bgrow", [1, 4], BF16)
        berow = sb("berow", [1, 8], BF16)
        combine = sb("combine", [128, NT * NE0], F32)
        m1 = sb("m1", [128, 1], F32)
        m2 = sb("m2", [128, 1], F32)
        m11 = sb("m11", [1, 1], F32)
        negmax = sb("negmax", [128, 1], F32)
        acc = sb("acc", [128, D], F32)
        sqd = sb("sqd", [128, D], F32)
        xin = sb("xin", [128, 2 * D], F32)
        xinb = sb("xinb", [128, 2 * D], BF16)
        identF = sb("identF", [128, 128], F32)
        woT = sb("woT", [128, ND * D], BF16)

        phA = ExitStack()
        xT = sb("xT", [128, ND * S], BF16, phA)
        wiT = sb("wiT", [128, ND * 3 * D], BF16, phA)
        wtmp2 = sb("wtmp2", [128, 8 * D], BF16, phA)

        pA = psum("pA", [128, 512], F32)
        pB = psum("pB", [128, 512], F32)
        pC = psum("pC", [128, 512], F32)
        pD = psum("pD", [128, 512], F32)
        pMax = psum("pMax", [1, 128], F32)
        pCol = psum("pCol", [128, 4], F32)
        pT1 = psum("pT1", [128, 128], BF16)
        pT2 = psum("pT2", [128, 128], BF16)

        if F:
            E = F["E"]; p = F["p"]
        else:
            s_t0 = sem("s_t"); s_v0 = sem("s_v"); s_s0 = sem("s_s"); s_g0 = sem("s_g")
            p = Prog()
            E = Emit(p, {"t": s_t0, "v": s_v0, "s": s_s0, "g": s_g0}, nc=nc, ctx=ctx)
        s_t = E.c["t"].sem; s_v = E.c["v"].sem; s_s = E.c["s"].sem; s_g = E.c["g"].sem

        def dump(name, src_ap, cname="v", guards=("vector", "scalar", "sync")):
            if not dbg:
                return
            ms = E.c[cname].val
            E.on("gpsimd", lambda e, ms=ms, cname=cname: e.wait_ge(E.c[cname].sem, ms))
            if "db" in E.c and E.c["db"].val:
                E.on("gpsimd", lambda e, v=E.c["db"].val: e.wait_ge(E.c["db"].sem, v))
            E.dma("gpsimd", "db", lambda e, name=name, src_ap=src_ap: e.dma_start(
                out=dbg_h[name][0:src_ap.shape[0], 0:src_ap.shape[-1]], in_=src_ap))
            for eng in guards:
                E.wait(eng, "db", E.c["db"].val)

        E.dma("gpsimd", "gC", lambda e: e.dma_start(out=ident[:], in_=cId_h[:]))
        E.dma("gpsimd", "gC", lambda e: e.dma_start(out=onesrB[:], in_=cOR_h[:]))
        E.dma("gpsimd", "gC", lambda e: e.dma_start(out=onesrF[:], in_=cOR_h[:]))
        E.dma("sync", "sC", lambda e: e.dma_start(out=identF[:], in_=cId_h[:]))
        E.dma("gpsimd", "gC", lambda e: e.dma_start(out=bicols[:], in_=bicols_h[:]))
        E.dma("gpsimd", "gC", lambda e: e.dma_start(out=bivrow[:], in_=bivrow_h[:]))
        E.dma("gpsimd", "gC", lambda e: e.dma_start(out=borow[:], in_=borow_h[:]))
        E.dma("gpsimd", "gC", lambda e: e.dma_start(out=b1cols[:], in_=b1c_h[:]))
        E.dma("gpsimd", "gC", lambda e: e.dma_start(
            out=wgsb[:].rearrange("p (n m) -> p n m", n=ND),
            in_=wg_h.rearrange("(n p) m -> p n m", p=128)))
        E.dma("gpsimd", "gC", lambda e: e.dma_start(
            out=wesb[:].rearrange("p (n m) -> p n m", n=ND),
            in_=wef_h.rearrange("(n p) m -> p n m", p=128)))
        E.dma("gpsimd", "gC", lambda e: e.dma_start(out=bgrow[:], in_=bg_h[:]))
        E.dma("gpsimd", "gC", lambda e: e.dma_start(out=berow[:], in_=bef_h[:]))
        ld_g = E.c["gC"].val
        ld_s = E.c["sC"].val
        E.on("vector", lambda e: e.memset(epscol[:], LN_EPS))
        E.wait("vector", "gC", ld_g)
        E.wait("vector", "sC", ld_s)
        E.wait("tensor", "gC", ld_g)
        E.wait("tensor", "sC", ld_s)

        emit_x_to_xT(E, x1_h, NT, xT, xin, xinb, ident, pT1, pT2, S)
        emit_w_transpose(E, wi_h, 3 * D, wtmp2, wiT, 3 * D, ident, pT1, pT2,
                         half_rows=8)
        emit_w_transpose(E, wo_h, D, wtmp2, woT, D, ident, pT1, pT2,
                         half_rows=8)

        bev2 = {}
        kq_dma = {}
        ci = 0
        for which in range(2):
            wof = (D, 0)[which]
            dst_s = (kTs, qTs)[which]
            bco = (8, 0)[which]
            for pr in range(NPAIR):
                for tc in range(S // 512):
                    slot = ci % 4
                    xo = slot * 512
                    grp = "kq%d" % slot
                    ps = (pA, pB)[ci % 2]
                    bt_wait(E, bev2, ci % 2)
                    def mmk(e, pr=pr, tc=tc, ps=ps, wof=wof):
                        for kt in range(ND):
                            e.matmul(
                                ps[:, :],
                                wiT[:, kt * 3 * D + wof + pr * 128:kt * 3 * D + wof + (pr + 1) * 128],
                                xT[:, kt * S + tc * 512:kt * S + (tc + 1) * 512],
                                start=(kt == 0), stop=(kt == ND - 1))
                    E.on("tensor", mmk)
                    tms = E.milestone("tensor", "t")
                    E.wait("vector", "t", tms)
                    if slot in kq_dma:
                        E.wait("vector", grp, kq_dma[slot])
                    def evk(e, pr=pr, ps=ps, xo=xo, bco=bco):
                        e.tensor_scalar_add(xinb[:, xo:xo + 512], ps[:, :],
                                            bicols[:, bco + pr:bco + pr + 1])
                    E.on("vector", evk)
                    sms = E.milestone("vector", "v")
                    bt_set(bev2, ci % 2, "v", sms)
                    E.on("gpsimd", lambda e, sms=sms: e.wait_ge(s_v, sms))
                    gv = E.dma("gpsimd", grp, lambda e, pr=pr, tc=tc, xo=xo,
                               dst_s=dst_s: e.dma_start(
                        out=dst_s[pr, :, tc * 512:(tc + 1) * 512],
                        in_=xinb[:, xo:xo + 512]))
                    kq_dma[slot] = gv
                    ci += 1
        for sl in range(4):
            E.wait("vector", "kq%d" % sl, kq_dma[sl])
        v_dma = {}
        for tt in range(NT):
            xo = (tt % 2) * 1024
            grp = "kqv%d" % (tt % 2)
            for ch in range(2):
                ps = (pA, pB)[ch]
                bt_wait(E, bev2, ch)
                def mmv(e, tt=tt, ch=ch, ps=ps):
                    e.matmul(ps[:, :], onesrB[:], bivrow[0:1, ch * 512:(ch + 1) * 512],
                             start=True, stop=False)
                    for kt in range(ND):
                        e.matmul(
                            ps[:, :],
                            xT[:, kt * S + tt * 128:kt * S + (tt + 1) * 128],
                            wiT[:, kt * 3 * D + 2 * D + ch * 512:kt * 3 * D + 2 * D + (ch + 1) * 512],
                            start=False, stop=(kt == ND - 1))
                E.on("tensor", mmv)
            tms = E.milestone("tensor", "t")
            E.wait("vector", "t", tms)
            if (tt % 2) in v_dma:
                E.wait("vector", grp, v_dma[tt % 2])
            def evv(e, xo=xo):
                e.tensor_copy(xinb[:, xo:xo + 512], pA[:, :])
                e.tensor_copy(xinb[:, xo + 512:xo + 1024], pB[:, :])
            E.on("vector", evv)
            vms = E.milestone("vector", "v")
            bt_set(bev2, 0, "v", vms)
            bt_set(bev2, 1, "v", vms)
            if tt == 0:
                dump("v0", xinb[:, 0:1024])
            E.on("gpsimd", lambda e, vms=vms: e.wait_ge(s_v, vms))
            gv = E.dma("gpsimd", grp, lambda e, tt=tt, xo=xo: e.dma_start(
                out=vs[tt * 128:(tt + 1) * 128, :], in_=xinb[:, xo:xo + 1024]))
            v_dma[tt % 2] = gv

        emit_barrier(E)
        phA.close()
        gcast1 = sb("gcast1", [128, D], F32)
        bcast1 = sb("bcast1", [128, D], F32)
        gcast2 = sb("gcast2", [128, D], F32)
        bcast2 = sb("bcast2", [128, D], F32)
        phB = ExitStack()
        kTh = sb("kTh", [64, 2 * S], BF16, phB)
        qTh = sb("qTh", [64, 2 * S], BF16, phB)
        aosl = sb("aosl", [64, 1024], BF16, phB)
        vOnesE = sb("vOnesE", [128, NT * 128], BF16, phB)
        vOnesO = sb("vOnesO", [128, NT * 128], BF16, phB)
        osum = sb("osum", [128, 1024], F32, phB)
        ET = sb("ET", [128, NT * 512], BF16, phB)
        onorm = sb("onorm", [128, 1024], F32, phB)
        rows1 = sb("rows1", [1, D], F32, phB)
        for row_h, dstc in ((g1_h, gcast1), (b1_h, bcast1), (g2_h, gcast2),
                            (b2_h, bcast2)):
            E.wait("sync", "t", E.c["t"].val)
            ldr = E.dma("sync", "sC", lambda e, row_h=row_h: e.dma_start(
                out=rows1[0:1, 0:D], in_=row_h[:]))
            E.wait("tensor", "sC", ldr)
            emit_bcast_row(E, rows1[0:1, 0:D], dstc, onesrF, pA)

        E.on("vector", lambda e: e.memset(
            vOnesE[:].rearrange("p (n m) -> p n m", n=NT)[:, :, 64:128], 1.0))
        E.on("vector", lambda e: e.memset(
            vOnesO[:].rearrange("p (n m) -> p n m", n=NT)[:, :, 0:64], 1.0))
        E.milestone("vector", "v")

        head_ld = {}
        battn = {}
        wao_hist2 = {}
        for h in range(NH):
            pr, sub = h // 2, h % 2
            hb = (h % 2)
            hgrp = ("hA", "hB")[h % 2]
            if h >= 2:
                E.wait("sync", "t", head_ld[h - 2])
            E.dma("sync", hgrp, lambda e, pr=pr, sub=sub, hb=hb: e.dma_start(
                out=kTh[:, hb * S:(hb + 1) * S],
                in_=kTs[pr, sub * 64:(sub + 1) * 64, :]))
            E.dma("sync", hgrp, lambda e, pr=pr, sub=sub, hb=hb: e.dma_start(
                out=qTh[:, hb * S:(hb + 1) * S],
                in_=qTs[pr, sub * 64:(sub + 1) * 64, :]))
            vOn = (vOnesE, vOnesO)[h % 2]
            vlo = 0 if h % 2 == 0 else 64
            E.dma("sync", hgrp, lambda e, h=h, vOn=vOn, vlo=vlo: e.dma_start(
                out=vOn[:].rearrange("p (n m) -> p n m", n=NT)[:, :, vlo:vlo + 64],
                in_=vs.rearrange("(n p) m -> p n m", p=128)[:, :, h * HD2:(h + 1) * HD2]))
            ldh = E.c[hgrp].val
            if h == 0 and dbg:
                dump("kT0", kTh[0:64, 0:S], cname="hA")
                dump("qT0", qTh[0:64, 0:MY], cname="hA")
            E.wait("tensor", hgrp, ldh)
            for mc8 in range(8):
                ps = (pA, pB)[mc8 % 2]
                bt_wait(E, battn, mc8 % 2)
                def mm0(e, hb=hb, mc8=mc8, ps=ps):
                    e.matmul(ps[:, :], kTh[0:64, hb * S:hb * S + 128],
                             qTh[0:64, hb * S + mc8 * 512:hb * S + (mc8 + 1) * 512],
                             start=True, stop=True)
                E.on("tensor", mm0)
                t0 = E.milestone("tensor", "t")
                E.wait("vector", "t", t0)
                def vmax(e, mc8=mc8, ps=ps):
                    e.reduce_max(m2[:], ps[:, :], axis=AX.X)
                    if mc8 == 0:
                        e.tensor_copy(negmax[:], m2[:])
                    else:
                        e.tensor_tensor(negmax[:], negmax[:], m2[:], ALU.max)
                E.on("vector", vmax)
                bt_set(battn, mc8 % 2, "v", E.milestone("vector", "v"))
            E.wait("tensor", "v", E.c["v"].val)
            def tmax(e):
                e.transpose(pMax[:, :], negmax[:], identF[:])
            E.on("tensor", tmax)
            t1 = E.milestone("tensor", "t")
            E.wait("vector", "t", t1)
            def vmax2(e):
                e.reduce_max(m11[:], pMax[0:1, :], axis=AX.X)
                e.tensor_scalar_mul(m11[:], m11[:], -0.125)
            E.on("vector", vmax2)
            v2ms = E.milestone("vector", "v")
            E.wait("tensor", "v", v2ms)
            def tbc(e):
                e.matmul(pCol[:, 0:1], onesrF[0:1, :], m11[:], start=True,
                         stop=True)
            E.on("tensor", tbc)
            t2 = E.milestone("tensor", "t")
            E.wait("vector", "t", t2)
            E.on("vector", lambda e: e.tensor_copy(negmax[:], pCol[:, 0:1]))
            vnm = E.milestone("vector", "v")
            E.wait("scalar", "v", vnm)
            E.wait("tensor", "v", vnm)
            for tc in range(S // 512):
                for jt in range(NT):
                    ps = (pA, pB)[jt % 2]
                    bt_wait(E, battn, jt % 2)
                    def mmj(e, jt=jt, hb=hb, tc=tc, ps=ps):
                        e.matmul(ps[:, :],
                                 kTh[0:64, hb * S + jt * 128:hb * S + (jt + 1) * 128],
                                 qTh[0:64, hb * S + tc * 512:hb * S + (tc + 1) * 512],
                                 start=True, stop=True)
                    E.on("tensor", mmj)
                    tj = E.milestone("tensor", "t")
                    E.wait("scalar", "t", tj)
                    def sexp(e, jt=jt, ps=ps):
                        e.activation(ET[:, jt * 512:(jt + 1) * 512], ps[:, :],
                                     AF.Exp, bias=negmax[:], scale=0.125)
                    E.on("scalar", sexp)
                    bt_set(battn, jt % 2, "s", E.milestone("scalar", "s"))
                po_ = (pC, pD)[tc % 2]
                bt_wait(E, battn, "o%d" % (tc % 2))
                E.wait("tensor", "s", E.c["s"].val)
                def mmo(e, vOn=vOn, po_=po_):
                    for jt in range(NT):
                        e.matmul(po_[:, :], vOn[:, jt * 128:(jt + 1) * 128],
                                 ET[:, jt * 512:(jt + 1) * 512],
                                 start=(jt == 0), stop=(jt == NT - 1))
                E.on("tensor", mmo)
                to = E.milestone("tensor", "t")
                E.wait("vector", "t", to)
                oo = (tc % 2) * 512
                osgrp = ("os0", "os1")[tc % 2]
                aogrp = ("aoA", "aoB")[tc % 2]
                def vev(e, po_=po_, oo=oo):
                    e.tensor_copy(onorm[:, oo:oo + 512], po_[:, :])
                E.on("vector", vev)
                vev = E.milestone("vector", "v")
                bt_set(battn, "o%d" % (tc % 2), "v", vev)
                E.on("gpsimd", lambda e, vev=vev: e.wait_ge(s_v, vev))
                if h % 2 == 0:
                    mv = E.dma("gpsimd", osgrp, lambda e, oo=oo: e.dma_start(
                        out=osum[0:64, oo:oo + 512], in_=onorm[64:128, oo:oo + 512]))
                else:
                    mv = E.dma("gpsimd", osgrp, lambda e, oo=oo: e.dma_start(
                        out=osum[64:128, oo:oo + 512], in_=onorm[0:64, oo:oo + 512]))
                E.wait("vector", osgrp, mv)
                if (tc % 2) in wao_hist2:
                    E.wait("vector", aogrp, wao_hist2[tc % 2])
                def vnorm(e, h=h, oo=oo):
                    lo = (h % 2) * 64
                    e.reciprocal(osum[lo:lo + 64, oo:oo + 512],
                                 osum[lo:lo + 64, oo:oo + 512])
                    e.tensor_mul(aosl[0:64, oo:oo + 512],
                                 onorm[lo:lo + 64, oo:oo + 512],
                                 osum[lo:lo + 64, oo:oo + 512])
                E.on("vector", vnorm)
                vno = E.milestone("vector", "v")
                E.on("gpsimd", lambda e, vno=vno: e.wait_ge(s_v, vno))
                wao = E.dma("gpsimd", aogrp, lambda e, h=h, tc=tc, oo=oo:
                            e.dma_start(
                    out=bass.AP(tensor=aoTs.tensor,
                                offset=(h % 2) * 64 * (ND * S) + (h // 2) * S + tc * 512,
                                ap=[[ND * S, 64], [1, 512]]),
                    in_=aosl[0:64, oo:oo + 512]))
                wao_hist2[tc % 2] = wao
            head_ld[h] = E.c["t"].val


        emit_barrier(E)
        phB.close()
        phB2 = ExitStack()
        aog = sb("aog", [128, ND * 128], BF16, phB2)
        ustg = sb("ustg", [128, ND * 128], BF16, phB2)
        res_ms = {}
        mmop_ms = {}
        for tt in range(NT):
            bo = (tt % 2) * D
            grpn = ("xA", "xB")[tt % 2]
            if tt >= 2:
                E.wait("sync", "v", res_ms[tt - 2])
            else:
                E.wait("sync", "v", E.c["v"].val)
                E.wait("sync", "t", E.c["t"].val)
            ldx = E.dma("sync", grpn, lambda e, tt=tt, bo=bo: e.dma_start(
                out=xin[:, bo:bo + D], in_=x1_h[tt * 128:(tt + 1) * 128, :]))
            if (tt - 1) in mmop_ms:
                E.on("gpsimd", lambda e, v=mmop_ms[tt - 1]: e.wait_ge(E.c["t"].sem, v))
            lao = E.dma("gpsimd", "aol", lambda e, tt=tt: e.dma_start(
                out=aog[:].rearrange("p (n m) -> p n m", n=ND),
                in_=aoTs.rearrange("p (n m) -> p n m", n=ND)[:, :, tt * 128:(tt + 1) * 128]))
            E.wait("tensor", "aol", lao)
            for ch in range(2):
                ps = (pA, pB)[ch]
                def mmop(e, tt=tt, ch=ch, ps=ps):
                    e.matmul(ps[:, :], onesrB[:], borow[0:1, ch * 512:(ch + 1) * 512],
                             start=True, stop=False)
                    for kt in range(ND):
                        e.matmul(ps[:, :],
                                 aog[:, kt * 128:(kt + 1) * 128],
                                 woT[:, kt * D + ch * 512:kt * D + (ch + 1) * 512],
                                 start=False, stop=(kt == ND - 1))
                E.on("tensor", mmop)
            tms = E.milestone("tensor", "t")
            mmop_ms[tt] = tms
            E.wait("vector", "t", tms)
            E.wait("vector", grpn, ldx)
            if "us" in E.c:
                E.wait("vector", "us", E.c["us"].val)
            def vres(e, bo=bo):
                e.scalar_tensor_tensor(acc[:, 0:512], xin[:, bo:bo + 512], 1.0,
                                       pA[:, :], ALU.mult, ALU.add)
                e.scalar_tensor_tensor(acc[:, 512:1024], xin[:, bo + 512:bo + D],
                                       1.0, pB[:, :], ALU.mult, ALU.add)
            E.on("vector", vres)
            E.milestone("vector", "v")
            E.wait("tensor", "v", E.c["v"].val)
            lnm = emit_ln_tile(E, acc, m1, m2, sqd, epscol, gcast1, bcast1,
                               out_bf=xinb[:, 0:D])
            E.wait("sync", "v", lnm)
            E.dma("sync", "us", lambda e, tt=tt: e.dma_start(
                out=us[tt * 128:(tt + 1) * 128, :], in_=acc[:, 0:D]))
            stu = E.c["us"].val
            E.wait("tensor", "v", lnm)
            if "umw" in E.c:
                E.wait("vector", "umw", E.c["umw"].val)
            for dt in range(ND):
                pt = (pT1, pT2)[dt % 2]
                bt_wait(E, bev2, "pT%d" % (dt % 2))
                def tru(e, dt=dt, pt=pt):
                    e.transpose(pt[:], xinb[:, dt * 128:(dt + 1) * 128], ident[:])
                E.on("tensor", tru)
                tms = E.milestone("tensor", "t")
                E.wait("vector", "t", tms)
                def evu(e, dt=dt, pt=pt):
                    e.tensor_copy(ustg[:, dt * 128:(dt + 1) * 128], pt[:])
                E.on("vector", evu)
                bt_set(bev2, "pT%d" % (dt % 2), "v", E.milestone("vector", "v"))
            E.wait("tensor", "v", E.c["v"].val)
            umv = E.c["v"].val
            E.on("gpsimd", lambda e, umv=umv: e.wait_ge(E.c["v"].sem, umv))
            E.dma("gpsimd", "umw", lambda e, tt=tt: e.dma_start(
                out=UMts.rearrange("p (n m) -> p n m", n=ND)[:, :, tt * 128:(tt + 1) * 128],
                in_=ustg[:].rearrange("p (n m) -> p n m", n=ND)))
            bt_wait(E, bev2, "pColD")
            def mmg(e, tt=tt):
                e.matmul(pCol[:, 0:4], onesrB[:], bgrow[0:1, :], start=True,
                         stop=False)
                for kt in range(ND):
                    e.matmul(pCol[:, 0:4],
                             ustg[:, kt * 128:(kt + 1) * 128],
                             wgsb[:, kt * 4:(kt + 1) * 4],
                             start=False, stop=(kt == ND - 1))
                e.matmul(pD[:, 0:8], onesrB[:], berow[0:1, :], start=True,
                         stop=False)
                for kt in range(ND):
                    e.matmul(pD[:, 0:8],
                             ustg[:, kt * 128:(kt + 1) * 128],
                             wesb[:, kt * 8:(kt + 1) * 8],
                             start=False, stop=(kt == ND - 1))
            E.on("tensor", mmg)
            tg = E.milestone("tensor", "t")
            E.wait("vector", "t", tg)
            def vgate(e, tt=tt):
                lg = sqd[:, 0:4]; ep = sqd[:, 8:16]
                gmax = sqd[:, 16:17]; sg = sqd[:, 17:18]
                emax = sqd[:, 20:24]; esum = sqd[:, 24:28]
                epsel = sqd[:, 28:30]; epm = sqd[:, 30:31]
                ohg = sqd[:, 32:36]; ohe = sqd[:, 36:38]
                gpep = sqd[:, 38:39]
                e.tensor_copy(lg, pCol[:, 0:4])
                e.tensor_copy(ep, pD[:, 0:8])
                e.reduce_max(gmax, lg, axis=AX.X)
                e.tensor_scalar(ohg, lg, gmax, None, ALU.is_ge)
                e.tensor_scalar(lg, lg, gmax, None, ALU.subtract)
            E.on("vector", vgate)
            vg1 = E.milestone("vector", "v")
            bt_set(bev2, "pColD", "v", vg1)
            E.wait("scalar", "v", vg1)
            E.on("scalar", lambda e: e.activation(sqd[:, 0:4], sqd[:, 0:4], AF.Exp))
            sg1 = E.milestone("scalar", "s")
            E.wait("vector", "s", sg1)
            def vgate2(e, tt=tt):
                lg = sqd[:, 0:4]; ep = sqd[:, 8:16]
                sg = sqd[:, 17:18]
                emax = sqd[:, 20:24]
                e.reduce_sum(sg, lg, axis=AX.X)
                e.reciprocal(sg, sg)
                e.tensor_tensor(emax, ep[:, 0:8:2], ep[:, 1:8:2], ALU.max)
                e.tensor_tensor(ep[:, 0:8:2], ep[:, 0:8:2], emax, ALU.subtract)
                e.tensor_tensor(ep[:, 1:8:2], ep[:, 1:8:2], emax, ALU.subtract)
            E.on("vector", vgate2)
            vg2 = E.milestone("vector", "v")
            E.wait("scalar", "v", vg2)
            E.on("scalar", lambda e: e.activation(sqd[:, 8:16], sqd[:, 8:16], AF.Exp))
            sg2 = E.milestone("scalar", "s")
            E.wait("vector", "s", sg2)
            def vgate3(e, tt=tt):
                lg = sqd[:, 0:4]; ep = sqd[:, 8:16]
                sg = sqd[:, 17:18]
                esum = sqd[:, 24:28]
                epsel = sqd[:, 28:30]; epm = sqd[:, 30:31]
                ohg = sqd[:, 32:36]; ohe = sqd[:, 36:38]
                gpep = sqd[:, 38:39]
                tmp2 = sqd[:, 40:42]
                e.tensor_add(esum, ep[:, 0:8:2], ep[:, 1:8:2])
                e.reciprocal(esum, esum)
                e.tensor_mul(ep[:, 0:8:2], ep[:, 0:8:2], esum)
                e.tensor_mul(ep[:, 1:8:2], ep[:, 1:8:2], esum)
                e.memset(epsel, 0.0)
                for g in range(4):
                    e.tensor_scalar(tmp2, ep[:, 2 * g:2 * g + 2],
                                    ohg[:, g:g + 1], None, ALU.mult)
                    e.tensor_add(epsel, epsel, tmp2)
                e.reduce_max(epm, epsel, axis=AX.X)
                e.tensor_scalar(ohe, epsel, epm, None, ALU.is_ge)
                e.tensor_mul(gpep, sg, epm)
                for g in range(4):
                    for j in range(2):
                        col = combine[:, tt * NE0 + 2 * g + j:tt * NE0 + 2 * g + j + 1]
                        e.tensor_mul(col, ohg[:, g:g + 1], ohe[:, j:j + 1])
                        e.tensor_mul(col, col, gpep)
            E.on("vector", vgate3)
            res_ms[tt] = E.milestone("vector", "v")

        emit_barrier(E)
        phB2.close()
        w1b = sb("w1b", [128, 2 * ND * 1024], BF16)
        w2b = sb("w2b", [128, 2 * 8 * D], BF16)
        hT = sb("hT", [128, 8 * 1024], BF16)
        yacc = sb("yacc", [128, 8 * D], F32)
        b2row = sb("b2row_s", [1, 2 * D], BF16)
        UMg = sb("UMg", [128, ND * 1024], BF16)

        fin_ms = {}
        bevm = {}
        w1_t = {}
        w2_t = {}
        for grp in range(4):
            E.on("vector", lambda e: e.memset(yacc[:, :], 0.0))
            ya0 = E.milestone("vector", "v")
            E.on("gpsimd", lambda e, v=E.c["t"].val: e.wait_ge(E.c["t"].sem, v))
            lum = E.dma("gpsimd", "uml", lambda e, grp=grp: e.dma_start(
                out=UMg[:].rearrange("p (n m) -> p n m", n=ND),
                in_=UMts.rearrange("p (n m) -> p n m", n=ND)[
                    :, :, grp * 1024:(grp + 1) * 1024]))
            E.wait("tensor", "uml", lum)
            for ei in range(NE0):
                for fh in range(2):
                    slab = ei * 2 + fh
                    ws = slab % 2
                    w1grp = ("wm1a", "wm1b")[ws]
                    w2grp = ("wm2a", "wm2b")[ws]
                    bgrp2 = ("wb2a", "wb2b")[ws]
                    if ws in w1_t:
                        E.on("gpsimd", lambda e, v=w1_t[ws]:
                             e.wait_ge(E.c["t"].sem, v))
                    ldw = E.dma("gpsimd", w1grp, lambda e, ei=ei, fh=fh, ws=ws:
                                e.dma_start(
                        out=w1b[:, ws * ND * 1024:(ws + 1) * ND * 1024].rearrange(
                            "p (n m) -> p n m", n=ND),
                        in_=w1_h[ei].rearrange("(n p) m -> p n m", p=128)[
                            :, :, fh * 1024:(fh + 1) * 1024]))
                    E.wait("tensor", w1grp, ldw)
                    for ft8 in range(8):
                        for ch in range(2):
                            ps = (pA, pB)[ch]
                            bt_wait(E, bevm, ("pA", "pB")[ch])
                            def mmh(e, ft8=ft8, ch=ch, ps=ps, ws=ws):
                                for kt in range(ND):
                                    e.matmul(
                                        ps[:, :],
                                        w1b[:, ws * ND * 1024 + kt * 1024 + ft8 * 128:
                                           ws * ND * 1024 + kt * 1024 + (ft8 + 1) * 128],
                                        UMg[:, kt * 1024 + ch * 512:kt * 1024 + (ch + 1) * 512],
                                        start=(kt == 0), stop=(kt == ND - 1))
                            E.on("tensor", mmh)
                            tms = E.milestone("tensor", "t")
                            E.wait("scalar", "t", tms)
                            def evh(e, ei=ei, fh=fh, ft8=ft8, ch=ch, ps=ps):
                                bc = b1cols[:, ei * 16 + fh * 8 + ft8:
                                            ei * 16 + fh * 8 + ft8 + 1]
                                e.activation(
                                    hT[:, ft8 * 1024 + ch * 512:ft8 * 1024 + (ch + 1) * 512],
                                    ps[:, :], AF.Relu, bias=bc)
                            E.on("scalar", evh)
                            bt_set(bevm, ("pA", "pB")[ch], "s",
                                   E.milestone("scalar", "s"))
                    w1_t[ws] = E.c["t"].val
                    if ws in w2_t:
                        E.on("gpsimd", lambda e, v=w2_t[ws]:
                             e.wait_ge(E.c["t"].sem, v))
                    ldw2 = E.dma("gpsimd", w2grp, lambda e, ei=ei, fh=fh, ws=ws:
                                 e.dma_start(
                        out=w2b[:, ws * 8 * D:(ws + 1) * 8 * D].rearrange(
                            "p (n m) -> p n m", n=8),
                        in_=w2_h[ei].rearrange("(n p) m -> p n m", p=128)[
                            :, fh * 8:(fh + 1) * 8, :]))
                    ldb2 = E.dma("gpsimd", bgrp2, lambda e, ei=ei, ws=ws:
                                 e.dma_start(
                        out=b2row[0:1, ws * D:(ws + 1) * D],
                        in_=b2r_h[0:1, ei * D:(ei + 1) * D]))
                    E.wait("tensor", w2grp, ldw2)
                    E.wait("tensor", bgrp2, ldb2)
                    E.wait("tensor", "s", E.c["s"].val)
                    for tt8 in range(8):
                        for ch in range(2):
                            ci = tt8 * 2 + ch
                            ps = (pC, pD, pA, pB)[ci % 4]
                            bk = ("pC", "pD", "pA", "pB")[ci % 4]
                            bt_wait(E, bevm, bk)
                            def mmy(e, ei=ei, fh=fh, tt8=tt8, ch=ch, ps=ps, ws=ws):
                                if fh == 0:
                                    e.matmul(ps[:, 0:512], onesrB[:],
                                             b2row[0:1, ws * D + ch * 512:ws * D + (ch + 1) * 512],
                                             start=True, stop=False)
                                for kt in range(8):
                                    e.matmul(ps[:, 0:512],
                                             hT[:, kt * 1024 + tt8 * 128:kt * 1024 + (tt8 + 1) * 128],
                                             w2b[:, ws * 8 * D + kt * D + ch * 512:
                                                ws * 8 * D + kt * D + (ch + 1) * 512],
                                             start=(fh == 1 and kt == 0),
                                             stop=(kt == 7))
                            E.on("tensor", mmy)
                            tms = E.milestone("tensor", "t")
                            E.wait("vector", "t", tms)
                            def evy(e, grp=grp, ei=ei, tt8=tt8, ch=ch, ps=ps):
                                col = combine[:, (grp * 8 + tt8) * NE0 + ei:
                                              (grp * 8 + tt8) * NE0 + ei + 1]
                                e.scalar_tensor_tensor(
                                    yacc[:, tt8 * D + ch * 512:tt8 * D + (ch + 1) * 512],
                                    ps[:, 0:512], col,
                                    yacc[:, tt8 * D + ch * 512:tt8 * D + (ch + 1) * 512],
                                    ALU.mult, ALU.add)
                            E.on("vector", evy)
                            bt_set(bevm, bk, "v",
                                   E.milestone("vector", "v"))
                    w2_t[ws] = E.c["t"].val
            for tt8 in range(8):
                tt = grp * 8 + tt8
                bo = (tt % 2) * D
                grpn = ("xA", "xB")[tt % 2]
                E.wait("sync", "us", stu)
                if tt >= 2 and (tt - 2) in fin_ms:
                    E.wait("sync", "v", fin_ms[tt - 2])
                else:
                    E.wait("sync", "v", E.c["v"].val)
                ldu = E.dma("sync", grpn, lambda e, tt=tt, bo=bo: e.dma_start(
                    out=xin[:, bo:bo + D], in_=us[tt * 128:(tt + 1) * 128, :]))
                E.wait("vector", grpn, ldu)
                if "out" in E.c:
                    E.wait("vector", "out", E.c["out"].val)
                def vr2(e, tt8=tt8, bo=bo):
                    e.tensor_add(acc[:, 0:D], xin[:, bo:bo + D],
                                 yacc[:, tt8 * D:(tt8 + 1) * D])
                E.on("vector", vr2)
                vr2m = E.milestone("vector", "v")
                emit_ln_tile(E, acc, m1, m2, sqd, epscol, gcast2, bcast2)
                E.wait("sync", "v", vr2m)
                ldx2 = E.dma("sync", grpn, lambda e, tt=tt, bo=bo: e.dma_start(
                    out=xin[:, bo:bo + D], in_=x1_h[tt * 128:(tt + 1) * 128, :]))
                E.wait("vector", grpn, ldx2)
                E.on("vector", lambda e, bo=bo: e.tensor_add(
                    acc[:, 0:D], acc[:, 0:D], xin[:, bo:bo + D]))
                E.milestone("vector", "v")
                fin = emit_ln_tile(E, acc, m1, m2, sqd, epscol)
                fin_ms[tt] = fin
                E.wait("sync", "v", fin)
                E.dma("sync", "out", lambda e, tt=tt: e.dma_start(
                    out=out_h[tt * 128:(tt + 1) * 128, :], in_=acc[:, 0:D]))
        E.wait("sync", "out", E.c["out"].val)
        if "db" in E.c:
            E.on("gpsimd", lambda e, v=E.c["db"].val, s=E.c["db"].sem: e.wait_ge(s, v))
        if not F:
            p.play(ctx.enter_context(nc.Block()))
    return nc





def _consts_stage3():
    T, HDm = S, 64
    half = np.arange(0, HDm, 2, dtype=np.float64)
    inv_freq = 1.0 / 10000.0 ** (half / HDm)
    t = np.arange(T, dtype=np.float64)
    freqs = np.repeat(t[:, None] * inv_freq[None, :], 2, axis=-1)
    scale_vec = (half + 0.4 * HDm) / (1.4 * HDm)
    power = (t - T // 2) / 512.0
    scale = np.repeat(scale_vec[None, :] ** power[:, None], 2, axis=-1)
    cos, sin = np.cos(freqs), np.sin(freqs)
    gamma = 1.0 - 2.0 ** (-5.0 - np.arange(16, dtype=np.float64))
    jj = np.arange(128, dtype=np.float64)
    vbar = gamma[None, :] ** (-jj[:, None])
    endsc = (gamma[None, :] ** jj[:, None]) * (HDm ** -0.5)
    E16 = np.zeros((16, D))
    for h in range(16):
        E16[h, h * 64:(h + 1) * 64] = 1.0
    return {
        "cCS": (cos * scale).astype(np.float32),
        "cSS": (sin * scale).astype(np.float32),
        "cCI": (cos / scale).astype(np.float32),
        "cSI": (sin / scale).astype(np.float32),
        "cVbar": vbar.astype(np.float32),
        "cEnd": endsc.astype(np.float32),
        "cE16": E16.astype(np.float32),
    }


_GAMMA128 = [float((1.0 - 2.0 ** (-5.0 - h)) ** 128) for h in range(16)]


def build_stage3(F=None):
    nc = F["nc"] if F else bass.Bass("TRN2", target_bir_lowering=False)
    x2_h = F["x2"] if F else nc.dram_tensor("x2", [S, D], F32, kind="ExternalInput").ap()
    wqkv_h = F["wqkv"] if F else nc.dram_tensor("wqkv", [3 * D, D], F32, kind="ExternalInput").ap()
    wg_h = F["wgated"] if F else nc.dram_tensor("wgated", [D, D], F32, kind="ExternalInput").ap()
    wp_h = F["wproj"] if F else nc.dram_tensor("wproj", [D, D], F32, kind="ExternalInput").ap()
    gnw_h = F["gnw"] if F else nc.dram_tensor("gnw", [1, D], F32, kind="ExternalInput").ap()
    gnb_h = F["gnb"] if F else nc.dram_tensor("gnb", [1, D], F32, kind="ExternalInput").ap()
    cCS_h = F["cCS"] if F else nc.dram_tensor("cCS", [S, 64], F32, kind="ExternalInput").ap()
    cSS_h = F["cSS"] if F else nc.dram_tensor("cSS", [S, 64], F32, kind="ExternalInput").ap()
    cCI_h = F["cCI"] if F else nc.dram_tensor("cCI", [S, 64], F32, kind="ExternalInput").ap()
    cSI_h = F["cSI"] if F else nc.dram_tensor("cSI", [S, 64], F32, kind="ExternalInput").ap()
    cVb_h = F["cVbar"] if F else nc.dram_tensor("cVbar", [128, 16], F32, kind="ExternalInput").ap()
    cEn_h = F["cEnd"] if F else nc.dram_tensor("cEnd", [128, 16], F32, kind="ExternalInput").ap()
    cE16_h = F["cE16"] if F else nc.dram_tensor("cE16", [16, D], F32, kind="ExternalInput").ap()
    cLTT_h = F["cLTT"] if F else nc.dram_tensor("cLTT", [128, 128], F32, kind="ExternalInput").ap()
    cId_h = F["cIdent"] if F else nc.dram_tensor("cIdent", [128, 128], F32, kind="ExternalInput").ap()
    cOR_h = F["cOnesRow"] if F else nc.dram_tensor("cOnesRow", [1, 128], F32, kind="ExternalInput").ap()
    out_h = F["out"] if F else nc.dram_tensor("out", [S, D], F32, kind="ExternalOutput").ap()
    ybs = nc.dram_tensor("ybs", [NPAIR, 128, S], BF16, kind="Internal").ap()

    NCH = S // 128
    NSTAT = 64 * S

    with ExitStack() as ctx:
        def sb(name, shape, dtype, c=None):
            return (c or ctx).enter_context(nc.sbuf_tensor("c" + name, shape, dtype))
        def psum(name, shape, dtype):
            return ctx.enter_context(nc.psum_tensor("c" + name, shape, dtype))
        def sem(name):
            return ctx.enter_context(nc.semaphore("c" + name))

        ident = sb("identB", [128, 128], BF16)
        identF = sb("identF3", [128, 128], F32)
        onesrF = sb("onesrF3", [1, 128], F32)
        epscol = sb("epscol3", [128, 1], F32)
        ropeCS = sb("ropeCS", [128, NCH * 64], BF16)
        ropeSS = sb("ropeSS", [128, NCH * 64], BF16)
        ropeCI = sb("ropeCI", [128, NCH * 64], BF16)
        ropeSI = sb("ropeSI", [128, NCH * 64], BF16)
        vbarC = sb("vbarC", [128, 16], F32)
        endC = sb("endC", [128, 16], F32)
        e16b = sb("e16b", [16, D], F32)
        Ypair = sb("Ypair", [128, NCH * 128], BF16)
        ytb = sb("ytb", [128, D], BF16)
        statsP = sb("statsP", [128, 2 * 16], F32)
        m1 = sb("m13", [128, 1], F32)
        m2 = sb("m23", [128, 1], F32)
        acc = sb("acc3", [128, D], F32)
        sqd = sb("sqd3", [128, D], F32)
        xin = sb("xin3", [128, 2 * D], F32)
        xinb = sb("xinb3", [128, 2 * D], BF16)
        qTp = sb("qTp", [128, S], BF16)
        kTp = sb("kTp", [128, S], BF16)
        kp = sb("kp", [128, NCH * 128], BF16)
        vp = sb("vp", [128, NCH * 128], BF16)
        vbarT = sb("vbarT", [128, 256], BF16)
        sbT = sb("sbT", [128, 1024], BF16)
        TmA = sb("TmA", [128, 64], F32)
        TmB = sb("TmB", [128, 64], BF16)
        qk3 = sb("qk3", [128, 384], F32)
        rp1 = sb("rp1", [128, 64], F32)
        rp2 = sb("rp2", [128, 64], F32)
        wqkvTp = sb("wqkvTp", [128, ND * 384], BF16)
        wrow = sb("wrow", [128, 3 * D], BF16)
        maskT = sb("maskT", [128, 128], F32)
        ys = sb("ys", [128, 64], F32)
        stat16 = sb("stat16", [16, 4], F32)
        rowsA = sb("rowsA", [1, 4 * D], F32)
        scaleB = sb("scaleB", [128, D], F32)
        biasB = sb("biasB", [128, D], F32)

        phA = ExitStack()
        xT = sb("xT3", [128, ND * S], BF16, phA)

        pA = psum("pA3", [128, 512], F32)
        pB = psum("pB3", [128, 512], F32)
        pC = psum("pC3", [128, 512], F32)
        pD = psum("pD3", [128, 512], F32)
        pT1 = psum("pT13", [128, 128], BF16)
        pT2 = psum("pT23", [128, 128], BF16)
        pY = psum("pY3", [128, 64], F32)
        pSm = psum("pSm3", [128, 128], F32)

        if F:
            E = F["E"]; p = F["p"]
        else:
            s_t0 = sem("s_t3"); s_v0 = sem("s_v3"); s_s0 = sem("s_s3"); s_g0 = sem("s_g3")
            p = Prog()
            E = Emit(p, {"t": s_t0, "v": s_v0, "s": s_s0, "g": s_g0}, nc=nc, ctx=ctx)
        s_t = E.c["t"].sem; s_v = E.c["v"].sem; s_s = E.c["s"].sem; s_g = E.c["g"].sem

        E.dma("gpsimd", "gC", lambda e: e.dma_start(out=ident[:], in_=cId_h[:]))
        E.dma("gpsimd", "gC", lambda e: e.dma_start(out=onesrF[:], in_=cOR_h[:]))
        E.dma("sync", "sC", lambda e: e.dma_start(out=identF[:], in_=cId_h[:]))
        E.dma("sync", "sC", lambda e: e.dma_start(out=maskT[:], in_=cLTT_h[:]))
        for dst, src_h in ((ropeCS, cCS_h), (ropeSS, cSS_h), (ropeCI, cCI_h),
                           (ropeSI, cSI_h)):
            E.dma("gpsimd", "gC", lambda e, dst=dst, src_h=src_h: e.dma_start(
                out=dst[:].rearrange("p (n m) -> p n m", n=NCH),
                in_=src_h.rearrange("(n p) m -> p n m", p=128)))
        E.dma("gpsimd", "gC", lambda e: e.dma_start(out=vbarC[:], in_=cVb_h[:]))
        E.dma("gpsimd", "gC", lambda e: e.dma_start(out=endC[:], in_=cEn_h[:]))
        E.dma("gpsimd", "gC", lambda e: e.dma_start(out=e16b[:], in_=cE16_h[:]))
        E.dma("sync", "sC", lambda e: e.dma_start(out=rowsA[0:1, 2 * D:3 * D], in_=gnw_h[:]))
        E.dma("sync", "sC", lambda e: e.dma_start(out=rowsA[0:1, 3 * D:4 * D], in_=gnb_h[:]))
        ld_g = E.c["gC"].val
        ld_s = E.c["sC"].val
        def vinit(e):
            e.memset(epscol[:], LN_EPS)
            e.memset(statsP[:, :], 0.0)
        E.on("vector", vinit)
        E.wait("vector", "gC", ld_g)
        E.wait("vector", "sC", ld_s)
        E.milestone("vector", "v")
        E.wait("tensor", "gC", ld_g)
        E.wait("tensor", "sC", ld_s)

        emit_x_to_xT(E, x2_h, NT, xT, xin, xinb, ident, pT1, pT2, S)
        xt_t = E.c["t"].val
        E.wait("vector", "t", xt_t)

        bev3 = {}
        for pr in range(NPAIR):
            ldw = E.dma("gpsimd", "wT", lambda e, pr=pr: e.dma_start(
                out=wrow[:].rearrange("p (n m) -> p n m", n=3),
                in_=wqkv_h.rearrange("(n p) m -> p n m", p=128)[:, pr:pr + 17:8, :]))
            E.wait("tensor", "wT", ldw)
            for wj in range(3):
                for dt in range(ND):
                    pt = (pT1, pT2)[dt % 2]
                    bt_wait(E, bev3, "pT%d" % (dt % 2))
                    def trw(e, wj=wj, dt=dt, pt=pt):
                        e.transpose(pt[:], wrow[:, wj * D + dt * 128:wj * D + (dt + 1) * 128],
                                    ident[:])
                    E.on("tensor", trw)
                    tms = E.milestone("tensor", "t")
                    E.wait("vector", "t", tms)
                    def evw(e, wj=wj, dt=dt, pt=pt):
                        e.tensor_copy(
                            wqkvTp[:, dt * 384 + wj * 128:dt * 384 + (wj + 1) * 128],
                            pt[:])
                    E.on("vector", evw)
                    bt_set(bev3, "pT%d" % (dt % 2), "v",
                           E.milestone("vector", "v"))
            E.wait("tensor", "v", E.c["v"].val)
            trqk_t = {}
            for ch in range(NCH):
                qps = (pA, pB)[ch % 2]
                bt_wait(E, bev3, ("pA", "pB")[ch % 2])
                def mmqkv(e, ch=ch, qps=qps):
                    for kt in range(ND):
                        e.matmul(qps[:, 0:384],
                                 xT[:, kt * S + ch * 128:kt * S + (ch + 1) * 128],
                                 wqkvTp[:, kt * 384:(kt + 1) * 384],
                                 start=(kt == 0), stop=(kt == ND - 1))
                E.on("tensor", mmqkv)
                tq = E.milestone("tensor", "t")
                E.wait("vector", "t", tq)
                def vevq(e, qps=qps):
                    e.tensor_copy(qk3[:, :], qps[:, 0:384])
                E.on("vector", vevq)
                bt_set(bev3, ("pA", "pB")[ch % 2], "v",
                       E.milestone("vector", "v"))
                def vrope(e, ch=ch):
                    for sub in range(2):
                        q0 = sub * 64
                        e.tensor_scalar_mul(rp1[:, 0:64:2], qk3[:, q0 + 1:q0 + 64:2], -1.0)
                        e.tensor_copy(rp1[:, 1:64:2], qk3[:, q0:q0 + 64:2])
                        e.tensor_scalar_mul(rp2[:, 0:64:2], qk3[:, 128 + q0 + 1:128 + q0 + 64:2], -1.0)
                        e.tensor_copy(rp2[:, 1:64:2], qk3[:, 128 + q0:128 + q0 + 64:2])
                        e.tensor_mul(qk3[:, q0:q0 + 64], qk3[:, q0:q0 + 64],
                                     ropeCS[:, ch * 64:(ch + 1) * 64])
                        e.tensor_mul(rp1[:, 0:64], rp1[:, 0:64],
                                     ropeSS[:, ch * 64:(ch + 1) * 64])
                        e.tensor_add(qk3[:, q0:q0 + 64], qk3[:, q0:q0 + 64], rp1[:, 0:64])
                        e.tensor_mul(qk3[:, 128 + q0:128 + q0 + 64],
                                     qk3[:, 128 + q0:128 + q0 + 64],
                                     ropeCI[:, ch * 64:(ch + 1) * 64])
                        e.tensor_mul(rp2[:, 0:64], rp2[:, 0:64],
                                     ropeSI[:, ch * 64:(ch + 1) * 64])
                        e.tensor_add(qk3[:, 128 + q0:128 + q0 + 64],
                                     qk3[:, 128 + q0:128 + q0 + 64], rp2[:, 0:64])
                    e.tensor_copy(kp[:, ch * 128:(ch + 1) * 128], qk3[:, 128:256])
                    e.tensor_copy(vp[:, ch * 128:(ch + 1) * 128], qk3[:, 256:384])
                    xo = (ch % 2) * 256
                    e.tensor_copy(xinb[:, xo:xo + 128], qk3[:, 0:128])
                    e.tensor_copy(xinb[:, xo + 128:xo + 256], qk3[:, 128:256])
                if ch >= 2:
                    E.wait("vector", "t", trqk_t[ch - 2])
                E.on("vector", vrope)
                vr = E.milestone("vector", "v")
                E.wait("tensor", "v", vr)
                xo = (ch % 2) * 256
                for which in range(2):
                    pt = (pT1, pT2)[which]
                    bt_wait(E, bev3, "pT%d" % which)
                    def trqk(e, which=which, pt=pt, xo=xo):
                        e.transpose(pt[:], xinb[:, xo + which * 128:xo + (which + 1) * 128],
                                    ident[:])
                    E.on("tensor", trqk)
                    tms = E.milestone("tensor", "t")
                    E.wait("vector", "t", tms)
                    def evqk(e, ch=ch, which=which, pt=pt):
                        dst = (qTp, kTp)[which]
                        e.tensor_copy(dst[:, ch * 128:(ch + 1) * 128], pt[:])
                    E.on("vector", evqk)
                    bt_set(bev3, "pT%d" % which, "v", E.milestone("vector", "v"))
                trqk_t[ch] = E.c["t"].val
            E.wait("tensor", "v", E.c["v"].val)

            def tminit(e):
                e.memset(TmA[:, :], 0.0)
                e.memset(TmB[:, :], 0.0)
            E.on("vector", tminit)
            tm0 = E.milestone("vector", "v")
            E.wait("tensor", "v", tm0)
            vvb_ms = {}
            evs_ms = {}
            evA_ms = {}
            mma_t = {}
            mmy_t = {}
            for m in range(NCH):
                for sub in range(2):
                    h = pr * 2 + sub
                    vsl = (m % 2) * 128 + sub * 64
                    ssl = (m % 2) * 512 + sub * 128
                    sck = ("pSm", "pB")[sub]
                    yk = ("pY", "pD")[sub]
                    ak = ("pC", "pA")[sub]
                    psc = (pSm, pB)[sub]
                    py_ = (pY, pD)[sub]
                    pa_ = (pC, pA)[sub]
                    if (m - 2, sub) in mma_t:
                        E.wait("vector", "t", mma_t[(m - 2, sub)])
                    def vvb(e, m=m, sub=sub, h=h, vsl=vsl):
                        e.tensor_scalar(vbarT[:, vsl:vsl + 64],
                                        vp[:, m * 128 + sub * 64:m * 128 + sub * 64 + 64],
                                        vbarC[:, h:h + 1], None, ALU.mult)
                    E.on("vector", vvb)
                    vvb_ms[(m, sub)] = E.milestone("vector", "v")
                    bt_wait(E, bev3, sck)
                    def mms(e, m=m, sub=sub, psc=psc):
                        e.matmul(psc[:, 0:128],
                                 kTp[sub * 64:(sub + 1) * 64, m * 128:(m + 1) * 128],
                                 qTp[sub * 64:(sub + 1) * 64, m * 128:(m + 1) * 128],
                                 start=True, stop=True)
                    E.on("tensor", mms)
                    tsb = E.milestone("tensor", "t")
                    E.wait("vector", "t", tsb)
                    if (m - 2, sub) in mmy_t:
                        E.wait("vector", "t", mmy_t[(m - 2, sub)])
                    def evs(e, ssl=ssl, psc=psc):
                        e.scalar_tensor_tensor(sbT[:, ssl:ssl + 128], psc[:, 0:128],
                                               1.0, maskT[:, :], ALU.mult, ALU.mult)
                    E.on("vector", evs)
                    evs_ms[(m, sub)] = E.milestone("vector", "v")
                    bt_set(bev3, sck, "v", evs_ms[(m, sub)])
                    bt_wait(E, bev3, yk)
                    E.wait("tensor", "v", evs_ms[(m, sub)])
                    E.wait("tensor", "v", vvb_ms[(m, sub)])
                    if (m - 1, sub) in evA_ms:
                        E.wait("tensor", "v", evA_ms[(m - 1, sub)])
                    def mmy3(e, m=m, sub=sub, vsl=vsl, ssl=ssl, py_=py_):
                        e.matmul(py_[:, 0:64], sbT[:, ssl:ssl + 128],
                                 vbarT[:, vsl:vsl + 64],
                                 start=True, stop=(m == 0))
                        if m > 0:
                            e.matmul(py_[:, 0:64],
                                     qTp[sub * 64:(sub + 1) * 64, m * 128:(m + 1) * 128],
                                     TmB[sub * 64:(sub + 1) * 64, :],
                                     start=False, stop=True)
                    E.on("tensor", mmy3)
                    tym = E.milestone("tensor", "t")
                    mmy_t[(m, sub)] = tym
                    E.wait("vector", "t", tym)
                    def evy3(e, m=m, sub=sub, h=h, py_=py_):
                        e.tensor_scalar(ys[:, :], py_[:, 0:64], endC[:, h:h + 1],
                                        None, ALU.mult)
                        e.reduce_sum(m1[:], ys[:, :], axis=AX.X)
                        e.tensor_add(statsP[:, h:h + 1], statsP[:, h:h + 1], m1[:])
                        e.tensor_mul(sqd[:, 0:64], ys[:, :], ys[:, :])
                        e.reduce_sum(m1[:], sqd[:, 0:64], axis=AX.X)
                        e.tensor_add(statsP[:, 16 + h:16 + h + 1],
                                     statsP[:, 16 + h:16 + h + 1], m1[:])
                        e.tensor_copy(Ypair[:, m * 128 + sub * 64:m * 128 + (sub + 1) * 64],
                                      ys[:, :])
                    E.on("vector", evy3)
                    bt_set(bev3, yk, "v", E.milestone("vector", "v"))
                    bt_wait(E, bev3, ak)
                    def mma(e, m=m, sub=sub, vsl=vsl, pa_=pa_):
                        e.matmul(pa_[sub * 64:(sub + 1) * 64, 0:64],
                                 kp[:, m * 128 + sub * 64:m * 128 + sub * 64 + 64],
                                 vbarT[:, vsl:vsl + 64], start=True, stop=True,
                                 tile_position=(0, sub * 64))
                    E.on("tensor", mma)
                    tam = E.milestone("tensor", "t")
                    mma_t[(m, sub)] = tam
                    E.wait("vector", "t", tam)
                    def evA(e, sub=sub, h=h, pa_=pa_):
                        sl = slice(sub * 64, (sub + 1) * 64)
                        e.tensor_add(TmA[sl, :], TmA[sl, :], pa_[sl, 0:64])
                        e.tensor_scalar_mul(TmA[sl, :], TmA[sl, :], _GAMMA128[h])
                        e.tensor_copy(TmB[sl, :], TmA[sl, :])
                    E.on("vector", evA)
                    evA_ms[(m, sub)] = E.milestone("vector", "v")
                    bt_set(bev3, ak, "v", evA_ms[(m, sub)])
            ypm = E.c["v"].val
            E.on("gpsimd", lambda e, ypm=ypm: e.wait_ge(E.c["v"].sem, ypm))
            ybst = E.dma("gpsimd", "yb", lambda e, pr=pr: e.dma_start(
                out=ybs[pr], in_=Ypair[:, :]))
            E.wait("vector", "yb", ybst)

        for k in ("pSm", "pA", "pB", "pC", "pD", "pY"):
            bt_wait(E, bev3, k)
        E.wait("tensor", "v", E.c["v"].val)
        for half in range(2):
            pt = (pSm, pSm)[0]
            def trst(e, half=half):
                e.transpose(pSm[0:16, :], statsP[:, half * 16:(half + 1) * 16],
                            identF[:])
            E.on("tensor", trst)
            tms = E.milestone("tensor", "t")
            E.wait("vector", "t", tms)
            def evst(e, half=half):
                e.reduce_sum(stat16[:, half:half + 1], pSm[0:16, :], axis=AX.X)
            E.on("vector", evst)
            vms = E.milestone("vector", "v")
            E.wait("tensor", "v", vms)
        def vstats(e):
            e.tensor_scalar_mul(stat16[:, 2:3], stat16[:, 0:1], 1.0 / NSTAT)
            e.tensor_scalar_mul(stat16[:, 1:2], stat16[:, 1:2], 1.0 / NSTAT)
            e.tensor_mul(stat16[:, 3:4], stat16[:, 2:3], stat16[:, 2:3])
            e.tensor_sub(stat16[:, 3:4], stat16[:, 1:2], stat16[:, 3:4])
            e.tensor_scalar_add(stat16[:, 3:4], stat16[:, 3:4], LN_EPS)
        E.on("vector", vstats)
        vsm = E.milestone("vector", "v")
        E.wait("scalar", "v", vsm)
        E.on("scalar", lambda e: e.activation(stat16[:, 3:4], stat16[:, 3:4], AF.Sqrt))
        ssm = E.milestone("scalar", "s")
        E.wait("vector", "s", ssm)
        E.on("vector", lambda e: e.reciprocal(stat16[:, 3:4], stat16[:, 3:4]))
        vrm = E.milestone("vector", "v")
        E.wait("tensor", "v", vrm)
        for which in range(2):
            col = 2 + which
            for chh in range(2):
                ps2 = (pA, pB)[chh]
                def mme(e, col=col, chh=chh, ps2=ps2):
                    e.matmul(ps2[0:1, :], stat16[0:16, col:col + 1],
                             e16b[:, chh * 512:(chh + 1) * 512],
                             start=True, stop=True)
                E.on("tensor", mme)
                tms = E.milestone("tensor", "t")
                E.wait("vector", "t", tms)
                def eve(e, which=which, chh=chh, ps2=ps2):
                    e.tensor_copy(rowsA[0:1, which * D + chh * 512:which * D + (chh + 1) * 512],
                                  ps2[0:1, :])
                E.on("vector", eve)
                vms = E.milestone("vector", "v")
                E.wait("tensor", "v", vms)
        def vrows(e):
            e.tensor_mul(rowsA[0:1, D:2 * D], rowsA[0:1, D:2 * D],
                         rowsA[0:1, 2 * D:3 * D])
            e.tensor_mul(rowsA[0:1, 0:D], rowsA[0:1, 0:D], rowsA[0:1, D:2 * D])
            e.tensor_sub(rowsA[0:1, 0:D], rowsA[0:1, 3 * D:4 * D], rowsA[0:1, 0:D])
        E.on("vector", vrows)
        vrwm = E.milestone("vector", "v")
        E.wait("tensor", "v", vrwm)
        emit_bcast_row(E, rowsA[0:1, D:2 * D], scaleB, onesrF, pC)
        emit_bcast_row(E, rowsA[0:1, 0:D], biasB, onesrF, pC)

        emit_barrier(E)
        phA.close()
        wgT = sb("wgT", [128, ND * D], BF16)
        wpT = sb("wpT", [128, ND * D], BF16)
        wtmp3 = sb("wtmp3", [128, ND * D], BF16)
        xTt = sb("xTt", [128, ND * 128], BF16)
        gt = sb("gt", [128, D], F32)
        emit_w_transpose(E, wg_h, D, wtmp3, wgT, D, ident, pT1, pT2, half_rows=8)
        emit_w_transpose(E, wp_h, D, wtmp3, wpT, D, ident, pT1, pT2, half_rows=8)

        res_ms = {}
        for tt in range(NT):
            bo = (tt % 2) * D
            grpn = ("xA", "xB")[tt % 2]
            if tt >= 2:
                E.wait("sync", "v", res_ms[tt - 2])
            else:
                E.wait("sync", "t", E.c["t"].val)
            ldx = E.dma("sync", grpn, lambda e, tt=tt, bo=bo: e.dma_start(
                out=xin[:, bo:bo + D], in_=x2_h[tt * 128:(tt + 1) * 128, :]))
            E.wait("vector", grpn, ldx)
            if tt < 2:
                E.wait("vector", "t", E.c["t"].val)
            E.on("vector", lambda e, bo=bo: e.tensor_copy(xinb[:, bo:bo + D],
                                                          xin[:, bo:bo + D]))
            cvm = E.milestone("vector", "v")
            E.wait("tensor", "v", cvm)
            for dt in range(ND):
                pt = (pT1, pT2)[dt % 2]
                bt_wait(E, bev3, "pT%d" % (dt % 2))
                def trx(e, bo=bo, dt=dt, pt=pt):
                    e.transpose(pt[:], xinb[:, bo + dt * 128:bo + (dt + 1) * 128],
                                ident[:])
                E.on("tensor", trx)
                tms = E.milestone("tensor", "t")
                E.wait("vector", "t", tms)
                def evx(e, dt=dt, pt=pt):
                    e.tensor_copy(xTt[:, dt * 128:(dt + 1) * 128], pt[:])
                E.on("vector", evx)
                bt_set(bev3, "pT%d" % (dt % 2), "v", E.milestone("vector", "v"))
            E.on("gpsimd", lambda e, v=E.c["v"].val: e.wait_ge(E.c["v"].sem, v))
            ldy = E.dma("gpsimd", "yt", lambda e, tt=tt: e.dma_start(
                out=ytb[:].rearrange("p (n m) -> p n m", n=NPAIR),
                in_=ybs[:, :, tt * 128:(tt + 1) * 128].rearrange("n p m -> p n m")))
            E.wait("vector", "yt", ldy)
            E.wait("tensor", "v", E.c["v"].val)
            bt_wait(E, bev3, "pA")
            bt_wait(E, bev3, "pB")
            for chh in range(2):
                ps2 = (pA, pB)[chh]
                def mmg3(e, chh=chh, ps2=ps2):
                    for kt in range(ND):
                        e.matmul(ps2[:, :], xTt[:, kt * 128:(kt + 1) * 128],
                                 wgT[:, kt * D + chh * 512:kt * D + (chh + 1) * 512],
                                 start=(kt == 0), stop=(kt == ND - 1))
                E.on("tensor", mmg3)
            tms = E.milestone("tensor", "t")
            E.wait("scalar", "t", tms)
            def evg3(e):
                e.activation(gt[:, 0:512], pA[:, :], AF.Silu)
                e.activation(gt[:, 512:1024], pB[:, :], AF.Silu)
            E.on("scalar", evg3)
            sms = E.milestone("scalar", "s")
            bt_set(bev3, "pA", "s", sms)
            bt_set(bev3, "pB", "s", sms)
            E.wait("vector", "s", sms)
            if "out" in E.c:
                E.wait("vector", "out", E.c["out"].val)
            def vz3(e, tt=tt):
                e.tensor_copy(acc[:, 0:D], ytb[:, 0:D])
                e.tensor_mul(acc[:, 0:D], acc[:, 0:D], scaleB[:, 0:D])
                e.tensor_add(acc[:, 0:D], acc[:, 0:D], biasB[:, 0:D])
                e.tensor_mul(acc[:, 0:D], acc[:, 0:D], gt[:, 0:D])
                e.tensor_copy(xinb[:, 0:D], acc[:, 0:D])
            E.on("vector", vz3)
            vzm = E.milestone("vector", "v")
            E.wait("tensor", "v", vzm)
            for dt in range(ND):
                pt = (pT1, pT2)[dt % 2]
                bt_wait(E, bev3, "pT%d" % (dt % 2))
                def trz(e, dt=dt, pt=pt):
                    e.transpose(pt[:], xinb[:, dt * 128:(dt + 1) * 128], ident[:])
                E.on("tensor", trz)
                tms = E.milestone("tensor", "t")
                E.wait("vector", "t", tms)
                def evz(e, dt=dt, pt=pt):
                    e.tensor_copy(xTt[:, dt * 128:(dt + 1) * 128], pt[:])
                E.on("vector", evz)
                bt_set(bev3, "pT%d" % (dt % 2), "v", E.milestone("vector", "v"))
            E.wait("tensor", "v", E.c["v"].val)
            bt_wait(E, bev3, "pC")
            bt_wait(E, bev3, "pD")
            for chh in range(2):
                ps2 = (pC, pD)[chh]
                def mmp3(e, chh=chh, ps2=ps2):
                    for kt in range(ND):
                        e.matmul(ps2[:, :], xTt[:, kt * 128:(kt + 1) * 128],
                                 wpT[:, kt * D + chh * 512:kt * D + (chh + 1) * 512],
                                 start=(kt == 0), stop=(kt == ND - 1))
                E.on("tensor", mmp3)
            tms = E.milestone("tensor", "t")
            E.wait("vector", "t", tms)
            if "out" in E.c:
                E.wait("vector", "out", E.c["out"].val)
            def vres3(e, bo=bo):
                e.scalar_tensor_tensor(acc[:, 0:512], xin[:, bo:bo + 512], 1.0,
                                       pC[:, :], ALU.mult, ALU.add)
                e.scalar_tensor_tensor(acc[:, 512:1024], xin[:, bo + 512:bo + D],
                                       1.0, pD[:, :], ALU.mult, ALU.add)
            E.on("vector", vres3)
            vrm3 = E.milestone("vector", "v")
            bt_set(bev3, "pC", "v", vrm3)
            bt_set(bev3, "pD", "v", vrm3)
            fin = emit_ln_tile(E, acc, m1, m2, sqd, epscol)
            res_ms[tt] = fin
            E.wait("sync", "v", fin)
            E.dma("sync", "out", lambda e, tt=tt: e.dma_start(
                out=out_h[tt * 128:(tt + 1) * 128, :], in_=acc[:, 0:D]))

        E.wait("sync", "out", E.c["out"].val)
        if not F:
            p.play(ctx.enter_context(nc.Block()))
    return nc





SLW = 4224
PEW = 8192
PEW2 = 6144
NTM4 = MY // 128


def _consts_stage4():
    pos = np.arange(S - 1, -S, -1, dtype=np.float64)
    inv_freq = 1.0 / 10000.0 ** (np.arange(0, D, 2, dtype=np.float64) / D)
    ang = pos[:, None] * inv_freq[None, :]
    pe = np.concatenate([np.sin(ang), np.cos(ang)], axis=-1)
    peT_rev = pe[::-1].T.copy()
    peT_pad = np.zeros((D, PEW), dtype=np.float64)
    peT_pad[:, :2 * S - 1] = peT_rev
    return {0: np.ascontiguousarray(peT_pad[:, 2048:2048 + PEW2]).astype(np.float32),
            1: np.ascontiguousarray(peT_pad[:, 0:PEW2]).astype(np.float32)}


def _consts_sel(half):
    ident = np.eye(128, dtype=np.float32)
    zero = np.zeros((128, 128), dtype=np.float32)
    return (ident, zero) if half == 0 else (zero, ident)


def build_stage4(F=None):
    nc = F["nc"] if F else bass.Bass("TRN2", target_bir_lowering=False)
    x3_h = F["x3"] if F else nc.dram_tensor("x3", [S, D], F32, kind="ExternalInput").ap()
    wq_h = F["wq4"] if F else nc.dram_tensor("wq4", [D, D], F32, kind="ExternalInput").ap()
    wk_h = F["wk4"] if F else nc.dram_tensor("wk4", [D, D], F32, kind="ExternalInput").ap()
    wv_h = F["wv4"] if F else nc.dram_tensor("wv4", [D, D], F32, kind="ExternalInput").ap()
    wp_h = F["wp4"] if F else nc.dram_tensor("wp4", [D, D], F32, kind="ExternalInput").ap()
    wo_h = F["wo4"] if F else nc.dram_tensor("wo4", [D, D], F32, kind="ExternalInput").ap()
    g1_h = F["ln1g4"] if F else nc.dram_tensor("ln1g4", [1, D], F32, kind="ExternalInput").ap()
    b1_h = F["ln1b4"] if F else nc.dram_tensor("ln1b4", [1, D], F32, kind="ExternalInput").ap()
    g2_h = F["ln2g4"] if F else nc.dram_tensor("ln2g4", [1, D], F32, kind="ExternalInput").ap()
    b2_h = F["ln2b4"] if F else nc.dram_tensor("ln2b4", [1, D], F32, kind="ExternalInput").ap()
    selw_h = F["selw"] if F else nc.dram_tensor("selw", [D, 32], F32, kind="ExternalInput").ap()
    keys_h = F["keys"] if F else nc.dram_tensor("keys", [32, D, 128], F32, kind="ExternalInput").ap()
    vals_h = F["vals"] if F else nc.dram_tensor("vals", [32, 128, D], F32, kind="ExternalInput").ap()
    pes_h = F["peslice"] if F else nc.dram_tensor("peslice", [D, PEW2], F32, kind="ExternalInput").ap()
    sel0_h = F["sel0"] if F else nc.dram_tensor("sel0", [128, 128], F32, kind="ExternalInput").ap()
    sel1_h = F["sel1"] if F else nc.dram_tensor("sel1", [128, 128], F32, kind="ExternalInput").ap()
    cId_h = F["cIdent"] if F else nc.dram_tensor("cIdent", [128, 128], F32, kind="ExternalInput").ap()
    cOR_h = F["cOnesRow"] if F else nc.dram_tensor("cOnesRow", [1, 128], F32, kind="ExternalInput").ap()
    out_h = F["out"] if F else nc.dram_tensor("out", [MY, D], F32, kind="ExternalOutput").ap()
    kts = nc.dram_tensor("kts4", [NPAIR, 128, S], BF16, kind="Internal").ap()
    qts = nc.dram_tensor("qts4", [NPAIR, 128, MY], BF16, kind="Internal").ap()
    vts = nc.dram_tensor("vts4", [NPAIR, S, 128], BF16, kind="Internal").ap()
    pts = nc.dram_tensor("pts4", [NPAIR, 128, PEW2], BF16, kind="Internal").ap()
    rts_t = nc.dram_tensor("rts4", [2, 128, SLW], BF16, kind="Internal")
    rts = rts_t.ap()
    us4 = nc.dram_tensor("us4", [MY, D], F32, kind="Internal").ap()
    x3ms = nc.dram_tensor("x3ms", [MY, D], F32, kind="Internal").ap()
    aoTs4 = nc.dram_tensor("aoTs4", [128, ND * MY], BF16, kind="Internal").ap()
    X2hTs = nc.dram_tensor("X2hTs", [128, ND * MY], BF16, kind="Internal").ap()

    with ExitStack() as ctx:
        def sb(name, shape, dtype, c=None):
            return (c or ctx).enter_context(nc.sbuf_tensor("d" + name, shape, dtype))
        def psum(name, shape, dtype):
            return ctx.enter_context(nc.psum_tensor("d" + name, shape, dtype))
        def sem(name):
            return ctx.enter_context(nc.semaphore("d" + name))

        ident = sb("ident4", [128, 128], BF16)
        onesrF = sb("onesrF4", [1, 128], F32)
        epscol = sb("epscol4", [128, 1], F32)
        m1 = sb("m14", [128, 1], F32)
        m2 = sb("m24", [128, 1], F32)
        acc = sb("acc4", [128, D], F32)
        sqd = sb("sqd4", [128, D], F32)
        xin = sb("xin4", [128, 2 * D], F32)
        xinb = sb("xinb4", [128, 2 * D], BF16)
        rows1 = sb("rows14", [1, D], F32)
        gcast1 = sb("gcast14", [128, D], F32)
        bcast1 = sb("bcast14", [128, D], F32)
        gcast2 = sb("gcast24", [128, D], F32)
        bcast2 = sb("bcast24", [128, D], F32)
        gates = sb("gates4", [128, NTM4 * 32], F32)
        selb = sb("selb", [128, 64], F32)
        selc = sb("selc", [128, 40], F32)
        sel0sb = sb("sel0sb", [128, 128], F32)
        sel1sb = sb("sel1sb", [128, 128], F32)
        xselb = sb("xselb", [128, D], F32)

        phA = ExitStack()
        XhT = sb("XhT", [128, ND * S], BF16, phA)
        XhMT = sb("XhMT", [128, ND * MY], BF16, phA)
        wsl = sb("wsl", [128, ND * 512], BF16, phA)
        peb = sb("peb", [128, 2 * ND * 512], BF16, phA)
        qTp4 = sb("qTp4", [128, MY], BF16, phA)
        kTp4 = sb("kTp4", [128, S], BF16, phA)
        vpb = sb("vpb", [128, NT * 128], BF16, phA)
        PThp = sb("PThp", [128, PEW2], BF16, phA)

        pA = psum("pA4", [128, 512], F32)
        pB = psum("pB4", [128, 512], F32)
        pO = psum("pO4", [64, 128], F32)
        pO2 = psum("pO24", [64, 128], F32)
        pQ1 = psum("pQ14", [128, 128], BF16)
        pQ2 = psum("pQ24", [128, 128], BF16)
        pQ3 = psum("pQ34", [128, 128], BF16)
        pQ4 = psum("pQ44", [128, 128], BF16)
        pT1, pT2 = pQ1, pQ2

        if F:
            E = F["E"]; p = F["p"]
        else:
            s_t0 = sem("s_t4"); s_v0 = sem("s_v4"); s_s0 = sem("s_s4"); s_g0 = sem("s_g4")
            p = Prog()
            E = Emit(p, {"t": s_t0, "v": s_v0, "s": s_s0, "g": s_g0}, nc=nc, ctx=ctx)
        s_t = E.c["t"].sem; s_v = E.c["v"].sem; s_s = E.c["s"].sem; s_g = E.c["g"].sem

        E.dma("gpsimd", "gC", lambda e: e.dma_start(out=ident[:], in_=cId_h[:]))
        E.dma("gpsimd", "gC", lambda e: e.dma_start(out=onesrF[:], in_=cOR_h[:]))
        E.dma("gpsimd", "gC", lambda e: e.dma_start(out=sel0sb[:], in_=sel0_h[:]))
        E.dma("gpsimd", "gC", lambda e: e.dma_start(out=sel1sb[:], in_=sel1_h[:]))
        ld_g = E.c["gC"].val
        E.on("vector", lambda e: e.memset(epscol[:], LN_EPS))
        E.wait("vector", "gC", ld_g)
        E.milestone("vector", "v")
        E.wait("tensor", "gC", ld_g)

        for row_h, dstc in ((g1_h, gcast1), (b1_h, bcast1), (g2_h, gcast2),
                            (b2_h, bcast2)):
            E.wait("sync", "t", E.c["t"].val)
            ldr = E.dma("sync", "sC", lambda e, row_h=row_h: e.dma_start(
                out=rows1[0:1, 0:D], in_=row_h[:]))
            E.wait("tensor", "sC", ldr)
            emit_bcast_row(E, rows1[0:1, 0:D], dstc, onesrF, pA)

        def emit_ln_transpose(src_h, ntiles, dstT, SW):
            xin_free = {}
            trans_done = {}
            bev = {}
            for tt in range(ntiles):
                bo = (tt % 2) * D
                grpn = ("xA", "xB")[tt % 2]
                if tt >= 2:
                    E.wait("sync", "v", xin_free[tt - 2])
                else:
                    E.wait("sync", "v", E.c["v"].val)
                    E.wait("sync", "t", E.c["t"].val)
                ldx = E.dma("sync", grpn, lambda e, tt=tt, bo=bo: e.dma_start(
                    out=xin[:, bo:bo + D], in_=src_h[tt * 128:(tt + 1) * 128, :]))
                E.wait("vector", grpn, ldx)
                if tt < 2:
                    E.wait("vector", "t", E.c["t"].val)
                def vld(e, bo=bo):
                    e.tensor_copy(acc[:, 0:D], xin[:, bo:bo + D])
                E.on("vector", vld)
                xin_free[tt] = E.milestone("vector", "v")
                if tt >= 2:
                    E.wait("vector", "t", trans_done[tt - 2])
                lnm = emit_ln_tile(E, acc, m1, m2, sqd, epscol, gcast1, bcast1,
                                   out_bf=xinb[:, bo:bo + D])
                E.wait("tensor", "v", lnm)
                for dt in range(ND):
                    pt = (pT1, pT2)[dt % 2]
                    bt_wait(E, bev, dt % 2)
                    def trx(e, bo=bo, dt=dt, pt=pt):
                        e.transpose(pt[:], xinb[:, bo + dt * 128:bo + (dt + 1) * 128],
                                    ident[:])
                    E.on("tensor", trx)
                    tms = E.milestone("tensor", "t")
                    E.wait("vector", "t", tms)
                    def evx(e, tt=tt, dt=dt, pt=pt, dstT=dstT, SW=SW):
                        e.tensor_copy(dstT[:, dt * SW + tt * 128:dt * SW + (tt + 1) * 128],
                                      pt[:])
                    E.on("vector", evx)
                    bt_set(bev, dt % 2, "v", E.milestone("vector", "v"))
                trans_done[tt] = E.c["t"].val
            E.wait("tensor", "v", E.c["v"].val)
        emit_ln_transpose(x3_h, NT, XhT, S)

        E.dma_group("x3m")
        bevA = {}
        selp_tr = {}
        x3m_st = 0
        trm_last = 0
        for tp in range(NTM4):
            if tp >= 1:
                E.wait("sync", "t", selp_tr[tp - 1])
            else:
                E.wait("sync", "v", E.c["v"].val)
                E.wait("sync", "t", E.c["t"].val)
            lda = E.dma("sync", "xA", lambda e, tp=tp: e.dma_start(
                out=xin[:, 0:D], in_=x3_h[tp * 128:(tp + 1) * 128, :]))
            ldb = E.dma("sync", "xB", lambda e, tp=tp: e.dma_start(
                out=xin[:, D:2 * D], in_=x3_h[(tp + 16) * 128:(tp + 17) * 128, :]))
            E.wait("tensor", "xA", lda)
            E.wait("tensor", "xB", ldb)
            if tp == 0:
                E.wait("tensor", "v", E.c["v"].val)
            for ch in range(2):
                ps = (pA, pB)[ch]
                def mmsel4(e, ch=ch, ps=ps):
                    e.matmul(ps[:, :], sel0sb[:], xin[:, ch * 512:(ch + 1) * 512],
                             start=True, stop=False)
                    e.matmul(ps[:, :], sel1sb[:],
                             xin[:, D + ch * 512:D + (ch + 1) * 512],
                             start=False, stop=True)
                E.on("tensor", mmsel4)
            selp_tr[tp] = E.milestone("tensor", "t")
            E.wait("vector", "t", selp_tr[tp])
            E.wait("vector", "x3m", x3m_st)
            def evsel(e):
                e.raw.tensor_copy(acc[:, 0:512], pA[:, :])
                e.tensor_copy(acc[:, 512:1024], pB[:, :])
            E.on("vector", evsel)
            evm = E.milestone("vector", "v")
            E.wait("tensor", "v", evm)
            E.wait("sync", "v", evm)
            x3m_st = E.dma("sync", "x3m", lambda e, tp=tp: e.dma_start(
                out=x3ms[tp * 128:(tp + 1) * 128, :], in_=acc[:, 0:D]))
            E.wait("vector", "x3m", x3m_st)
            E.wait("vector", "t", trm_last)
            lnm = emit_ln_tile(E, acc, m1, m2, sqd, epscol, gcast1, bcast1,
                               out_bf=xinb[:, 0:D])
            E.wait("tensor", "v", lnm)
            for dt in range(ND):
                pt = (pT1, pT2)[dt % 2]
                bt_wait(E, bevA, dt % 2)
                def trxm(e, dt=dt, pt=pt):
                    e.transpose(pt[:], xinb[:, dt * 128:(dt + 1) * 128], ident[:])
                E.on("tensor", trxm)
                tms = E.milestone("tensor", "t")
                E.wait("vector", "t", tms)
                def evxm(e, tp=tp, dt=dt, pt=pt):
                    e.tensor_copy(
                        XhMT[:, dt * MY + tp * 128:dt * MY + (tp + 1) * 128],
                        pt[:])
                E.on("vector", evxm)
                bt_set(bevA, dt % 2, "v", E.milestone("vector", "v"))
            trm_last = E.c["t"].val
        E.wait("tensor", "v", E.c["v"].val)

        for pr in range(NPAIR):
            bev = {}
            lw = 0
            for wi, w_h in enumerate((wq_h, wk_h, wv_h, wp_h)):
                lw = E.dma("gpsimd", "wpr", lambda e, wi=wi, w_h=w_h, pr=pr:
                           e.dma_start(
                    out=wsl[:].rearrange("p (n m) -> p n m", n=ND)[:, :, wi * 128:(wi + 1) * 128],
                    in_=w_h.rearrange("(n p) m -> p n m", p=128)[:, :, pr * 128:(pr + 1) * 128]))
            E.wait("tensor", "wpr", lw)
            for mc in range(MY // 512):
                ps = (pA, pB)[mc % 2]
                bt_wait(E, bev, mc % 2)
                def mmq4(e, mc=mc, ps=ps):
                    for kt in range(ND):
                        e.matmul(ps[:, :], wsl[:, kt * 512:kt * 512 + 128],
                                 XhMT[:, kt * MY + mc * 512:kt * MY + (mc + 1) * 512],
                                 start=(kt == 0), stop=(kt == ND - 1))
                E.on("tensor", mmq4)
                tms = E.milestone("tensor", "t")
                E.wait("vector", "t", tms)
                def evq4(e, mc=mc, ps=ps):
                    e.tensor_copy(qTp4[:, mc * 512:(mc + 1) * 512], ps[:, :])
                E.on("vector", evq4)
                bt_set(bev, mc % 2, "v", E.milestone("vector", "v"))
            vq = E.c["v"].val
            E.on("gpsimd", lambda e, vq=vq: e.wait_ge(E.c["v"].sem, vq))
            sq = E.dma("gpsimd", "kq", lambda e, pr=pr: e.dma_start(
                out=qts[pr], in_=qTp4[:, :]))
            for sc_ in range(S // 512):
                ps = (pA, pB)[sc_ % 2]
                bt_wait(E, bev, sc_ % 2)
                def mmk4(e, sc_=sc_, ps=ps):
                    for kt in range(ND):
                        e.matmul(ps[:, :], wsl[:, kt * 512 + 128:kt * 512 + 256],
                                 XhT[:, kt * S + sc_ * 512:kt * S + (sc_ + 1) * 512],
                                 start=(kt == 0), stop=(kt == ND - 1))
                E.on("tensor", mmk4)
                tms = E.milestone("tensor", "t")
                E.wait("vector", "t", tms)
                def evk4(e, sc_=sc_, ps=ps):
                    e.tensor_copy(kTp4[:, sc_ * 512:(sc_ + 1) * 512], ps[:, :])
                E.on("vector", evk4)
                bt_set(bev, sc_ % 2, "v", E.milestone("vector", "v"))
            vk = E.c["v"].val
            E.on("gpsimd", lambda e, vk=vk: e.wait_ge(E.c["v"].sem, vk))
            sk = E.dma("gpsimd", "kq", lambda e, pr=pr: e.dma_start(
                out=kts[pr], in_=kTp4[:, :]))
            for tt in range(NT):
                ps = (pA, pB)[tt % 2]
                po = (tt % 2) * 128
                bt_wait(E, bev, tt % 2)
                def mmv4(e, tt=tt, ps=ps):
                    for kt in range(ND):
                        e.matmul(ps[:, 0:128],
                                 XhT[:, kt * S + tt * 128:kt * S + (tt + 1) * 128],
                                 wsl[:, kt * 512 + 256:kt * 512 + 384],
                                 start=(kt == 0), stop=(kt == ND - 1))
                E.on("tensor", mmv4)
                tms = E.milestone("tensor", "t")
                E.wait("vector", "t", tms)
                def evv4(e, tt=tt, ps=ps):
                    e.tensor_copy(vpb[:, tt * 128:(tt + 1) * 128], ps[:, 0:128])
                E.on("vector", evv4)
                bt_set(bev, tt % 2, "v", E.milestone("vector", "v"))
            vv = E.c["v"].val
            E.on("gpsimd", lambda e, vv=vv: e.wait_ge(E.c["v"].sem, vv))
            sv = E.dma("gpsimd", "kq", lambda e, pr=pr: e.dma_start(
                out=vts[pr].rearrange("(n p) m -> p n m", p=128), in_=vpb[:].rearrange("p (n m) -> p n m", n=NT)))
            pe_t_hist = {}
            for pc in range(PEW2 // 512):
                slot = (pc % 2) * ND * 512
                pegrp = ("pe0", "pe1")[pc % 2]
                if pc >= 2:
                    E.on("gpsimd", lambda e, v=pe_t_hist[pc - 2]:
                         e.wait_ge(E.c["t"].sem, v))
                lpe = E.dma("gpsimd", pegrp, lambda e, pc=pc, slot=slot: e.dma_start(
                    out=peb[:, slot:slot + ND * 512].rearrange("p (n m) -> p n m", n=ND),
                    in_=pes_h.rearrange("(n p) m -> p n m", p=128)[:, :, pc * 512:(pc + 1) * 512]))
                E.wait("tensor", pegrp, lpe)
                ps = (pA, pB)[pc % 2]
                bt_wait(E, bev, pc % 2)
                def mmp4(e, pc=pc, ps=ps, slot=slot):
                    for kt in range(ND):
                        e.matmul(ps[:, :], wsl[:, kt * 512 + 384:kt * 512 + 512],
                                 peb[:, slot + kt * 512:slot + (kt + 1) * 512],
                                 start=(kt == 0), stop=(kt == ND - 1))
                E.on("tensor", mmp4)
                tms = E.milestone("tensor", "t")
                pe_t_hist[pc] = tms
                E.wait("vector", "t", tms)
                def evp4(e, pc=pc, ps=ps):
                    e.tensor_copy(PThp[:, pc * 512:(pc + 1) * 512], ps[:, :])
                E.on("vector", evp4)
                bt_set(bev, pc % 2, "v", E.milestone("vector", "v"))
            vpth = E.c["v"].val
            E.on("gpsimd", lambda e, vpth=vpth: e.wait_ge(E.c["v"].sem, vpth))
            sp = E.dma("gpsimd", "kq", lambda e, pr=pr: e.dma_start(
                out=pts[pr], in_=PThp[:, :]))
            E.wait("vector", "kq", E.c["kq"].val)
            E.wait("tensor", "kq", E.c["kq"].val)

        emit_barrier(E)
        phA.close()
        phB12 = ExitStack()
        wob = sb("wob", [128, ND * D], BF16, phB12)
        selwb = sb("selwb", [128, ND * 32], BF16, phB12)
        phB1 = ExitStack()
        kTh4 = sb("kTh4", [64, 2 * S], BF16, phB1)
        qTh4 = sb("qTh4", [64, 2 * MY], BF16, phB1)
        aosl4 = sb("aosl4", [64, 256], BF16, phB1)
        PTh4 = sb("PTh4", [64, 2 * PEW2], BF16, phB1)
        vh4 = sb("vh4", [128, 2 * NT * 64], BF16, phB1)
        Rbuf = sb("Rbuf", [128, SLW], BF16, phB1)
        scb = sb("scb", [128, S], F32, phB1)
        Eb = sb("Eb", [128, S], BF16, phB1)
        bdb = sb("bdb", [128, 2 * S], BF16, phB1)
        ETt = sb("ETt", [128, 256], BF16, phB1)
        rs = sb("rs4", [128, 1], F32, phB1)

        head_guard = {}
        bev = {}
        pq_ev = {}
        wao_hist = {}
        rd_hist = {}
        bd_guard = {}
        ldh_val = {}
        st = {"last_wR": None, "scm_prev": 0, "trE_last": 0}

        def emit_head_loads(h):
            pr, sub = h // 2, h % 2
            hb = h % 2
            hgrp = ("hA", "hB")[hb]
            if h >= 2:
                E.wait("sync", "t", head_guard[h - 2])
            E.dma("sync", hgrp, lambda e, pr=pr, sub=sub, hb=hb: e.dma_start(
                out=kTh4[:, hb * S:(hb + 1) * S],
                in_=kts[pr, sub * 64:(sub + 1) * 64, :]))
            E.dma("sync", hgrp, lambda e, pr=pr, sub=sub, hb=hb: e.dma_start(
                out=qTh4[:, hb * MY:(hb + 1) * MY],
                in_=qts[pr, sub * 64:(sub + 1) * 64, :]))
            E.dma("sync", hgrp, lambda e, pr=pr, sub=sub, hb=hb: e.dma_start(
                out=PTh4[:, hb * PEW2:(hb + 1) * PEW2],
                in_=pts[pr, sub * 64:(sub + 1) * 64, :]))
            ldh_val[h] = E.dma("sync", hgrp, lambda e, pr=pr, sub=sub, hb=hb:
                               e.dma_start(
                out=vh4[:, hb * NT * 64:(hb + 1) * NT * 64].rearrange(
                    "p (n m) -> p n m", n=NT),
                in_=vts[pr].rearrange("(n p) m -> p n m", p=128)[:, :, sub * 64:(sub + 1) * 64]))

        def emit_R(h, qt):
            hb = h % 2
            qoff = hb * MY
            poff = hb * PEW2
            wbase = (15 - qt) * 128
            rslot = qt % 2
            rwgrp = ("rw0", "rw1")[rslot]
            bdgrp = ("bd0", "bd1")[rslot]
            hgrp = ("hA", "hB")[hb]
            if qt == 0:
                E.wait("tensor", hgrp, ldh_val[h])
            if st["last_wR"] is not None:
                E.wait("vector", st["last_wR"][0], st["last_wR"][1])
            for rc in range(9):
                n0 = rc * 512
                n1 = min(n0 + 512, SLW)
                ps = (pA, pB)[rc % 2]
                bt_wait(E, bev, rc % 2)
                def mmr(e, qt=qt, qoff=qoff, poff=poff, wbase=wbase,
                        n0=n0, n1=n1, ps=ps):
                    e.matmul(ps[:, 0:n1 - n0],
                             qTh4[:, qoff + qt * 128:qoff + (qt + 1) * 128],
                             PTh4[:, poff + wbase + n0:poff + wbase + n1],
                             start=True, stop=True)
                E.on("tensor", mmr)
                tms = E.milestone("tensor", "t")
                E.wait("vector", "t", tms)
                def evr(e, n0=n0, n1=n1, ps=ps):
                    e.tensor_copy(Rbuf[:, n0:n1], ps[:, 0:n1 - n0])
                E.on("vector", evr)
                bt_set(bev, rc % 2, "v", E.milestone("vector", "v"))
            vr = E.c["v"].val
            E.on("gpsimd", lambda e, vr=vr: e.wait_ge(E.c["v"].sem, vr))
            if rslot in bd_guard:
                E.on("gpsimd", lambda e, v=bd_guard[rslot]:
                     e.wait_ge(E.c["v"].sem, v))
            wR = E.dma("gpsimd", rwgrp, lambda e, rslot=rslot: e.dma_start(
                out=rts[rslot], in_=Rbuf[:, :]))
            st["last_wR"] = (rwgrp, wR)
            E.wait("sync", rwgrp, wR)
            rd_hist[rslot] = E.dma("sync", bdgrp, lambda e, rslot=rslot:
                                   e.dma_start(
                out=bdb[:, rslot * S:(rslot + 1) * S],
                in_=bass.AP(tensor=rts_t, offset=rslot * 128 * SLW + 127,
                            ap=[[SLW - 1, 128], [1, S]])))

        def emit_AV(h, qt):
            hb = h % 2
            qoff = hb * MY
            koff = hb * S
            voff = hb * NT * 64
            rslot = qt % 2
            bdgrp = ("bd0", "bd1")[rslot]
            E.wait("vector", "s", st["scm_prev"])
            for ch in range(8):
                ps = (pA, pB)[ch % 2]
                bt_wait(E, bev, ch % 2)
                def mmac(e, qt=qt, qoff=qoff, koff=koff, ch=ch, ps=ps):
                    e.matmul(ps[:, :],
                             qTh4[:, qoff + qt * 128:qoff + (qt + 1) * 128],
                             kTh4[:, koff + ch * 512:koff + (ch + 1) * 512],
                             start=True, stop=True)
                E.on("tensor", mmac)
                tms = E.milestone("tensor", "t")
                E.wait("vector", "t", tms)
                if ch == 0:
                    E.wait("vector", bdgrp, rd_hist[rslot])
                def evac_(e, ch=ch, ps=ps, rslot=rslot):
                    e.scalar_tensor_tensor(scb[:, ch * 512:(ch + 1) * 512],
                                           ps[:, :], 1.0,
                                           bdb[:, rslot * S + ch * 512:rslot * S + (ch + 1) * 512],
                                           ALU.mult, ALU.add)
                E.on("vector", evac_)
                bt_set(bev, ch % 2, "v", E.milestone("vector", "v"))
            bd_guard[rslot] = E.c["v"].val
            def vmax4(e):
                e.reduce_max(m1[:], scb[:, 0:512], axis=AX.X)
                e.tensor_scalar_mul(m1[:], m1[:], -0.125)
            E.on("vector", vmax4)
            vmx = E.milestone("vector", "v")
            E.wait("scalar", "v", vmx)
            E.wait("scalar", "t", st["trE_last"])
            def sexp4(e):
                e.activation(scb[:, :], scb[:, :], AF.Exp, bias=m1[:],
                             scale=0.125, accum_out=rs[:])
            E.on("scalar", sexp4)
            sms = E.milestone("scalar", "s")
            E.wait("vector", "s", sms)
            E.on("vector", lambda e: e.reciprocal(rs[:], rs[:]))
            vrc = E.milestone("vector", "v")
            E.wait("scalar", "v", vrc)
            def scast4(e):
                e.activation(Eb[:, :], scb[:, :], AF.Copy, scale=rs[:])
            E.on("scalar", scast4)
            scm = E.milestone("scalar", "s")
            st["scm_prev"] = scm
            E.wait("tensor", "s", scm)
            pol = (pO, pO2)[qt % 2]
            for jt in range(NT):
                pq = (pQ1, pQ2, pQ3, pQ4)[jt % 4]
                if (jt % 4) in pq_ev:
                    cn, v = pq_ev[jt % 4]
                    E.wait("tensor", cn, v)
                if jt >= 1:
                    cn, v = pq_ev[(jt - 1) % 4]
                    E.wait("tensor", cn, v)
                if jt == 0:
                    bt_wait(E, bev, "pO%d" % (qt % 2))
                def pair(e, jt=jt, pq=pq, voff=voff, pol=pol):
                    e.transpose(pq[:], Eb[:, jt * 128:(jt + 1) * 128], ident[:])
                    if jt >= 1:
                        eslot = ((jt - 1) % 2) * 128
                        e.matmul(pol[:, :],
                                 vh4[:, voff + (jt - 1) * 64:voff + jt * 64],
                                 ETt[:, eslot:eslot + 128],
                                 start=(jt == 1), stop=False)
                E.on("tensor", pair)
                tms = E.milestone("tensor", "t")
                ev_eng = ("vector", "scalar")[jt % 2]
                E.wait(ev_eng, "t", tms)
                eslot = (jt % 2) * 128
                if ev_eng == "vector":
                    def evE(e, pq=pq, eslot=eslot):
                        e.tensor_copy(ETt[:, eslot:eslot + 128], pq[:])
                    E.on("vector", evE)
                    pq_ev[jt % 4] = ("v", E.milestone("vector", "v"))
                else:
                    def evE(e, pq=pq, eslot=eslot):
                        e.activation(ETt[:, eslot:eslot + 128], pq[:], AF.Copy)
                    E.on("scalar", evE)
                    pq_ev[jt % 4] = ("s", E.milestone("scalar", "s"))
            cn, v = pq_ev[(NT - 1) % 4]
            E.wait("tensor", cn, v)
            def mmov_last(e, voff=voff, pol=pol):
                eslot = ((NT - 1) % 2) * 128
                e.matmul(pol[:, :],
                         vh4[:, voff + (NT - 1) * 64:voff + NT * 64],
                         ETt[:, eslot:eslot + 128],
                         start=False, stop=True)
            E.on("tensor", mmov_last)
            tcomp = E.milestone("tensor", "t")
            st["trE_last"] = tcomp
            E.wait("vector", "t", tcomp)
            aslot = (qt % 2) * 128
            aogrp = ("ao40", "ao41")[qt % 2]
            if (qt % 2) in wao_hist:
                E.wait("vector", aogrp, wao_hist[qt % 2])
            def evo4(e, pol=pol, aslot=aslot):
                e.tensor_copy(aosl4[:, aslot:aslot + 128], pol[:, :])
            E.on("vector", evo4)
            vms = E.milestone("vector", "v")
            bt_set(bev, "pO%d" % (qt % 2), "v", vms)
            E.on("gpsimd", lambda e, vms=vms: e.wait_ge(E.c["v"].sem, vms))
            wao = E.dma("gpsimd", aogrp, lambda e, h=h, qt=qt, aslot=aslot:
                        e.dma_start(
                out=bass.AP(tensor=aoTs4.tensor,
                            offset=(h % 2) * 64 * (ND * MY) + (h // 2) * MY + qt * 128,
                            ap=[[ND * MY, 64], [1, 128]]),
                in_=aosl4[:, aslot:aslot + 128]))
            wao_hist[qt % 2] = wao

        pairs4 = [(h, qt) for h in range(NH) for qt in range(NTM4)]
        emit_head_loads(0)
        emit_head_loads(1)
        emit_R(*pairs4[0])
        for i, (h, qt) in enumerate(pairs4):
            if i + 1 < len(pairs4):
                emit_R(*pairs4[i + 1])
            emit_AV(h, qt)
            if qt == NTM4 - 1:
                head_guard[h] = E.c["t"].val
                if h + 2 < NH:
                    emit_head_loads(h + 2)

        emit_barrier(E)
        phB1.close()
        aog4 = sb("aog4", [128, ND * 128], BF16, phB12)
        x2stg = sb("x2stg", [128, ND * 128], BF16, phB12)
        lwo = E.dma("gpsimd", "wo4g", lambda e: e.dma_start(
            out=wob[:].rearrange("p (n m) -> p n m", n=ND),
            in_=wo_h.rearrange("(n p) m -> p n m", p=128)))
        lsw = E.dma("gpsimd", "wo4g", lambda e: e.dma_start(
            out=selwb[:].rearrange("p (n m) -> p n m", n=ND),
            in_=selw_h.rearrange("(n p) m -> p n m", p=128)))
        E.wait("tensor", "wo4g", lsw)
        res_ms = {}
        mmop_ms = {}
        for tt in range(NTM4):
            bo = (tt % 2) * D
            grpn = ("xA", "xB")[tt % 2]
            if tt >= 2:
                E.wait("sync", "v", res_ms[tt - 2])
            else:
                E.wait("sync", "v", E.c["v"].val)
            ldx = E.dma("sync", grpn, lambda e, tt=tt, bo=bo: e.dma_start(
                out=xin[:, bo:bo + D], in_=x3ms[tt * 128:(tt + 1) * 128, :]))
            if (tt - 1) in mmop_ms:
                E.on("gpsimd", lambda e, v=mmop_ms[tt - 1]: e.wait_ge(E.c["t"].sem, v))
            lao = E.dma("gpsimd", "aol4", lambda e, tt=tt: e.dma_start(
                out=aog4[:].rearrange("p (n m) -> p n m", n=ND),
                in_=aoTs4.rearrange("p (n m) -> p n m", n=ND)[:, :, tt * 128:(tt + 1) * 128]))
            E.wait("tensor", "aol4", lao)
            for ch in range(2):
                ps = (pA, pB)[ch]
                def mmwo(e, tt=tt, ch=ch, ps=ps):
                    for kt in range(ND):
                        e.matmul(ps[:, :],
                                 aog4[:, kt * 128:(kt + 1) * 128],
                                 wob[:, kt * D + ch * 512:kt * D + (ch + 1) * 512],
                                 start=(kt == 0), stop=(kt == ND - 1))
                E.on("tensor", mmwo)
            tms = E.milestone("tensor", "t")
            mmop_ms[tt] = tms
            E.wait("vector", "t", tms)
            E.wait("vector", grpn, ldx)
            if "us" in E.c:
                E.wait("vector", "us", E.c["us"].val)
            def vsrc2(e, bo=bo):
                e.scalar_tensor_tensor(acc[:, 0:512], xin[:, bo:bo + 512], 1.0,
                                       pA[:, :], ALU.mult, ALU.add)
                e.scalar_tensor_tensor(acc[:, 512:1024], xin[:, bo + 512:bo + D],
                                       1.0, pB[:, :], ALU.mult, ALU.add)
            E.on("vector", vsrc2)
            vsm = E.milestone("vector", "v")
            E.wait("tensor", "v", vsm)
            E.wait("sync", "v", vsm)
            E.dma("sync", "us", lambda e, tt=tt: e.dma_start(
                out=us4[tt * 128:(tt + 1) * 128, :], in_=acc[:, 0:D]))
            E.wait("vector", "us", E.c["us"].val)
            lnm = emit_ln_tile(E, acc, m1, m2, sqd, epscol, gcast2, bcast2,
                               out_bf=xinb[:, 0:D])
            E.wait("tensor", "v", lnm)
            if "x2w" in E.c:
                E.wait("vector", "x2w", E.c["x2w"].val)
            bev2 = {}
            for dt in range(ND):
                pt = (pT1, pT2)[dt % 2]
                bt_wait(E, bev2, dt % 2)
                def trx2(e, dt=dt, pt=pt):
                    e.transpose(pt[:], xinb[:, dt * 128:(dt + 1) * 128], ident[:])
                E.on("tensor", trx2)
                tms = E.milestone("tensor", "t")
                E.wait("vector", "t", tms)
                def evx2(e, dt=dt, pt=pt):
                    e.tensor_copy(x2stg[:, dt * 128:(dt + 1) * 128], pt[:])
                E.on("vector", evx2)
                bt_set(bev2, dt % 2, "v", E.milestone("vector", "v"))
            x2v = E.c["v"].val
            E.wait("tensor", "v", x2v)
            E.on("gpsimd", lambda e, x2v=x2v: e.wait_ge(E.c["v"].sem, x2v))
            E.dma("gpsimd", "x2w", lambda e, tt=tt: e.dma_start(
                out=X2hTs.rearrange("p (n m) -> p n m", n=ND)[:, :, tt * 128:(tt + 1) * 128],
                in_=x2stg[:].rearrange("p (n m) -> p n m", n=ND)))
            def mmsel(e, tt=tt):
                for kt in range(ND):
                    e.matmul(pA[:, 0:32],
                             x2stg[:, kt * 128:(kt + 1) * 128],
                             selwb[:, kt * 32:(kt + 1) * 32],
                             start=(kt == 0), stop=(kt == ND - 1))
            E.on("tensor", mmsel)
            tms = E.milestone("tensor", "t")
            E.wait("scalar", "t", tms)
            sslot = (tt % 2) * 32
            if tt >= 2:
                E.wait("scalar", "v", res_ms[tt - 2])
            E.on("scalar", lambda e, sslot=sslot: e.activation(
                selb[:, sslot:sslot + 32], pA[:, 0:32], AF.Sigmoid))
            ssm = E.milestone("scalar", "s")
            E.wait("vector", "s", ssm)
            E.wait("tensor", "s", ssm)
            def vtop4(e, tt=tt, sslot=sslot):
                c = selc[:, 0:32]
                mx = selc[:, 32:33]
                e.tensor_copy(c, selb[:, sslot:sslot + 32])
                for it in range(3):
                    e.reduce_max(mx, c, axis=AX.X)
                    e.tensor_scalar(sqd[:, 0:32], c, mx, None, ALU.is_ge)
                    e.scalar_tensor_tensor(c, sqd[:, 0:32], -1e30, c,
                                           ALU.mult, ALU.add)
                e.reduce_max(mx, c, axis=AX.X)
                e.tensor_scalar(sqd[:, 0:32], selb[:, sslot:sslot + 32], mx,
                                None, ALU.is_ge)
                e.tensor_mul(gates[:, tt * 32:(tt + 1) * 32],
                             selb[:, sslot:sslot + 32], sqd[:, 0:32])
            E.on("vector", vtop4)
            res_ms[tt] = E.milestone("vector", "v")

        emit_barrier(E)
        phB12.close()
        kb = sb("kb4", [128, 2 * ND * 128], BF16)
        vb = sb("vb4", [128, 2 * D], BF16)
        hTe = sb("hTe", [128, 2 * 1024], BF16)
        yacc4 = sb("yacc4", [128, 8 * D], F32)
        X2hg = sb("X2hg", [128, ND * 1024], BF16)
        fin_ms = {}
        bevm = {}
        wk_t = {}
        for grp in range(MY // 1024):
            E.on("vector", lambda e: e.memset(yacc4[:, :], 0.0))
            ya0 = E.milestone("vector", "v")
            E.on("gpsimd", lambda e, v=E.c["t"].val: e.wait_ge(E.c["t"].sem, v))
            lx2 = E.dma("gpsimd", "x2l", lambda e, grp=grp: e.dma_start(
                out=X2hg[:].rearrange("p (n m) -> p n m", n=ND),
                in_=X2hTs.rearrange("p (n m) -> p n m", n=ND)[
                    :, :, grp * 1024:(grp + 1) * 1024]))
            E.wait("tensor", "x2l", lx2)
            for ei in range(32):
                es = ei % 2
                kgrp = ("wmk0", "wmk1")[es]
                vgrp = ("wmv0", "wmv1")[es]
                if es in wk_t:
                    E.on("gpsimd", lambda e, v=wk_t[es]: e.wait_ge(E.c["t"].sem, v))
                lk = E.dma("gpsimd", kgrp, lambda e, ei=ei, es=es: e.dma_start(
                    out=kb[:, es * ND * 128:(es + 1) * ND * 128].rearrange(
                        "p (n m) -> p n m", n=ND),
                    in_=keys_h[ei].rearrange("(n p) m -> p n m", p=128)))
                lv = E.dma("gpsimd", vgrp, lambda e, ei=ei, es=es: e.dma_start(
                    out=vb[:, es * D:(es + 1) * D], in_=vals_h[ei]))
                E.wait("tensor", kgrp, lk)
                E.wait("tensor", vgrp, lv)
                hs = es * 1024
                for mc in range(2):
                    ps = (pA, pB)[mc]
                    bt_wait(E, bevm, mc)
                    def mmh4(e, mc=mc, ps=ps, es=es):
                        for kt in range(ND):
                            e.matmul(ps[:, :],
                                     kb[:, es * ND * 128 + kt * 128:es * ND * 128 + (kt + 1) * 128],
                                     X2hg[:, kt * 1024 + mc * 512:kt * 1024 + (mc + 1) * 512],
                                     start=(kt == 0), stop=(kt == ND - 1))
                    E.on("tensor", mmh4)
                    tms = E.milestone("tensor", "t")
                    E.wait("scalar", "t", tms)
                    def evh4(e, mc=mc, ps=ps, hs=hs):
                        e.activation(hTe[:, hs + mc * 512:hs + (mc + 1) * 512],
                                     ps[:, :], AF.Relu)
                    E.on("scalar", evh4)
                    bt_set(bevm, mc, "s", E.milestone("scalar", "s"))
                E.wait("tensor", "s", E.c["s"].val)
                if ei == 0:
                    E.wait("vector", "v", ya0)
                for tt8 in range(8):
                    for ch in range(2):
                        ps = (pA, pB)[ch]
                        bt_wait(E, bevm, ch)
                        def mmy4(e, tt8=tt8, ch=ch, ps=ps, hs=hs, es=es):
                            e.matmul(ps[:, :],
                                     hTe[:, hs + tt8 * 128:hs + (tt8 + 1) * 128],
                                     vb[:, es * D + ch * 512:es * D + (ch + 1) * 512],
                                     start=True, stop=True)
                        E.on("tensor", mmy4)
                        tms = E.milestone("tensor", "t")
                        E.wait("vector", "t", tms)
                        def evy4(e, grp=grp, ei=ei, tt8=tt8, ch=ch, ps=ps):
                            col = gates[:, (grp * 8 + tt8) * 32 + ei:
                                        (grp * 8 + tt8) * 32 + ei + 1]
                            e.scalar_tensor_tensor(
                                yacc4[:, tt8 * D + ch * 512:tt8 * D + (ch + 1) * 512],
                                ps[:, :], col,
                                yacc4[:, tt8 * D + ch * 512:tt8 * D + (ch + 1) * 512],
                                ALU.mult, ALU.add)
                        E.on("vector", evy4)
                        bt_set(bevm, ch, "v", E.milestone("vector", "v"))
                wk_t[es] = E.c["t"].val
            for tt8 in range(8):
                tt = grp * 8 + tt8
                bo = (tt % 2) * D
                grpn = ("xA", "xB")[tt % 2]
                E.wait("sync", "us", E.c["us"].val)
                if tt >= 2 and (tt - 2) in fin_ms:
                    E.wait("sync", "v", fin_ms[tt - 2])
                else:
                    E.wait("sync", "v", E.c["v"].val)
                ldu = E.dma("sync", grpn, lambda e, tt=tt, bo=bo: e.dma_start(
                    out=xin[:, bo:bo + D], in_=us4[tt * 128:(tt + 1) * 128, :]))
                E.wait("vector", grpn, ldu)
                if "out" in E.c:
                    E.wait("vector", "out", E.c["out"].val)
                def vfin1(e, tt8=tt8, bo=bo):
                    e.tensor_add(acc[:, 0:D], xin[:, bo:bo + D],
                                 yacc4[:, tt8 * D:(tt8 + 1) * D])
                E.on("vector", vfin1)
                v1m = E.milestone("vector", "v")
                E.wait("sync", "v", v1m)
                ldx3 = E.dma("sync", grpn, lambda e, tt=tt, bo=bo: e.dma_start(
                    out=xin[:, bo:bo + D], in_=x3ms[tt * 128:(tt + 1) * 128, :]))
                E.wait("vector", grpn, ldx3)
                E.on("vector", lambda e, bo=bo: e.tensor_add(acc[:, 0:D], acc[:, 0:D],
                                                             xin[:, bo:bo + D]))
                E.milestone("vector", "v")
                fin = emit_ln_tile(E, acc, m1, m2, sqd, epscol)
                fin_ms[tt] = fin
                E.wait("sync", "v", fin)
                E.dma("sync", "out", lambda e, tt=tt: e.dma_start(
                    out=out_h[tt * 128:(tt + 1) * 128, :], in_=acc[:, 0:D]))
        E.wait("sync", "out", E.c["out"].val)
        if not F:
            p.play(ctx.enter_context(nc.Block()))
    return nc

_BUILD_CACHE = {}


def _get(name, fn):
    if name not in _BUILD_CACHE:
        _BUILD_CACHE[name] = fn()
    return _BUILD_CACHE[name]


def kernel_multi(**inp):
    c1 = _consts_stage1()
    c3 = _consts_stage3()
    pes = _consts_stage4()
    f32 = lambda a: np.ascontiguousarray(np.asarray(a, dtype=np.float32))

    nc1 = _get("s1", build_stage1)
    maps = []
    for c in range(8):
        b = c // 2
        m = {"x": f32(inp["L0_attn_x"][b]), "Wq": f32(inp["L0_attn_Wq"]),
             "Wk": f32(inp["L0_attn_Wk"]), "Wv": f32(inp["L0_attn_Wv"])}
        m.update(c1)
        maps.append(m)
    r1 = run_bass_kernel_spmd(nc1, maps, core_ids=list(range(8)))
    x1 = np.stack([r1.results[2 * b]["out"] for b in range(4)])

    bi = np.asarray(inp["L0_ffn_in_proj_b"], np.float32)
    bicols = np.zeros((128, 16), np.float32)
    for pr in range(8):
        bicols[:, pr] = bi[pr * 128:(pr + 1) * 128]
        bicols[:, 8 + pr] = bi[1024 + pr * 128:1024 + (pr + 1) * 128]
    b1 = np.asarray(inp["L0_ffn_b1"], np.float32)
    b1cols = np.zeros((128, 128), np.float32)
    for e in range(8):
        for ft in range(16):
            b1cols[:, e * 16 + ft] = b1[e, ft * 128:(ft + 1) * 128]
    nc2 = _get("s2", build_stage2)
    maps = []
    for c in range(8):
        b, half = c // 2, c % 2
        m = {
            "x1": f32(x1[b]),
            "wi": f32(inp["L0_ffn_in_proj_w"]),
            "bicols": bicols,
            "bivrow": f32(bi[2048:].reshape(1, 1024)),
            "wo": f32(inp["L0_ffn_out_proj_w"]),
            "borow": f32(np.asarray(inp["L0_ffn_out_proj_b"]).reshape(1, 1024)),
            "ln1g": f32(np.asarray(inp["L0_ffn_ln1_g"]).reshape(1, 1024)),
            "ln1b": f32(np.asarray(inp["L0_ffn_ln1_b"]).reshape(1, 1024)),
            "ln2g": f32(np.asarray(inp["L0_ffn_ln2_g"]).reshape(1, 1024)),
            "ln2b": f32(np.asarray(inp["L0_ffn_ln2_b"]).reshape(1, 1024)),
            "wg": f32(inp["L0_ffn_w_group"]),
            "bg": f32(np.asarray(inp["L0_ffn_b_group"]).reshape(1, 4)),
            "wef": f32(np.asarray(inp["L0_ffn_w_expert"]).transpose(1, 0, 2).reshape(1024, 8)),
            "bef": f32(np.asarray(inp["L0_ffn_b_expert"]).reshape(1, 8)),
            "w1": f32(inp["L0_ffn_w1"]),
            "b1cols": b1cols,
            "w2": f32(inp["L0_ffn_w2"]),
            "b2rows": f32(np.asarray(inp["L0_ffn_b2"]).reshape(1, 8192)),
            "cIdent": c1["cIdent"], "cOnesRow": c1["cOnesRow"],
        }
        maps.append(m)
    r2 = run_bass_kernel_spmd(nc2, maps, core_ids=list(range(8)))
    x2 = np.stack([r2.results[2 * b]["out"] for b in range(4)])

    nc3 = _get("s3", build_stage3)
    maps = []
    for c in range(8):
        b = c // 2
        m = {
            "x2": f32(x2[b]),
            "wqkv": f32(inp["L1_attn_w_qkv"]),
            "wgated": f32(inp["L1_attn_w_gated"]),
            "wproj": f32(inp["L1_attn_w_proj"]),
            "gnw": f32(np.asarray(inp["L1_attn_gn_weight"]).reshape(1, 1024)),
            "gnb": f32(np.asarray(inp["L1_attn_gn_bias"]).reshape(1, 1024)),
            "cIdent": c1["cIdent"], "cOnesRow": c1["cOnesRow"],
            "cLTT": c1["cLTT"],
        }
        m.update(c3)
        maps.append(m)
    r3 = run_bass_kernel_spmd(nc3, maps, core_ids=list(range(8)))
    x3 = np.stack([r3.results[2 * b]["out"] for b in range(4)])

    nc4 = _get("s4", build_stage4)
    maps = []
    for c in range(8):
        b, half = c // 2, c % 2
        m = {
            "x3": f32(x3[b]),
            "wq4": f32(inp["L1_ffn_wq"]), "wk4": f32(inp["L1_ffn_wk"]),
            "wv4": f32(inp["L1_ffn_wv"]), "wp4": f32(inp["L1_ffn_wp"]),
            "wo4": f32(inp["L1_ffn_wo"]),
            "ln1g4": f32(np.asarray(inp["L1_ffn_ln1_g"]).reshape(1, 1024)),
            "ln1b4": f32(np.asarray(inp["L1_ffn_ln1_b"]).reshape(1, 1024)),
            "ln2g4": f32(np.asarray(inp["L1_ffn_ln2_g"]).reshape(1, 1024)),
            "ln2b4": f32(np.asarray(inp["L1_ffn_ln2_b"]).reshape(1, 1024)),
            "selw": f32(inp["L1_ffn_sel_w"]),
            "keys": f32(inp["L1_ffn_keys"]),
            "vals": f32(inp["L1_ffn_values"]),
            "peslice": pes[half],
            "cIdent": c1["cIdent"], "cOnesRow": c1["cOnesRow"],
        }
        maps.append(m)
    r4 = run_bass_kernel_spmd(nc4, maps, core_ids=list(range(8)))
    out = np.stack([r4.results[2 * b]["out"] for b in range(4)]).astype(np.float32)
    return out



def build_fused():
    nc = bass.Bass("TRN2", target_bir_lowering=False)
    dt_in = [
        ("x", [S, D]), ("Wq", [D, D]), ("Wk", [D, D]), ("Wv", [D, D]),
        ("cF", [D, FPAD]), ("cG", [FPAD, D]), ("cLTT", [128, 128]),
        ("cIdent", [128, 128]), ("cOnesCol", [128, 1]), ("cOnesRow", [1, 128]),
        ("wi", [3 * D, D]), ("bicols", [128, 16]), ("bivrow", [1, D]),
        ("wo", [D, D]), ("borow", [1, D]),
        ("ln1g", [1, D]), ("ln1b", [1, D]), ("ln2g", [1, D]), ("ln2b", [1, D]),
        ("wg", [D, 4]), ("bg", [1, 4]), ("wef", [D, 8]), ("bef", [1, 8]),
        ("w1", [NE0, D, FF]), ("b1cols", [128, NE0 * 16]),
        ("w2", [NE0, FF, D]), ("b2rows", [1, NE0 * D]),
        ("wqkv", [3 * D, D]), ("wgated", [D, D]), ("wproj", [D, D]),
        ("gnw", [1, D]), ("gnb", [1, D]),
        ("cCS", [S, 64]), ("cSS", [S, 64]), ("cCI", [S, 64]), ("cSI", [S, 64]),
        ("cVbar", [128, 16]), ("cEnd", [128, 16]), ("cE16", [16, D]),
        ("wq4", [D, D]), ("wk4", [D, D]), ("wv4", [D, D]), ("wp4", [D, D]),
        ("wo4", [D, D]),
        ("ln1g4", [1, D]), ("ln1b4", [1, D]), ("ln2g4", [1, D]), ("ln2b4", [1, D]),
        ("selw", [D, 32]), ("keys", [32, D, 128]), ("vals", [32, 128, D]),
        ("peslice", [D, PEW2]), ("sel0", [128, 128]), ("sel1", [128, 128]),
    ]
    F = {"nc": nc}
    for nm, shp in dt_in:
        F[nm] = nc.dram_tensor(nm, shp, F32, kind="ExternalInput").ap()
    out_h = nc.dram_tensor("out", [MY, D], F32, kind="ExternalOutput").ap()
    x1s = nc.dram_tensor("x1s", [S, D], F32, kind="Internal").ap()
    x2s = nc.dram_tensor("x2s", [S, D], F32, kind="Internal").ap()
    x3s = nc.dram_tensor("x3s", [S, D], F32, kind="Internal").ap()

    with ExitStack() as mctx:
        s_t = mctx.enter_context(nc.semaphore("g_t"))
        s_v = mctx.enter_context(nc.semaphore("g_v"))
        s_s = mctx.enter_context(nc.semaphore("g_s"))
        s_g = mctx.enter_context(nc.semaphore("g_g"))
        p = Prog()
        E = Emit(p, {"t": s_t, "v": s_v, "s": s_s, "g": s_g}, nc=nc, ctx=mctx)
        F["E"] = E
        F["p"] = p

        F1 = dict(F); F1["out"] = x1s
        build_stage1(F=F1)
        emit_barrier(E)
        F2 = dict(F); F2["x1"] = x1s; F2["out"] = x2s
        build_stage2(F=F2)
        emit_barrier(E)
        F3 = dict(F); F3["x2"] = x2s; F3["out"] = x3s
        build_stage3(F=F3)
        emit_barrier(E)
        F4 = dict(F); F4["x3"] = x3s; F4["out"] = out_h
        build_stage4(F=F4)
        p.play(mctx.enter_context(nc.Block()))
    return nc


def _fused_in_maps(inp):
    c1 = _consts_stage1()
    c3 = _consts_stage3()
    pes = _consts_stage4()
    f32 = lambda a: np.ascontiguousarray(np.asarray(a, dtype=np.float32))
    bi = np.asarray(inp["L0_ffn_in_proj_b"], np.float32)
    bicols = np.zeros((128, 16), np.float32)
    for pr in range(8):
        bicols[:, pr] = bi[pr * 128:(pr + 1) * 128]
        bicols[:, 8 + pr] = bi[1024 + pr * 128:1024 + (pr + 1) * 128]
    b1 = np.asarray(inp["L0_ffn_b1"], np.float32)
    b1cols = np.zeros((128, 128), np.float32)
    for e in range(8):
        for ft in range(16):
            b1cols[:, e * 16 + ft] = b1[e, ft * 128:(ft + 1) * 128]
    base = {
        "Wq": f32(inp["L0_attn_Wq"]), "Wk": f32(inp["L0_attn_Wk"]),
        "Wv": f32(inp["L0_attn_Wv"]),
        "wi": f32(inp["L0_ffn_in_proj_w"]), "bicols": bicols,
        "bivrow": f32(bi[2048:].reshape(1, 1024)),
        "wo": f32(inp["L0_ffn_out_proj_w"]),
        "borow": f32(np.asarray(inp["L0_ffn_out_proj_b"]).reshape(1, 1024)),
        "ln1g": f32(np.asarray(inp["L0_ffn_ln1_g"]).reshape(1, 1024)),
        "ln1b": f32(np.asarray(inp["L0_ffn_ln1_b"]).reshape(1, 1024)),
        "ln2g": f32(np.asarray(inp["L0_ffn_ln2_g"]).reshape(1, 1024)),
        "ln2b": f32(np.asarray(inp["L0_ffn_ln2_b"]).reshape(1, 1024)),
        "wg": f32(inp["L0_ffn_w_group"]),
        "bg": f32(np.asarray(inp["L0_ffn_b_group"]).reshape(1, 4)),
        "wef": f32(np.asarray(inp["L0_ffn_w_expert"]).transpose(1, 0, 2).reshape(1024, 8)),
        "bef": f32(np.asarray(inp["L0_ffn_b_expert"]).reshape(1, 8)),
        "w1": f32(inp["L0_ffn_w1"]), "b1cols": b1cols,
        "w2": f32(inp["L0_ffn_w2"]),
        "b2rows": f32(np.asarray(inp["L0_ffn_b2"]).reshape(1, 8192)),
        "wqkv": f32(inp["L1_attn_w_qkv"]),
        "wgated": f32(inp["L1_attn_w_gated"]),
        "wproj": f32(inp["L1_attn_w_proj"]),
        "gnw": f32(np.asarray(inp["L1_attn_gn_weight"]).reshape(1, 1024)),
        "gnb": f32(np.asarray(inp["L1_attn_gn_bias"]).reshape(1, 1024)),
        "wq4": f32(inp["L1_ffn_wq"]), "wk4": f32(inp["L1_ffn_wk"]),
        "wv4": f32(inp["L1_ffn_wv"]), "wp4": f32(inp["L1_ffn_wp"]),
        "wo4": f32(inp["L1_ffn_wo"]),
        "ln1g4": f32(np.asarray(inp["L1_ffn_ln1_g"]).reshape(1, 1024)),
        "ln1b4": f32(np.asarray(inp["L1_ffn_ln1_b"]).reshape(1, 1024)),
        "ln2g4": f32(np.asarray(inp["L1_ffn_ln2_g"]).reshape(1, 1024)),
        "ln2b4": f32(np.asarray(inp["L1_ffn_ln2_b"]).reshape(1, 1024)),
        "selw": f32(inp["L1_ffn_sel_w"]),
        "keys": f32(inp["L1_ffn_keys"]),
        "vals": f32(inp["L1_ffn_values"]),
    }
    base.update(c1)
    base.update(c3)
    maps = []
    for c in range(8):
        b, half = c // 2, c % 2
        m = dict(base)
        m["x"] = np.ascontiguousarray(np.asarray(inp["L0_attn_x"][b], np.float32))
        m["peslice"] = pes[half]
        m["sel0"], m["sel1"] = _consts_sel(half)
        maps.append(m)
    return maps


def kernel(**inp):
    nc = _get("fused", build_fused)
    maps = _fused_in_maps(inp)
    res = run_bass_kernel_spmd(nc, maps, core_ids=list(range(8)))
    out = np.stack([np.concatenate([res.results[2 * b]["out"],
                                    res.results[2 * b + 1]["out"]], axis=0)
                    for b in range(4)]).astype(np.float32)
    return out


# version 07
# speedup vs baseline: 2.1297x; 1.0509x over previous
import numpy as np
from contextlib import ExitStack
import concourse.bass as bass
import concourse.mybir as mybir
from concourse.bass_utils import run_bass_kernel_spmd

F32 = mybir.dt.float32
BF16 = mybir.dt.bfloat16
AF = mybir.ActivationFunctionType
ALU = mybir.AluOpType
AX = mybir.AxisListType

D = 1024
S = 4096
B = 4
MY = 2048
NT = S // 128
ND = D // 128
FH = D // 2 + 1
FPAD = 1152
NFK = FPAD // 128
LN_EPS = 1e-5



def _consts_stage1():
    e = np.arange(D, dtype=np.float64)[:, None]
    f = np.arange(FH, dtype=np.float64)[None, :]
    ang = 2.0 * np.pi * e * f / D
    Fm = np.zeros((D, FPAD), dtype=np.float64)
    Fm[:, :FH] = np.cos(ang)
    Fm[:, FH:2 * FH] = -np.sin(ang)
    c = np.full((FH,), 2.0); c[0] = 1.0; c[FH - 1] = 1.0
    f2 = np.arange(FH, dtype=np.float64)[:, None]
    d2 = np.arange(D, dtype=np.float64)[None, :]
    ang2 = 2.0 * np.pi * f2 * d2 / D
    Gm = np.zeros((FPAD, D), dtype=np.float64)
    Gm[:FH] = c[:, None] * np.cos(ang2) / D
    Gm[FH:2 * FH] = -c[:, None] * np.sin(ang2) / D
    LTT = np.triu(np.ones((128, 128), dtype=np.float64))
    return {
        "cF": Fm.astype(np.float32),
        "cG": Gm.astype(np.float32),
        "cLTT": LTT.astype(np.float32),
        "cIdent": np.eye(128, dtype=np.float32),
        "cOnesRow": np.ones((1, 128), dtype=np.float32),
        "cOnesCol": np.ones((128, 1), dtype=np.float32),
    }



ENGINES = ("sync", "gpsimd", "vector", "tensor", "scalar")

_NO_DRAIN = {"wait_ge", "drain", "dma_start", "indirect_dma_start", "sem_inc",
             "memzero"}


class AutoDrain:

    def __init__(self, e):
        self._e = e

    @property
    def raw(self):
        return self._e

    def __getattr__(self, name):
        attr = getattr(self._e, name)
        if name.startswith("_") or name in _NO_DRAIN or not callable(attr):
            return attr
        def wrapped(*a, **k):
            ins = attr(*a, **k)
            self._e.drain()
            return ins
        return wrapped


def bt_wait(E, bev, key, eng="tensor"):
    if key in bev:
        cn, v = bev[key]
        E.wait(eng, cn, v)


def bt_set(bev, key, cn, v):
    bev[key] = (cn, v)


class Prog:
    def __init__(self):
        self.ops = {k: [] for k in ENGINES}

    def add(self, eng, fn):
        self.ops[eng].append(fn)

    def play(self, blk):
        @blk.sync
        def _(e):
            for f in self.ops["sync"]:
                f(e)

        @blk.gpsimd
        def _(e):
            w = AutoDrain(e)
            for f in self.ops["gpsimd"]:
                f(w)

        @blk.vector
        def _(e):
            w = AutoDrain(e)
            for f in self.ops["vector"]:
                f(w)

        @blk.tensor
        def _(e):
            for f in self.ops["tensor"]:
                f(e)

        @blk.scalar
        def _(e):
            w = AutoDrain(e)
            for f in self.ops["scalar"]:
                f(w)


class Cnt:
    def __init__(self, sem):
        self.sem = sem
        self.val = 0


class Emit:

    def __init__(self, prog, sems, nc=None, ctx=None):
        self.p = prog
        self.c = {k: Cnt(v) for k, v in sems.items()}
        self.nc = nc
        self.ctx = ctx

    def dma_group(self, name):
        if name not in self.c:
            sem = self.ctx.enter_context(self.nc.semaphore("sem_" + name))
            self.c[name] = Cnt(sem)
        return name

    def on(self, eng, fn):
        self.p.add(eng, fn)

    def wait(self, eng, cname, val=None):
        cnt = self.c[cname]
        v = cnt.val if val is None else val
        if v > 0:
            self.p.add(eng, lambda e, s=cnt.sem, v=v: e.wait_ge(s, v))
        return v

    def milestone(self, eng, cname):
        cnt = self.c[cname]
        cnt.val += 1
        v = cnt.val
        self.p.add(eng, lambda e, s=cnt.sem: e.drain().then_inc(s, 1))
        return v

    def dma(self, eng, cname, fn):
        self.dma_group(cname)
        cnt = self.c[cname]
        cnt.val += 16
        def run(e, fn=fn, s=cnt.sem):
            fn(e).then_inc(s, 16)
        self.p.add(eng, run)
        return cnt.val



def build_stage1(dbg=False, F=None):
    nc = F["nc"] if F else bass.Bass("TRN2", target_bir_lowering=False)
    x_h = F["x"] if F else nc.dram_tensor("x", [S, D], F32, kind="ExternalInput").ap()
    wq_h = F["Wq"] if F else nc.dram_tensor("Wq", [D, D], F32, kind="ExternalInput").ap()
    wk_h = F["Wk"] if F else nc.dram_tensor("Wk", [D, D], F32, kind="ExternalInput").ap()
    wv_h = F["Wv"] if F else nc.dram_tensor("Wv", [D, D], F32, kind="ExternalInput").ap()
    cF_h = F["cF"] if F else nc.dram_tensor("cF", [D, FPAD], F32, kind="ExternalInput").ap()
    cG_h = F["cG"] if F else nc.dram_tensor("cG", [FPAD, D], F32, kind="ExternalInput").ap()
    cLTT_h = F["cLTT"] if F else nc.dram_tensor("cLTT", [128, 128], F32, kind="ExternalInput").ap()
    cId_h = F["cIdent"] if F else nc.dram_tensor("cIdent", [128, 128], F32, kind="ExternalInput").ap()
    cOC_h = F["cOnesCol"] if F else nc.dram_tensor("cOnesCol", [128, 1], F32, kind="ExternalInput").ap()
    cOR_h = F["cOnesRow"] if F else nc.dram_tensor("cOnesRow", [1, 128], F32, kind="ExternalInput").ap()
    out_h = F["out"] if F else nc.dram_tensor("out", [S, D], F32, kind="ExternalOutput").ap()
    if dbg:
        dbg_h = {k: nc.dram_tensor("dbg_" + k, [128, FPAD], F32,
                                   kind="ExternalOutput").ap()
                 for k in ["fk", "fv", "fkn", "prod", "fkv", "fq", "z", "acc0", "xT0",
                           "vh", "zT0", "cG0", "xin0"]}

    with ExitStack() as ctx:
        def sb(name, shape, dtype):
            return ctx.enter_context(nc.sbuf_tensor(name, shape, dtype))
        def psum(name, shape, dtype):
            return ctx.enter_context(nc.psum_tensor(name, shape, dtype))
        def sem(name):
            return ctx.enter_context(nc.semaphore(name))
        xT = sb("xT", [128, ND * S], BF16)
        wF = sb("wF", [128, 3 * ND * FPAD], BF16)
        cGs = sb("cGs", [128, NFK * D], BF16)
        ident = sb("ident", [128, 128], BF16)
        ltt = sb("ltt", [128, 128], BF16)
        onesc = sb("onesc", [128, 1], BF16)
        onesr = sb("onesr", [1, 128], BF16)
        fk = sb("fk", [128, FPAD], F32)
        fv = sb("fv", [128, FPAD], F32)
        fq = sb("fq", [128, FPAD], F32)
        sqb = sb("sq", [128, FH], F32)
        mag = sb("mag", [128, FH], F32)
        prodb = sb("prod", [128, FPAD], BF16)
        fkv = sb("fkv", [128, FPAD], F32)
        carry = sb("carry", [1, FPAD], BF16)
        zt = sb("zt", [128, FPAD], BF16)
        zT = sb("zT", [128, NFK * 128], BF16)
        acc = sb("acc", [128, D], F32)
        m1 = sb("m1", [128, 1], F32)
        m2 = sb("m2", [128, 1], F32)
        epscol = sb("epscol", [128, 1], F32)
        fold_ctx = ExitStack()
        cFs = fold_ctx.enter_context(nc.sbuf_tensor("cFs", [128, ND * FPAD], BF16))
        wtmp = fold_ctx.enter_context(nc.sbuf_tensor("wtmp", [128, ND * D], BF16))
        pA = psum("pA", [128, 512], F32)
        pB = psum("pB", [128, 512], F32)
        pC = psum("pC", [128, 512], F32)
        pD = psum("pD", [128, 512], F32)
        pE = psum("pE", [128, 512], F32)
        pF = psum("pF", [128, 512], F32)
        pT1 = psum("pT1", [128, 128], BF16)
        pT2 = psum("pT2", [128, 128], BF16)
        if F:
            E = F["E"]; p = F["p"]
        else:
            s_t0 = sem("s_t"); s_v0 = sem("s_v"); s_s0 = sem("s_s"); s_g0 = sem("s_g")
            p = Prog()
            E = Emit(p, {"t": s_t0, "v": s_v0, "s": s_s0, "g": s_g0}, nc=nc, ctx=ctx)
        s_t = E.c["t"].sem; s_v = E.c["v"].sem; s_s = E.c["s"].sem; s_g = E.c["g"].sem
        E.dma("gpsimd", "gC", lambda e: e.dma_start(out=ident[:], in_=cId_h[:]))
        E.dma("gpsimd", "gC", lambda e: e.dma_start(out=ltt[:], in_=cLTT_h[:]))
        E.dma("gpsimd", "gC", lambda e: e.dma_start(out=onesc[:], in_=cOC_h[:]))
        E.dma("gpsimd", "gC", lambda e: e.dma_start(out=onesr[:], in_=cOR_h[:]))
        E.dma("gpsimd", "gC", lambda e: e.dma_start(
            out=cFs[:].rearrange("p (n m) -> p n m", n=ND),
            in_=cF_h.rearrange("(n p) m -> p n m", p=128)))
        E.dma("gpsimd", "gC", lambda e: e.dma_start(
            out=cGs[:].rearrange("p (n m) -> p n m", n=NFK),
            in_=cG_h.rearrange("(n p) m -> p n m", p=128)))
        ld_consts = E.c["gC"].val

        def zinit(e):
            e.memset(prodb[:, 2 * FH:FPAD], 0.0)
            e.memset(carry[:, :], 0.0)
            e.memset(zt[:, 2 * FH:FPAD], 0.0)
            e.memset(fkv[:, 2 * FH:FPAD], 0.0)
            e.memset(epscol[:], LN_EPS)
        E.on("vector", zinit)
        E.milestone("vector", "v")

        bev1 = {}
        for wi, w_h in enumerate((wq_h, wk_h, wv_h)):
            ld_w = E.dma("gpsimd", "gC", lambda e, w_h=w_h: e.dma_start(
                out=wtmp[:].rearrange("p (n m) -> p n m", n=ND),
                in_=w_h.rearrange("(n p) m -> p n m", p=128)))
            for dt in range(ND):
                psg = ((pA, pB, pC), (pD, pE, pF))[dt % 2]
                bt_wait(E, bev1, "f%d" % (dt % 2))
                for ch in range(3):
                    n0 = ch * 512
                    n1 = min(n0 + 512, FPAD)
                    ps = psg[ch]
                    def mm_fold(e, dt=dt, n0=n0, n1=n1, ps=ps, ld_w=ld_w):
                        e.wait_ge(E.c["gC"].sem, max(ld_w, ld_consts))
                        for kt in range(ND):
                            e.matmul(
                                ps[:, 0:n1 - n0],
                                wtmp[:, kt * D + dt * 128:kt * D + (dt + 1) * 128],
                                cFs[:, kt * FPAD + n0:kt * FPAD + n1],
                                start=(kt == 0), stop=(kt == ND - 1))
                    E.on("tensor", mm_fold)
                tms = E.milestone("tensor", "t")
                E.wait("vector", "t", tms)
                def ev_fold(e, wi=wi, dt=dt, psg=psg):
                    base = (wi * ND + dt) * FPAD
                    e.tensor_copy(wF[:, base:base + 512], psg[0][:, :])
                    e.tensor_copy(wF[:, base + 512:base + 1024], psg[1][:, :])
                    e.tensor_copy(wF[:, base + 1024:base + FPAD],
                                  psg[2][:, 0:FPAD - 1024])
                E.on("vector", ev_fold)
                bt_set(bev1, "f%d" % (dt % 2), "v", E.milestone("vector", "v"))
            if wi < 2:
                E.wait("gpsimd", "t", E.c["t"].val)
        folds_v = E.c["v"].val
        folds_t = E.c["t"].val
        fold_ctx.close()
        xin = sb("xin", [128, 2 * D], F32)
        xinb = sb("xinb", [128, 2 * D], BF16)

        E.wait("sync", "t", folds_t)
        cast_v_ms = {}
        trans_t_ms = {}
        for tt in range(NT):
            bo = (tt % 2) * D
            if tt >= 2:
                E.wait("sync", "v", cast_v_ms[tt - 2])
            grpn = ("xA", "xB")[tt % 2]
            ldv = E.dma("sync", grpn, lambda e, tt=tt, bo=bo: e.dma_start(
                out=xin[:, bo:bo + D], in_=x_h[tt * 128:(tt + 1) * 128, :]))
            E.wait("vector", grpn, ldv)
            if tt >= 2:
                E.wait("vector", "t", trans_t_ms[tt - 2])
            E.on("vector", lambda e, bo=bo: e.tensor_copy(xinb[:, bo:bo + D],
                                                          xin[:, bo:bo + D]))
            cast_v_ms[tt] = E.milestone("vector", "v")
            E.wait("tensor", "v", max(cast_v_ms[tt], folds_v))
            for dt in range(ND):
                pt = (pT1, pT2)[dt % 2]
                bt_wait(E, bev1, "pT%d" % (dt % 2))
                def tr1(e, bo=bo, dt=dt, pt=pt):
                    e.transpose(pt[:], xinb[:, bo + dt * 128:bo + (dt + 1) * 128],
                                ident[:])
                E.on("tensor", tr1)
                tms = E.milestone("tensor", "t")
                E.wait("vector", "t", tms)
                def ev1(e, tt=tt, dt=dt, pt=pt):
                    e.tensor_copy(xT[:, dt * S + tt * 128:dt * S + (tt + 1) * 128],
                                  pt[:])
                E.on("vector", ev1)
                bt_set(bev1, "pT%d" % (dt % 2), "v", E.milestone("vector", "v"))
            trans_t_ms[tt] = E.c["t"].val
        xT_done_v = E.c["v"].val

        def emit_proj(dst_ps, w_idx, tt):
            for ch in range(3):
                n0 = ch * 512
                n1 = min(n0 + 512, FPAD)
                ps = dst_ps[ch]
                def mm(e, w_idx=w_idx, tt=tt, n0=n0, n1=n1, ps=ps):
                    for kt in range(ND):
                        e.matmul(
                            ps[:, 0:n1 - n0],
                            xT[:, kt * S + tt * 128:kt * S + (tt + 1) * 128],
                            wF[:, (w_idx * ND + kt) * FPAD + n0:
                               (w_idx * ND + kt) * FPAD + n1],
                            start=(kt == 0), stop=(kt == ND - 1))
                E.on("tensor", mm)

        def emit_normalize(dst):
            def v1(e, dst=dst):
                e.tensor_mul(mag[:, 0:FH], dst[:, 0:FH], dst[:, 0:FH])
                e.tensor_mul(sqb[:, 0:FH], dst[:, FH:2 * FH], dst[:, FH:2 * FH])
                e.tensor_add(mag[:, 0:FH], mag[:, 0:FH], sqb[:, 0:FH])
            E.on("vector", v1)
            vm = E.milestone("vector", "v")
            E.wait("scalar", "v", vm)
            E.on("scalar", lambda e: e.activation(mag[:, 0:FH], mag[:, 0:FH],
                                                  AF.Sqrt))
            sm = E.milestone("scalar", "s")
            E.wait("vector", "s", sm)
            def v2(e, dst=dst):
                e.tensor_scalar_add(mag[:, 0:FH], mag[:, 0:FH], 1e-8)
                e.reciprocal(mag[:, 0:FH], mag[:, 0:FH])
                e.tensor_mul(dst[:, 0:FH], dst[:, 0:FH], mag[:, 0:FH])
                e.tensor_mul(dst[:, FH:2 * FH], dst[:, FH:2 * FH], mag[:, 0:FH])
            E.on("vector", v2)

        def dump(name, src_ap, ms_eng="v"):
            if not dbg:
                return
            ms = E.c[ms_eng].val
            E.on("gpsimd", lambda e, ms=ms: e.wait_ge(E.c[ms_eng].sem, ms))
            if "db" in E.c and E.c["db"].val:
                E.on("gpsimd", lambda e, v=E.c["db"].val: e.wait_ge(E.c["db"].sem, v))
            E.dma("gpsimd", "db", lambda e, name=name, src_ap=src_ap:
                  e.dma_start(out=dbg_h[name][:, 0:src_ap.shape[-1]], in_=src_ap))
            for eng in ("vector", "scalar", "sync"):
                E.wait(eng, "db", E.c["db"].val)

        if dbg:
            E.wait("sync", "v", xT_done_v)
            E.on("gpsimd", lambda e: e.wait_ge(s_v, xT_done_v))
            E.dma("gpsimd", "db", lambda e: e.dma_start(out=dbg_h["xT0"][:, 0:128],
                                                        in_=xT[:, 0:128]))
        E.wait("tensor", "v", xT_done_v)
        res_v_ms = {}
        for tt in range(NT):
            emit_proj((pA, pB, pC), 1, tt)
            emit_proj((pD, pE, pF), 2, tt)
            t_kv = E.milestone("tensor", "t")
            E.wait("vector", "t", t_kv)
            def ev_kv(e):
                e.tensor_copy(fk[:, 0:512], pA[:, :])
                e.tensor_copy(fk[:, 512:1024], pB[:, :])
                e.tensor_copy(fk[:, 1024:FPAD], pC[:, 0:FPAD - 1024])
                e.tensor_copy(fv[:, 0:512], pD[:, :])
                e.tensor_copy(fv[:, 512:1024], pE[:, :])
                e.tensor_copy(fv[:, 1024:FPAD], pF[:, 0:FPAD - 1024])
            E.on("vector", ev_kv)
            v_kv = E.milestone("vector", "v")
            if tt == 0:
                dump("fk", fk[:, 0:FPAD])
                dump("fv", fv[:, 0:FPAD])
            E.wait("tensor", "v", v_kv)
            emit_proj((pA, pB, pC), 0, tt)
            t_q = E.milestone("tensor", "t")

            emit_normalize(fk)
            def v_prod(e):
                e.tensor_mul(sqb[:, 0:FH], fk[:, FH:2 * FH], fv[:, FH:2 * FH])
                e.tensor_mul(mag[:, 0:FH], fk[:, 0:FH], fv[:, 0:FH])
                e.tensor_sub(prodb[:, 0:FH], mag[:, 0:FH], sqb[:, 0:FH])
                e.tensor_mul(sqb[:, 0:FH], fk[:, 0:FH], fv[:, FH:2 * FH])
                e.tensor_mul(mag[:, 0:FH], fk[:, FH:2 * FH], fv[:, 0:FH])
                e.tensor_add(prodb[:, FH:2 * FH], sqb[:, 0:FH], mag[:, 0:FH])
            E.on("vector", v_prod)
            v_prod_ms = E.milestone("vector", "v")
            if tt == 0:
                dump("fkn", fk[:, 0:FPAD])
                dump("prod", prodb[:, 0:FPAD])

            E.wait("vector", "t", t_q)
            def ev_q(e):
                e.tensor_copy(fq[:, 0:512], pA[:, :])
                e.tensor_copy(fq[:, 512:1024], pB[:, :])
                e.tensor_copy(fq[:, 1024:FPAD], pC[:, 0:FPAD - 1024])
            E.on("vector", ev_q)
            v_q = E.milestone("vector", "v")

            E.wait("tensor", "v", v_prod_ms)
            for ch in range(3):
                n0 = ch * 512
                n1 = min(n0 + 512, FPAD)
                ps = (pD, pE, pF)[ch]
                def mm_cs(e, n0=n0, n1=n1, ps=ps):
                    e.matmul(ps[:, 0:n1 - n0], ltt[:], prodb[:, n0:n1],
                             start=True, stop=False)
                    e.matmul(ps[:, 0:n1 - n0], onesr[:], carry[:, n0:n1],
                             start=False, stop=True)
                E.on("tensor", mm_cs)
            t_cs = E.milestone("tensor", "t")
            E.wait("tensor", "v", v_q)
            def mm_carry(e):
                e.matmul(pA[0:1, 0:512], onesc[:], prodb[:, 0:512],
                         start=True, stop=False)
                e.matmul(pA[0:1, 0:512], onesr[:, 0:1], carry[:, 0:512],
                         start=False, stop=True)
                e.matmul(pB[0:1, 0:512], onesc[:], prodb[:, 512:1024],
                         start=True, stop=False)
                e.matmul(pB[0:1, 0:512], onesr[:, 0:1], carry[:, 512:1024],
                         start=False, stop=True)
                e.matmul(pC[0:1, 0:128], onesc[:], prodb[:, 1024:FPAD],
                         start=True, stop=False)
                e.matmul(pC[0:1, 0:128], onesr[:, 0:1], carry[:, 1024:FPAD],
                         start=False, stop=True)
            E.on("tensor", mm_carry)
            t_carry = E.milestone("tensor", "t")

            E.wait("vector", "t", t_carry)
            def ev_cs(e):
                e.tensor_copy(fkv[:, 0:512], pD[:, :])
                e.tensor_copy(fkv[:, 512:1024], pE[:, :])
                e.tensor_copy(fkv[:, 1024:2 * FH], pF[:, 0:2 * FH - 1024])
                e.tensor_copy(carry[:, 0:512], pA[0:1, 0:512])
                e.tensor_copy(carry[:, 512:1024], pB[0:1, 0:512])
                e.tensor_copy(carry[:, 1024:FPAD], pC[0:1, 0:128])
            E.on("vector", ev_cs)
            v_cs = E.milestone("vector", "v")
            if tt == 0:
                dump("fkv", fkv[:, 0:FPAD])
            E.wait("tensor", "v", v_cs)

            emit_normalize(fq)
            def v_z(e):
                e.tensor_mul(sqb[:, 0:FH], fkv[:, FH:2 * FH], fq[:, FH:2 * FH])
                e.tensor_mul(mag[:, 0:FH], fkv[:, 0:FH], fq[:, 0:FH])
                e.tensor_add(zt[:, 0:FH], mag[:, 0:FH], sqb[:, 0:FH])
                e.tensor_mul(sqb[:, 0:FH], fkv[:, 0:FH], fq[:, FH:2 * FH])
                e.tensor_mul(mag[:, 0:FH], fkv[:, FH:2 * FH], fq[:, 0:FH])
                e.tensor_sub(zt[:, FH:2 * FH], mag[:, 0:FH], sqb[:, 0:FH])
            E.on("vector", v_z)
            v_zms = E.milestone("vector", "v")
            if tt == 0:
                dump("fq", fq[:, 0:FPAD])
                dump("z", zt[:, 0:FPAD])

            E.wait("tensor", "v", v_zms)
            for ft in range(NFK):
                pt = (pT1, pT2)[ft % 2]
                bt_wait(E, bev1, "pT%d" % (ft % 2))
                def trz(e, ft=ft, pt=pt):
                    e.transpose(pt[:], zt[:, ft * 128:(ft + 1) * 128], ident[:])
                E.on("tensor", trz)
                tms = E.milestone("tensor", "t")
                E.wait("vector", "t", tms)
                def evz(e, ft=ft, pt=pt):
                    e.tensor_copy(zT[:, ft * 128:(ft + 1) * 128], pt[:])
                E.on("vector", evz)
                bt_set(bev1, "pT%d" % (ft % 2), "v", E.milestone("vector", "v"))
            E.wait("tensor", "v", E.c["v"].val)
            for ch in range(2):
                ps = (pD, pE)[ch]
                def mm_ir(e, ch=ch, ps=ps):
                    for kt in range(NFK):
                        e.matmul(ps[:, :],
                                 zT[:, kt * 128:(kt + 1) * 128],
                                 cGs[:, kt * D + ch * 512:kt * D + (ch + 1) * 512],
                                 start=(kt == 0), stop=(kt == NFK - 1))
                E.on("tensor", mm_ir)
            t_ir = E.milestone("tensor", "t")

            bo = (tt % 2) * D
            if tt >= 2:
                E.wait("sync", "v", res_v_ms[tt - 2])
            else:
                E.wait("sync", "v", cast_v_ms[NT - 2 + tt])
            grpn = ("xA", "xB")[tt % 2]
            ldx = E.dma("sync", grpn, lambda e, tt=tt, bo=bo: e.dma_start(
                out=xin[:, bo:bo + D], in_=x_h[tt * 128:(tt + 1) * 128, :]))
            E.wait("vector", grpn, ldx)
            E.wait("vector", "t", t_ir)
            if "out" in E.c:
                E.wait("vector", "out", E.c["out"].val)
            if dbg and tt == 0:
                def v_dump_xin(e, bo=bo):
                    e.tensor_copy(fk[:, 0:D], xin[:, bo:bo + D])
                E.on("vector", v_dump_xin)
                E.milestone("vector", "v")
                dump("xin0", fk[:, 0:D])
                def v_dump_vh(e):
                    e.tensor_copy(fv[:, 0:512], pD[:, :])
                    e.tensor_copy(fv[:, 512:1024], pE[:, :])
                E.on("vector", v_dump_vh)
                E.milestone("vector", "v")
                dump("vh", fv[:, 0:1024])
                dump("zT0", zT[:, 0:128])
                dump("cG0", cGs[:, 0:1024])
            def v_res(e, bo=bo):
                e.scalar_tensor_tensor(acc[:, 0:512], xin[:, bo:bo + 512], 2.0,
                                       pD[:, :], ALU.mult, ALU.add)
                e.scalar_tensor_tensor(acc[:, 512:1024], xin[:, bo + 512:bo + D],
                                       2.0, pE[:, :], ALU.mult, ALU.add)
                e.reduce_sum(m1[:], acc[:, 0:D], axis=AX.X)
                e.tensor_scalar_mul(m1[:], m1[:], -1.0 / D)
                e.tensor_scalar_add(acc[:, 0:D], acc[:, 0:D], m1[:])
                e.tensor_mul(fk[:, 0:D], acc[:, 0:D], acc[:, 0:D])
                e.reduce_sum(m2[:], fk[:, 0:D], axis=AX.X)
            E.on("vector", v_res)
            v_res_ms = E.milestone("vector", "v")
            if tt == 0:
                dump("acc0", acc[:, 0:D])
            E.wait("scalar", "v", v_res_ms)
            E.on("scalar", lambda e: e.activation(m2[:], m2[:], AF.Sqrt,
                                                  bias=epscol[:], scale=1.0 / D))
            s_ln = E.milestone("scalar", "s")
            E.wait("vector", "s", s_ln)
            def v_fin(e):
                e.reciprocal(m2[:], m2[:])
                e.tensor_scalar_mul(acc[:, 0:D], acc[:, 0:D], m2[:])
            E.on("vector", v_fin)
            v_fin_ms = E.milestone("vector", "v")
            res_v_ms[tt] = v_fin_ms
            E.wait("tensor", "v", v_res_ms)
            E.wait("sync", "v", v_fin_ms)
            E.dma("sync", "out", lambda e, tt=tt: e.dma_start(
                out=out_h[tt * 128:(tt + 1) * 128, :], in_=acc[:, 0:D]))

        E.wait("sync", "out", E.c["out"].val)
        if dbg and "db" in E.c:
            E.on("gpsimd", lambda e, v=E.c["db"].val: e.wait_ge(E.c["db"].sem, v))
        if not F:
            p.play(ctx.enter_context(nc.Block()))
    return nc





def emit_barrier(E):
    vals = {k: c.val for k, c in E.c.items()}
    for eng in ENGINES:
        for cname, val in vals.items():
            E.wait(eng, cname, val)
    E.milestone("tensor", "t")
    E.milestone("vector", "v")
    E.milestone("scalar", "s")


def emit_x_to_xT(E, x_h, ntiles, xT, xin, xinb, ident, pT1, pT2, SW,
                 first_ld_guard=None, slots=("xA", "xB")):
    cast_v = {}
    trans_t = {}
    bev = {}
    v_entry = E.c["v"].val
    t_entry = E.c["t"].val
    for tt in range(ntiles):
        bo = (tt % 2) * D
        grpn = slots[tt % 2]
        if tt >= 2:
            E.wait("sync", "v", cast_v[tt - 2])
        else:
            E.wait("sync", "v", v_entry)
            E.wait("sync", "t", t_entry)
            if first_ld_guard:
                E.wait("sync", first_ld_guard[0], first_ld_guard[1])
        ldv = E.dma("sync", grpn, lambda e, tt=tt, bo=bo: e.dma_start(
            out=xin[:, bo:bo + D], in_=x_h[tt * 128:(tt + 1) * 128, :]))
        E.wait("vector", grpn, ldv)
        if tt >= 2:
            E.wait("vector", "t", trans_t[tt - 2])
        else:
            E.wait("vector", "t", t_entry)
        E.on("vector", lambda e, bo=bo: e.tensor_copy(xinb[:, bo:bo + D],
                                                      xin[:, bo:bo + D]))
        cast_v[tt] = E.milestone("vector", "v")
        E.wait("tensor", "v", cast_v[tt])
        for dt in range(ND):
            pt = (pT1, pT2)[dt % 2]
            bt_wait(E, bev, dt % 2)
            def tr1(e, bo=bo, dt=dt, pt=pt):
                e.transpose(pt[:], xinb[:, bo + dt * 128:bo + (dt + 1) * 128],
                            ident[:])
            E.on("tensor", tr1)
            tms = E.milestone("tensor", "t")
            E.wait("vector", "t", tms)
            def ev1(e, tt=tt, dt=dt, pt=pt):
                e.tensor_copy(xT[:, dt * SW + tt * 128:dt * SW + (tt + 1) * 128],
                              pt[:])
            E.on("vector", ev1)
            bt_set(bev, dt % 2, "v", E.milestone("vector", "v"))
        trans_t[tt] = E.c["t"].val
    E.wait("tensor", "v", E.c["v"].val)
    return cast_v, trans_t


def emit_w_transpose(E, w_h, rows, wtmp, wT, wT_colstride, ident, pT1, pT2,
                     half_rows=None):
    nrt = rows // 128
    hr = half_rows or nrt
    bev = {}
    for h0 in range(0, nrt, hr):
        cnt = min(hr, nrt - h0)
        ld = E.dma("gpsimd", "wT", lambda e, h0=h0, cnt=cnt: e.dma_start(
            out=wtmp[:].rearrange("p (n m) -> p n m", n=hr)[:, 0:cnt, :],
            in_=w_h.rearrange("(n p) m -> p n m", p=128)[:, h0:h0 + cnt, :]))
        E.wait("tensor", "wT", ld)
        for et in range(cnt):
            for dt in range(ND):
                pt = (pT1, pT2)[dt % 2]
                bt_wait(E, bev, dt % 2)
                def trw(e, et=et, dt=dt, pt=pt):
                    e.transpose(pt[:], wtmp[:, et * D + dt * 128:et * D + (dt + 1) * 128],
                                ident[:])
                E.on("tensor", trw)
                tms = E.milestone("tensor", "t")
                E.wait("vector", "t", tms)
                def evw(e, et=et, h0=h0, dt=dt, pt=pt):
                    col = (h0 + et) * 128
                    e.tensor_copy(
                        wT[:, dt * wT_colstride + col:dt * wT_colstride + col + 128],
                        pt[:])
                E.on("vector", evw)
                bt_set(bev, dt % 2, "v", E.milestone("vector", "v"))
        E.wait("gpsimd", "t", E.c["t"].val)
    E.wait("tensor", "v", E.c["v"].val)


def emit_ln_tile(E, acc, m1, m2, sqbuf, epscol, gcast=None, bcast=None,
                 out_bf=None, out_f32=None, width=D):
    def v1(e):
        e.reduce_sum(m1[:], acc[:, 0:width], axis=AX.X)
        e.tensor_scalar_mul(m1[:], m1[:], -1.0 / width)
        e.tensor_scalar_add(acc[:, 0:width], acc[:, 0:width], m1[:])
        e.tensor_mul(sqbuf[:, 0:width], acc[:, 0:width], acc[:, 0:width])
        e.reduce_sum(m2[:], sqbuf[:, 0:width], axis=AX.X)
    E.on("vector", v1)
    vm = E.milestone("vector", "v")
    E.wait("scalar", "v", vm)
    E.on("scalar", lambda e: e.activation(m2[:], m2[:], AF.Sqrt,
                                          bias=epscol[:], scale=1.0 / width))
    sm = E.milestone("scalar", "s")
    E.wait("vector", "s", sm)
    def v2(e):
        e.reciprocal(m2[:], m2[:])
        e.tensor_scalar_mul(acc[:, 0:width], acc[:, 0:width], m2[:])
        if gcast is not None:
            e.tensor_mul(acc[:, 0:width], acc[:, 0:width], gcast[:, 0:width])
            e.tensor_add(acc[:, 0:width], acc[:, 0:width], bcast[:, 0:width])
        if out_bf is not None:
            e.tensor_copy(out_bf, acc[:, 0:width])
        if out_f32 is not None:
            e.tensor_copy(out_f32, acc[:, 0:width])
    E.on("vector", v2)
    return E.milestone("vector", "v")


def emit_bcast_row(E, row_f32, dst, onescol_f32, ps, width=D):
    vm = 0
    for ch in range((width + 511) // 512):
        n0, n1 = ch * 512, min((ch + 1) * 512, width)
        def mm(e, n0=n0, n1=n1):
            e.matmul(ps[:, 0:n1 - n0], onescol_f32[0:1, :], row_f32[0:1, n0:n1],
                     start=True, stop=True)
        E.on("tensor", mm)
        tms = E.milestone("tensor", "t")
        E.wait("vector", "t", tms)
        def ev(e, n0=n0, n1=n1):
            e.tensor_copy(dst[:, n0:n1], ps[:, 0:n1 - n0])
        E.on("vector", ev)
        vm = E.milestone("vector", "v")
        E.wait("tensor", "v", vm)
    return vm



NH = 16
HD2 = 64
NPAIR = 8
NTM = MY // 128
FF = 2048
NE0 = 8


def build_stage2(dbg=False, F=None):
    nc = F["nc"] if F else bass.Bass("TRN2", target_bir_lowering=False)
    x1_h = F["x1"] if F else nc.dram_tensor("x1", [S, D], F32, kind="ExternalInput").ap()
    wi_h = F["wi"] if F else nc.dram_tensor("wi", [3 * D, D], F32, kind="ExternalInput").ap()
    bicols_h = F["bicols"] if F else nc.dram_tensor("bicols", [128, 16], F32, kind="ExternalInput").ap()
    bivrow_h = F["bivrow"] if F else nc.dram_tensor("bivrow", [1, D], F32, kind="ExternalInput").ap()
    wo_h = F["wo"] if F else nc.dram_tensor("wo", [D, D], F32, kind="ExternalInput").ap()
    borow_h = F["borow"] if F else nc.dram_tensor("borow", [1, D], F32, kind="ExternalInput").ap()
    g1_h = F["ln1g"] if F else nc.dram_tensor("ln1g", [1, D], F32, kind="ExternalInput").ap()
    b1_h = F["ln1b"] if F else nc.dram_tensor("ln1b", [1, D], F32, kind="ExternalInput").ap()
    g2_h = F["ln2g"] if F else nc.dram_tensor("ln2g", [1, D], F32, kind="ExternalInput").ap()
    b2_h = F["ln2b"] if F else nc.dram_tensor("ln2b", [1, D], F32, kind="ExternalInput").ap()
    wg_h = F["wg"] if F else nc.dram_tensor("wg", [D, 4], F32, kind="ExternalInput").ap()
    bg_h = F["bg"] if F else nc.dram_tensor("bg", [1, 4], F32, kind="ExternalInput").ap()
    wef_h = F["wef"] if F else nc.dram_tensor("wef", [D, 8], F32, kind="ExternalInput").ap()
    bef_h = F["bef"] if F else nc.dram_tensor("bef", [1, 8], F32, kind="ExternalInput").ap()
    w1_h = F["w1"] if F else nc.dram_tensor("w1", [NE0, D, FF], F32, kind="ExternalInput").ap()
    b1c_h = F["b1cols"] if F else nc.dram_tensor("b1cols", [128, NE0 * 16], F32, kind="ExternalInput").ap()
    w2_h = F["w2"] if F else nc.dram_tensor("w2", [NE0, FF, D], F32, kind="ExternalInput").ap()
    b2r_h = F["b2rows"] if F else nc.dram_tensor("b2rows", [1, NE0 * D], F32, kind="ExternalInput").ap()
    cId_h = F["cIdent"] if F else nc.dram_tensor("cIdent", [128, 128], F32, kind="ExternalInput").ap()
    cOR_h = F["cOnesRow"] if F else nc.dram_tensor("cOnesRow", [1, 128], F32, kind="ExternalInput").ap()
    out_h = F["out"] if F else nc.dram_tensor("out", [S, D], F32, kind="ExternalOutput").ap()
    if dbg:
        dbg_h = {k: nc.dram_tensor("dbg_" + k, [128, 4096], F32,
                                   kind="ExternalOutput").ap()
                 for k in ["v0", "kT0", "qT0", "ao0", "u0", "comb", "yacc0", "ET0"]}
    kTs = nc.dram_tensor("kTs", [NPAIR, 128, S], BF16, kind="Internal").ap()
    qTs = nc.dram_tensor("qTs", [NPAIR, 128, S], BF16, kind="Internal").ap()
    vs = nc.dram_tensor("vs", [S, D], BF16, kind="Internal").ap()
    us = nc.dram_tensor("us", [S, D], F32, kind="Internal").ap()
    aoTs = nc.dram_tensor("aoTs", [128, ND * S], BF16, kind="Internal").ap()
    UMts = nc.dram_tensor("UMts", [128, ND * S], BF16, kind="Internal").ap()

    with ExitStack() as ctx:
        def sb(name, shape, dtype, c=None):
            return (c or ctx).enter_context(nc.sbuf_tensor("b" + name, shape, dtype))
        def psum(name, shape, dtype):
            return ctx.enter_context(nc.psum_tensor("b" + name, shape, dtype))
        def sem(name):
            return ctx.enter_context(nc.semaphore("b" + name))

        ident = sb("ident", [128, 128], BF16)
        onesrB = sb("onesrB", [1, 128], BF16)
        onesrF = sb("onesrF", [1, 128], F32)
        epscol = sb("epscol", [128, 1], F32)
        bicols = sb("bicols_s", [128, 16], F32)
        bivrow = sb("bivrow_s", [1, D], BF16)
        borow = sb("borow_s", [1, D], BF16)
        b1cols = sb("b1cols_s", [128, NE0 * 16], F32)
        wgsb = sb("wgsb", [128, ND * 4], BF16)
        wesb = sb("wesb", [128, ND * 8], BF16)
        bgrow = sb("bgrow", [1, 4], BF16)
        berow = sb("berow", [1, 8], BF16)
        combine = sb("combine", [128, NT * NE0], F32)
        m1 = sb("m1", [128, 1], F32)
        m2 = sb("m2", [128, 1], F32)
        m11 = sb("m11", [1, 1], F32)
        negmax = sb("negmax", [128, 1], F32)
        acc = sb("acc", [128, D], F32)
        sqd = sb("sqd", [128, D], F32)
        xin = sb("xin", [128, 2 * D], F32)
        xinb = sb("xinb", [128, 2 * D], BF16)
        identF = sb("identF", [128, 128], F32)
        woT = sb("woT", [128, ND * D], BF16)

        phA = ExitStack()
        xT = sb("xT", [128, ND * S], BF16, phA)
        wiT = sb("wiT", [128, ND * 3 * D], BF16, phA)
        wtmp2 = sb("wtmp2", [128, 8 * D], BF16, phA)

        pA = psum("pA", [128, 512], F32)
        pB = psum("pB", [128, 512], F32)
        pC = psum("pC", [128, 512], F32)
        pD = psum("pD", [128, 512], F32)
        pMax = psum("pMax", [1, 128], F32)
        pCol = psum("pCol", [128, 4], F32)
        pT1 = psum("pT1", [128, 128], BF16)
        pT2 = psum("pT2", [128, 128], BF16)

        if F:
            E = F["E"]; p = F["p"]
        else:
            s_t0 = sem("s_t"); s_v0 = sem("s_v"); s_s0 = sem("s_s"); s_g0 = sem("s_g")
            p = Prog()
            E = Emit(p, {"t": s_t0, "v": s_v0, "s": s_s0, "g": s_g0}, nc=nc, ctx=ctx)
        s_t = E.c["t"].sem; s_v = E.c["v"].sem; s_s = E.c["s"].sem; s_g = E.c["g"].sem

        def dump(name, src_ap, cname="v", guards=("vector", "scalar", "sync")):
            if not dbg:
                return
            ms = E.c[cname].val
            E.on("gpsimd", lambda e, ms=ms, cname=cname: e.wait_ge(E.c[cname].sem, ms))
            if "db" in E.c and E.c["db"].val:
                E.on("gpsimd", lambda e, v=E.c["db"].val: e.wait_ge(E.c["db"].sem, v))
            E.dma("gpsimd", "db", lambda e, name=name, src_ap=src_ap: e.dma_start(
                out=dbg_h[name][0:src_ap.shape[0], 0:src_ap.shape[-1]], in_=src_ap))
            for eng in guards:
                E.wait(eng, "db", E.c["db"].val)

        E.dma("gpsimd", "gC", lambda e: e.dma_start(out=ident[:], in_=cId_h[:]))
        E.dma("gpsimd", "gC", lambda e: e.dma_start(out=onesrB[:], in_=cOR_h[:]))
        E.dma("gpsimd", "gC", lambda e: e.dma_start(out=onesrF[:], in_=cOR_h[:]))
        E.dma("sync", "sC", lambda e: e.dma_start(out=identF[:], in_=cId_h[:]))
        E.dma("gpsimd", "gC", lambda e: e.dma_start(out=bicols[:], in_=bicols_h[:]))
        E.dma("gpsimd", "gC", lambda e: e.dma_start(out=bivrow[:], in_=bivrow_h[:]))
        E.dma("gpsimd", "gC", lambda e: e.dma_start(out=borow[:], in_=borow_h[:]))
        E.dma("gpsimd", "gC", lambda e: e.dma_start(out=b1cols[:], in_=b1c_h[:]))
        E.dma("gpsimd", "gC", lambda e: e.dma_start(
            out=wgsb[:].rearrange("p (n m) -> p n m", n=ND),
            in_=wg_h.rearrange("(n p) m -> p n m", p=128)))
        E.dma("gpsimd", "gC", lambda e: e.dma_start(
            out=wesb[:].rearrange("p (n m) -> p n m", n=ND),
            in_=wef_h.rearrange("(n p) m -> p n m", p=128)))
        E.dma("gpsimd", "gC", lambda e: e.dma_start(out=bgrow[:], in_=bg_h[:]))
        E.dma("gpsimd", "gC", lambda e: e.dma_start(out=berow[:], in_=bef_h[:]))
        ld_g = E.c["gC"].val
        ld_s = E.c["sC"].val
        E.on("vector", lambda e: e.memset(epscol[:], LN_EPS))
        E.wait("vector", "gC", ld_g)
        E.wait("vector", "sC", ld_s)
        E.wait("tensor", "gC", ld_g)
        E.wait("tensor", "sC", ld_s)

        emit_x_to_xT(E, x1_h, NT, xT, xin, xinb, ident, pT1, pT2, S)
        emit_w_transpose(E, wi_h, 3 * D, wtmp2, wiT, 3 * D, ident, pT1, pT2,
                         half_rows=8)
        emit_w_transpose(E, wo_h, D, wtmp2, woT, D, ident, pT1, pT2,
                         half_rows=8)

        bev2 = {}
        kq_dma = {}
        ci = 0
        for which in range(2):
            wof = (D, 0)[which]
            dst_s = (kTs, qTs)[which]
            bco = (8, 0)[which]
            for pr in range(NPAIR):
                for tc in range(S // 512):
                    slot = ci % 4
                    xo = slot * 512
                    grp = "kq%d" % slot
                    ps = (pA, pB)[ci % 2]
                    bt_wait(E, bev2, ci % 2)
                    def mmk(e, pr=pr, tc=tc, ps=ps, wof=wof):
                        for kt in range(ND):
                            e.matmul(
                                ps[:, :],
                                wiT[:, kt * 3 * D + wof + pr * 128:kt * 3 * D + wof + (pr + 1) * 128],
                                xT[:, kt * S + tc * 512:kt * S + (tc + 1) * 512],
                                start=(kt == 0), stop=(kt == ND - 1))
                    E.on("tensor", mmk)
                    tms = E.milestone("tensor", "t")
                    E.wait("vector", "t", tms)
                    if slot in kq_dma:
                        E.wait("vector", grp, kq_dma[slot])
                    def evk(e, pr=pr, ps=ps, xo=xo, bco=bco):
                        e.tensor_scalar_add(xinb[:, xo:xo + 512], ps[:, :],
                                            bicols[:, bco + pr:bco + pr + 1])
                    E.on("vector", evk)
                    sms = E.milestone("vector", "v")
                    bt_set(bev2, ci % 2, "v", sms)
                    E.on("gpsimd", lambda e, sms=sms: e.wait_ge(s_v, sms))
                    gv = E.dma("gpsimd", grp, lambda e, pr=pr, tc=tc, xo=xo,
                               dst_s=dst_s: e.dma_start(
                        out=dst_s[pr, :, tc * 512:(tc + 1) * 512],
                        in_=xinb[:, xo:xo + 512]))
                    kq_dma[slot] = gv
                    ci += 1
        for sl in range(4):
            E.wait("vector", "kq%d" % sl, kq_dma[sl])
        v_dma = {}
        for tt in range(NT):
            xo = (tt % 2) * 1024
            grp = "kqv%d" % (tt % 2)
            for ch in range(2):
                ps = (pA, pB)[ch]
                bt_wait(E, bev2, ch)
                def mmv(e, tt=tt, ch=ch, ps=ps):
                    e.matmul(ps[:, :], onesrB[:], bivrow[0:1, ch * 512:(ch + 1) * 512],
                             start=True, stop=False)
                    for kt in range(ND):
                        e.matmul(
                            ps[:, :],
                            xT[:, kt * S + tt * 128:kt * S + (tt + 1) * 128],
                            wiT[:, kt * 3 * D + 2 * D + ch * 512:kt * 3 * D + 2 * D + (ch + 1) * 512],
                            start=False, stop=(kt == ND - 1))
                E.on("tensor", mmv)
            tms = E.milestone("tensor", "t")
            E.wait("vector", "t", tms)
            if (tt % 2) in v_dma:
                E.wait("vector", grp, v_dma[tt % 2])
            def evv(e, xo=xo):
                e.tensor_copy(xinb[:, xo:xo + 512], pA[:, :])
                e.tensor_copy(xinb[:, xo + 512:xo + 1024], pB[:, :])
            E.on("vector", evv)
            vms = E.milestone("vector", "v")
            bt_set(bev2, 0, "v", vms)
            bt_set(bev2, 1, "v", vms)
            if tt == 0:
                dump("v0", xinb[:, 0:1024])
            E.on("gpsimd", lambda e, vms=vms: e.wait_ge(s_v, vms))
            gv = E.dma("gpsimd", grp, lambda e, tt=tt, xo=xo: e.dma_start(
                out=vs[tt * 128:(tt + 1) * 128, :], in_=xinb[:, xo:xo + 1024]))
            v_dma[tt % 2] = gv

        emit_barrier(E)
        phA.close()
        gcast1 = sb("gcast1", [128, D], F32)
        bcast1 = sb("bcast1", [128, D], F32)
        gcast2 = sb("gcast2", [128, D], F32)
        bcast2 = sb("bcast2", [128, D], F32)
        phB = ExitStack()
        kTh = sb("kTh", [64, 2 * S], BF16, phB)
        qTh = sb("qTh", [64, 2 * S], BF16, phB)
        aosl = sb("aosl", [64, 1024], BF16, phB)
        vOnesE = sb("vOnesE", [128, NT * 128], BF16, phB)
        vOnesO = sb("vOnesO", [128, NT * 128], BF16, phB)
        osum = sb("osum", [128, 1024], F32, phB)
        ET = sb("ET", [128, NT * 512], BF16, phB)
        onorm = sb("onorm", [128, 1024], F32, phB)
        rows1 = sb("rows1", [1, D], F32, phB)
        for row_h, dstc in ((g1_h, gcast1), (b1_h, bcast1), (g2_h, gcast2),
                            (b2_h, bcast2)):
            E.wait("sync", "t", E.c["t"].val)
            ldr = E.dma("sync", "sC", lambda e, row_h=row_h: e.dma_start(
                out=rows1[0:1, 0:D], in_=row_h[:]))
            E.wait("tensor", "sC", ldr)
            emit_bcast_row(E, rows1[0:1, 0:D], dstc, onesrF, pA)

        E.on("vector", lambda e: e.memset(
            vOnesE[:].rearrange("p (n m) -> p n m", n=NT)[:, :, 64:128], 1.0))
        E.on("vector", lambda e: e.memset(
            vOnesO[:].rearrange("p (n m) -> p n m", n=NT)[:, :, 0:64], 1.0))
        E.milestone("vector", "v")

        head_ld = {}
        battn = {}
        wao_hist2 = {}
        for h in range(NH):
            pr, sub = h // 2, h % 2
            hb = (h % 2)
            hgrp = ("hA", "hB")[h % 2]
            if h >= 2:
                E.wait("sync", "t", head_ld[h - 2])
            E.dma("sync", hgrp, lambda e, pr=pr, sub=sub, hb=hb: e.dma_start(
                out=kTh[:, hb * S:(hb + 1) * S],
                in_=kTs[pr, sub * 64:(sub + 1) * 64, :]))
            E.dma("sync", hgrp, lambda e, pr=pr, sub=sub, hb=hb: e.dma_start(
                out=qTh[:, hb * S:(hb + 1) * S],
                in_=qTs[pr, sub * 64:(sub + 1) * 64, :]))
            vOn = (vOnesE, vOnesO)[h % 2]
            vlo = 0 if h % 2 == 0 else 64
            E.dma("sync", hgrp, lambda e, h=h, vOn=vOn, vlo=vlo: e.dma_start(
                out=vOn[:].rearrange("p (n m) -> p n m", n=NT)[:, :, vlo:vlo + 64],
                in_=vs.rearrange("(n p) m -> p n m", p=128)[:, :, h * HD2:(h + 1) * HD2]))
            ldh = E.c[hgrp].val
            if h == 0 and dbg:
                dump("kT0", kTh[0:64, 0:S], cname="hA")
                dump("qT0", qTh[0:64, 0:MY], cname="hA")
            E.wait("tensor", hgrp, ldh)
            for mc8 in range(8):
                ps = (pA, pB)[mc8 % 2]
                bt_wait(E, battn, mc8 % 2)
                def mm0(e, hb=hb, mc8=mc8, ps=ps):
                    e.matmul(ps[:, :], kTh[0:64, hb * S:hb * S + 128],
                             qTh[0:64, hb * S + mc8 * 512:hb * S + (mc8 + 1) * 512],
                             start=True, stop=True)
                E.on("tensor", mm0)
                t0 = E.milestone("tensor", "t")
                E.wait("vector", "t", t0)
                def vmax(e, mc8=mc8, ps=ps):
                    e.reduce_max(m2[:], ps[:, :], axis=AX.X)
                    if mc8 == 0:
                        e.tensor_copy(negmax[:], m2[:])
                    else:
                        e.tensor_tensor(negmax[:], negmax[:], m2[:], ALU.max)
                E.on("vector", vmax)
                bt_set(battn, mc8 % 2, "v", E.milestone("vector", "v"))
            E.wait("tensor", "v", E.c["v"].val)
            def tmax(e):
                e.transpose(pMax[:, :], negmax[:], identF[:])
            E.on("tensor", tmax)
            t1 = E.milestone("tensor", "t")
            E.wait("vector", "t", t1)
            def vmax2(e):
                e.reduce_max(m11[:], pMax[0:1, :], axis=AX.X)
                e.tensor_scalar_mul(m11[:], m11[:], -0.125)
            E.on("vector", vmax2)
            v2ms = E.milestone("vector", "v")
            E.wait("tensor", "v", v2ms)
            def tbc(e):
                e.matmul(pCol[:, 0:1], onesrF[0:1, :], m11[:], start=True,
                         stop=True)
            E.on("tensor", tbc)
            t2 = E.milestone("tensor", "t")
            E.wait("vector", "t", t2)
            E.on("vector", lambda e: e.tensor_copy(negmax[:], pCol[:, 0:1]))
            vnm = E.milestone("vector", "v")
            E.wait("scalar", "v", vnm)
            E.wait("tensor", "v", vnm)
            for tc in range(S // 512):
                for jt in range(NT):
                    ps = (pA, pB)[jt % 2]
                    bt_wait(E, battn, jt % 2)
                    def mmj(e, jt=jt, hb=hb, tc=tc, ps=ps):
                        e.matmul(ps[:, :],
                                 kTh[0:64, hb * S + jt * 128:hb * S + (jt + 1) * 128],
                                 qTh[0:64, hb * S + tc * 512:hb * S + (tc + 1) * 512],
                                 start=True, stop=True)
                    E.on("tensor", mmj)
                    tj = E.milestone("tensor", "t")
                    E.wait("scalar", "t", tj)
                    def sexp(e, jt=jt, ps=ps):
                        e.activation(ET[:, jt * 512:(jt + 1) * 512], ps[:, :],
                                     AF.Exp, bias=negmax[:], scale=0.125)
                    E.on("scalar", sexp)
                    bt_set(battn, jt % 2, "s", E.milestone("scalar", "s"))
                po_ = (pC, pD)[tc % 2]
                bt_wait(E, battn, "o%d" % (tc % 2))
                E.wait("tensor", "s", E.c["s"].val)
                def mmo(e, vOn=vOn, po_=po_):
                    for jt in range(NT):
                        e.matmul(po_[:, :], vOn[:, jt * 128:(jt + 1) * 128],
                                 ET[:, jt * 512:(jt + 1) * 512],
                                 start=(jt == 0), stop=(jt == NT - 1))
                E.on("tensor", mmo)
                to = E.milestone("tensor", "t")
                E.wait("vector", "t", to)
                oo = (tc % 2) * 512
                osgrp = ("os0", "os1")[tc % 2]
                aogrp = ("aoA", "aoB")[tc % 2]
                def vev(e, po_=po_, oo=oo):
                    e.tensor_copy(onorm[:, oo:oo + 512], po_[:, :])
                E.on("vector", vev)
                vev = E.milestone("vector", "v")
                bt_set(battn, "o%d" % (tc % 2), "v", vev)
                E.on("gpsimd", lambda e, vev=vev: e.wait_ge(s_v, vev))
                if h % 2 == 0:
                    mv = E.dma("gpsimd", osgrp, lambda e, oo=oo: e.dma_start(
                        out=osum[0:64, oo:oo + 512], in_=onorm[64:128, oo:oo + 512]))
                else:
                    mv = E.dma("gpsimd", osgrp, lambda e, oo=oo: e.dma_start(
                        out=osum[64:128, oo:oo + 512], in_=onorm[0:64, oo:oo + 512]))
                E.wait("vector", osgrp, mv)
                if (tc % 2) in wao_hist2:
                    E.wait("vector", aogrp, wao_hist2[tc % 2])
                def vnorm(e, h=h, oo=oo):
                    lo = (h % 2) * 64
                    e.reciprocal(osum[lo:lo + 64, oo:oo + 512],
                                 osum[lo:lo + 64, oo:oo + 512])
                    e.tensor_mul(aosl[0:64, oo:oo + 512],
                                 onorm[lo:lo + 64, oo:oo + 512],
                                 osum[lo:lo + 64, oo:oo + 512])
                E.on("vector", vnorm)
                vno = E.milestone("vector", "v")
                E.on("gpsimd", lambda e, vno=vno: e.wait_ge(s_v, vno))
                wao = E.dma("gpsimd", aogrp, lambda e, h=h, tc=tc, oo=oo:
                            e.dma_start(
                    out=bass.AP(tensor=aoTs.tensor,
                                offset=(h % 2) * 64 * (ND * S) + (h // 2) * S + tc * 512,
                                ap=[[ND * S, 64], [1, 512]]),
                    in_=aosl[0:64, oo:oo + 512]))
                wao_hist2[tc % 2] = wao
            head_ld[h] = E.c["t"].val


        emit_barrier(E)
        phB.close()
        phB2 = ExitStack()
        aog = sb("aog", [128, ND * 128], BF16, phB2)
        ustg = sb("ustg", [128, ND * 128], BF16, phB2)
        res_ms = {}
        mmop_ms = {}
        for tt in range(NT):
            bo = (tt % 2) * D
            grpn = ("xA", "xB")[tt % 2]
            if tt >= 2:
                E.wait("sync", "v", res_ms[tt - 2])
            else:
                E.wait("sync", "v", E.c["v"].val)
                E.wait("sync", "t", E.c["t"].val)
            ldx = E.dma("sync", grpn, lambda e, tt=tt, bo=bo: e.dma_start(
                out=xin[:, bo:bo + D], in_=x1_h[tt * 128:(tt + 1) * 128, :]))
            if (tt - 1) in mmop_ms:
                E.on("gpsimd", lambda e, v=mmop_ms[tt - 1]: e.wait_ge(E.c["t"].sem, v))
            lao = E.dma("gpsimd", "aol", lambda e, tt=tt: e.dma_start(
                out=aog[:].rearrange("p (n m) -> p n m", n=ND),
                in_=aoTs.rearrange("p (n m) -> p n m", n=ND)[:, :, tt * 128:(tt + 1) * 128]))
            E.wait("tensor", "aol", lao)
            for ch in range(2):
                ps = (pA, pB)[ch]
                def mmop(e, tt=tt, ch=ch, ps=ps):
                    e.matmul(ps[:, :], onesrB[:], borow[0:1, ch * 512:(ch + 1) * 512],
                             start=True, stop=False)
                    for kt in range(ND):
                        e.matmul(ps[:, :],
                                 aog[:, kt * 128:(kt + 1) * 128],
                                 woT[:, kt * D + ch * 512:kt * D + (ch + 1) * 512],
                                 start=False, stop=(kt == ND - 1))
                E.on("tensor", mmop)
            tms = E.milestone("tensor", "t")
            mmop_ms[tt] = tms
            E.wait("vector", "t", tms)
            E.wait("vector", grpn, ldx)
            if "us" in E.c:
                E.wait("vector", "us", E.c["us"].val)
            def vres(e, bo=bo):
                e.scalar_tensor_tensor(acc[:, 0:512], xin[:, bo:bo + 512], 1.0,
                                       pA[:, :], ALU.mult, ALU.add)
                e.scalar_tensor_tensor(acc[:, 512:1024], xin[:, bo + 512:bo + D],
                                       1.0, pB[:, :], ALU.mult, ALU.add)
            E.on("vector", vres)
            E.milestone("vector", "v")
            E.wait("tensor", "v", E.c["v"].val)
            lnm = emit_ln_tile(E, acc, m1, m2, sqd, epscol, gcast1, bcast1,
                               out_bf=xinb[:, 0:D])
            E.wait("sync", "v", lnm)
            E.dma("sync", "us", lambda e, tt=tt: e.dma_start(
                out=us[tt * 128:(tt + 1) * 128, :], in_=acc[:, 0:D]))
            stu = E.c["us"].val
            E.wait("tensor", "v", lnm)
            if "umw" in E.c:
                E.wait("vector", "umw", E.c["umw"].val)
            for dt in range(ND):
                pt = (pT1, pT2)[dt % 2]
                bt_wait(E, bev2, "pT%d" % (dt % 2))
                def tru(e, dt=dt, pt=pt):
                    e.transpose(pt[:], xinb[:, dt * 128:(dt + 1) * 128], ident[:])
                E.on("tensor", tru)
                tms = E.milestone("tensor", "t")
                E.wait("vector", "t", tms)
                def evu(e, dt=dt, pt=pt):
                    e.tensor_copy(ustg[:, dt * 128:(dt + 1) * 128], pt[:])
                E.on("vector", evu)
                bt_set(bev2, "pT%d" % (dt % 2), "v", E.milestone("vector", "v"))
            E.wait("tensor", "v", E.c["v"].val)
            umv = E.c["v"].val
            E.on("gpsimd", lambda e, umv=umv: e.wait_ge(E.c["v"].sem, umv))
            E.dma("gpsimd", "umw", lambda e, tt=tt: e.dma_start(
                out=UMts.rearrange("p (n m) -> p n m", n=ND)[:, :, tt * 128:(tt + 1) * 128],
                in_=ustg[:].rearrange("p (n m) -> p n m", n=ND)))
            bt_wait(E, bev2, "pColD")
            def mmg(e, tt=tt):
                e.matmul(pCol[:, 0:4], onesrB[:], bgrow[0:1, :], start=True,
                         stop=False)
                for kt in range(ND):
                    e.matmul(pCol[:, 0:4],
                             ustg[:, kt * 128:(kt + 1) * 128],
                             wgsb[:, kt * 4:(kt + 1) * 4],
                             start=False, stop=(kt == ND - 1))
                e.matmul(pD[:, 0:8], onesrB[:], berow[0:1, :], start=True,
                         stop=False)
                for kt in range(ND):
                    e.matmul(pD[:, 0:8],
                             ustg[:, kt * 128:(kt + 1) * 128],
                             wesb[:, kt * 8:(kt + 1) * 8],
                             start=False, stop=(kt == ND - 1))
            E.on("tensor", mmg)
            tg = E.milestone("tensor", "t")
            E.wait("vector", "t", tg)
            def vgate(e, tt=tt):
                lg = sqd[:, 0:4]; ep = sqd[:, 8:16]
                gmax = sqd[:, 16:17]; sg = sqd[:, 17:18]
                emax = sqd[:, 20:24]; esum = sqd[:, 24:28]
                epsel = sqd[:, 28:30]; epm = sqd[:, 30:31]
                ohg = sqd[:, 32:36]; ohe = sqd[:, 36:38]
                gpep = sqd[:, 38:39]
                e.tensor_copy(lg, pCol[:, 0:4])
                e.tensor_copy(ep, pD[:, 0:8])
                e.reduce_max(gmax, lg, axis=AX.X)
                e.tensor_scalar(ohg, lg, gmax, None, ALU.is_ge)
                e.tensor_scalar(lg, lg, gmax, None, ALU.subtract)
            E.on("vector", vgate)
            vg1 = E.milestone("vector", "v")
            bt_set(bev2, "pColD", "v", vg1)
            E.wait("scalar", "v", vg1)
            E.on("scalar", lambda e: e.activation(sqd[:, 0:4], sqd[:, 0:4], AF.Exp))
            sg1 = E.milestone("scalar", "s")
            E.wait("vector", "s", sg1)
            def vgate2(e, tt=tt):
                lg = sqd[:, 0:4]; ep = sqd[:, 8:16]
                sg = sqd[:, 17:18]
                emax = sqd[:, 20:24]
                e.reduce_sum(sg, lg, axis=AX.X)
                e.reciprocal(sg, sg)
                e.tensor_tensor(emax, ep[:, 0:8:2], ep[:, 1:8:2], ALU.max)
                e.tensor_tensor(ep[:, 0:8:2], ep[:, 0:8:2], emax, ALU.subtract)
                e.tensor_tensor(ep[:, 1:8:2], ep[:, 1:8:2], emax, ALU.subtract)
            E.on("vector", vgate2)
            vg2 = E.milestone("vector", "v")
            E.wait("scalar", "v", vg2)
            E.on("scalar", lambda e: e.activation(sqd[:, 8:16], sqd[:, 8:16], AF.Exp))
            sg2 = E.milestone("scalar", "s")
            E.wait("vector", "s", sg2)
            def vgate3(e, tt=tt):
                lg = sqd[:, 0:4]; ep = sqd[:, 8:16]
                sg = sqd[:, 17:18]
                esum = sqd[:, 24:28]
                epsel = sqd[:, 28:30]; epm = sqd[:, 30:31]
                ohg = sqd[:, 32:36]; ohe = sqd[:, 36:38]
                gpep = sqd[:, 38:39]
                tmp2 = sqd[:, 40:42]
                e.tensor_add(esum, ep[:, 0:8:2], ep[:, 1:8:2])
                e.reciprocal(esum, esum)
                e.tensor_mul(ep[:, 0:8:2], ep[:, 0:8:2], esum)
                e.tensor_mul(ep[:, 1:8:2], ep[:, 1:8:2], esum)
                e.memset(epsel, 0.0)
                for g in range(4):
                    e.tensor_scalar(tmp2, ep[:, 2 * g:2 * g + 2],
                                    ohg[:, g:g + 1], None, ALU.mult)
                    e.tensor_add(epsel, epsel, tmp2)
                e.reduce_max(epm, epsel, axis=AX.X)
                e.tensor_scalar(ohe, epsel, epm, None, ALU.is_ge)
                e.tensor_mul(gpep, sg, epm)
                for g in range(4):
                    for j in range(2):
                        col = combine[:, tt * NE0 + 2 * g + j:tt * NE0 + 2 * g + j + 1]
                        e.tensor_mul(col, ohg[:, g:g + 1], ohe[:, j:j + 1])
                        e.tensor_mul(col, col, gpep)
            E.on("vector", vgate3)
            res_ms[tt] = E.milestone("vector", "v")

        emit_barrier(E)
        phB2.close()
        w1b = sb("w1b", [128, 2 * ND * 1024], BF16)
        w2b = sb("w2b", [128, 2 * 8 * D], BF16)
        hT = sb("hT", [128, 8 * 1024], BF16)
        yacc = sb("yacc", [128, 8 * D], F32)
        b2row = sb("b2row_s", [1, 2 * D], BF16)
        UMg = sb("UMg", [128, ND * 1024], BF16)

        fin_ms = {}
        bevm = {}
        w1_t = {}
        w2_t = {}
        for grp in range(4):
            E.on("vector", lambda e: e.memset(yacc[:, :], 0.0))
            ya0 = E.milestone("vector", "v")
            E.on("gpsimd", lambda e, v=E.c["t"].val: e.wait_ge(E.c["t"].sem, v))
            lum = E.dma("gpsimd", "uml", lambda e, grp=grp: e.dma_start(
                out=UMg[:].rearrange("p (n m) -> p n m", n=ND),
                in_=UMts.rearrange("p (n m) -> p n m", n=ND)[
                    :, :, grp * 1024:(grp + 1) * 1024]))
            E.wait("tensor", "uml", lum)
            for ei in range(NE0):
                for fh in range(2):
                    slab = ei * 2 + fh
                    ws = slab % 2
                    w1grp = ("wm1a", "wm1b")[ws]
                    w2grp = ("wm2a", "wm2b")[ws]
                    bgrp2 = ("wb2a", "wb2b")[ws]
                    if ws in w1_t:
                        E.on("gpsimd", lambda e, v=w1_t[ws]:
                             e.wait_ge(E.c["t"].sem, v))
                    ldw = E.dma("gpsimd", w1grp, lambda e, ei=ei, fh=fh, ws=ws:
                                e.dma_start(
                        out=w1b[:, ws * ND * 1024:(ws + 1) * ND * 1024].rearrange(
                            "p (n m) -> p n m", n=ND),
                        in_=w1_h[ei].rearrange("(n p) m -> p n m", p=128)[
                            :, :, fh * 1024:(fh + 1) * 1024]))
                    E.wait("tensor", w1grp, ldw)
                    for ft8 in range(8):
                        for ch in range(2):
                            ps = (pA, pB)[ch]
                            bt_wait(E, bevm, ("pA", "pB")[ch])
                            def mmh(e, ft8=ft8, ch=ch, ps=ps, ws=ws):
                                for kt in range(ND):
                                    e.matmul(
                                        ps[:, :],
                                        w1b[:, ws * ND * 1024 + kt * 1024 + ft8 * 128:
                                           ws * ND * 1024 + kt * 1024 + (ft8 + 1) * 128],
                                        UMg[:, kt * 1024 + ch * 512:kt * 1024 + (ch + 1) * 512],
                                        start=(kt == 0), stop=(kt == ND - 1))
                            E.on("tensor", mmh)
                            tms = E.milestone("tensor", "t")
                            E.wait("scalar", "t", tms)
                            def evh(e, ei=ei, fh=fh, ft8=ft8, ch=ch, ps=ps):
                                bc = b1cols[:, ei * 16 + fh * 8 + ft8:
                                            ei * 16 + fh * 8 + ft8 + 1]
                                e.activation(
                                    hT[:, ft8 * 1024 + ch * 512:ft8 * 1024 + (ch + 1) * 512],
                                    ps[:, :], AF.Relu, bias=bc)
                            E.on("scalar", evh)
                            bt_set(bevm, ("pA", "pB")[ch], "s",
                                   E.milestone("scalar", "s"))
                    w1_t[ws] = E.c["t"].val
                    if ws in w2_t:
                        E.on("gpsimd", lambda e, v=w2_t[ws]:
                             e.wait_ge(E.c["t"].sem, v))
                    ldw2 = E.dma("gpsimd", w2grp, lambda e, ei=ei, fh=fh, ws=ws:
                                 e.dma_start(
                        out=w2b[:, ws * 8 * D:(ws + 1) * 8 * D].rearrange(
                            "p (n m) -> p n m", n=8),
                        in_=w2_h[ei].rearrange("(n p) m -> p n m", p=128)[
                            :, fh * 8:(fh + 1) * 8, :]))
                    ldb2 = E.dma("gpsimd", bgrp2, lambda e, ei=ei, ws=ws:
                                 e.dma_start(
                        out=b2row[0:1, ws * D:(ws + 1) * D],
                        in_=b2r_h[0:1, ei * D:(ei + 1) * D]))
                    E.wait("tensor", w2grp, ldw2)
                    E.wait("tensor", bgrp2, ldb2)
                    E.wait("tensor", "s", E.c["s"].val)
                    for tt8 in range(8):
                        for ch in range(2):
                            ci = tt8 * 2 + ch
                            ps = (pC, pD, pA, pB)[ci % 4]
                            bk = ("pC", "pD", "pA", "pB")[ci % 4]
                            bt_wait(E, bevm, bk)
                            def mmy(e, ei=ei, fh=fh, tt8=tt8, ch=ch, ps=ps, ws=ws):
                                if fh == 0:
                                    e.matmul(ps[:, 0:512], onesrB[:],
                                             b2row[0:1, ws * D + ch * 512:ws * D + (ch + 1) * 512],
                                             start=True, stop=False)
                                for kt in range(8):
                                    e.matmul(ps[:, 0:512],
                                             hT[:, kt * 1024 + tt8 * 128:kt * 1024 + (tt8 + 1) * 128],
                                             w2b[:, ws * 8 * D + kt * D + ch * 512:
                                                ws * 8 * D + kt * D + (ch + 1) * 512],
                                             start=(fh == 1 and kt == 0),
                                             stop=(kt == 7))
                            E.on("tensor", mmy)
                            tms = E.milestone("tensor", "t")
                            E.wait("vector", "t", tms)
                            def evy(e, grp=grp, ei=ei, tt8=tt8, ch=ch, ps=ps):
                                col = combine[:, (grp * 8 + tt8) * NE0 + ei:
                                              (grp * 8 + tt8) * NE0 + ei + 1]
                                e.scalar_tensor_tensor(
                                    yacc[:, tt8 * D + ch * 512:tt8 * D + (ch + 1) * 512],
                                    ps[:, 0:512], col,
                                    yacc[:, tt8 * D + ch * 512:tt8 * D + (ch + 1) * 512],
                                    ALU.mult, ALU.add)
                            E.on("vector", evy)
                            bt_set(bevm, bk, "v",
                                   E.milestone("vector", "v"))
                    w2_t[ws] = E.c["t"].val
            for tt8 in range(8):
                tt = grp * 8 + tt8
                bo = (tt % 2) * D
                grpn = ("xA", "xB")[tt % 2]
                E.wait("sync", "us", stu)
                if tt >= 2 and (tt - 2) in fin_ms:
                    E.wait("sync", "v", fin_ms[tt - 2])
                else:
                    E.wait("sync", "v", E.c["v"].val)
                ldu = E.dma("sync", grpn, lambda e, tt=tt, bo=bo: e.dma_start(
                    out=xin[:, bo:bo + D], in_=us[tt * 128:(tt + 1) * 128, :]))
                E.wait("vector", grpn, ldu)
                if "out" in E.c:
                    E.wait("vector", "out", E.c["out"].val)
                def vr2(e, tt8=tt8, bo=bo):
                    e.tensor_add(acc[:, 0:D], xin[:, bo:bo + D],
                                 yacc[:, tt8 * D:(tt8 + 1) * D])
                E.on("vector", vr2)
                vr2m = E.milestone("vector", "v")
                emit_ln_tile(E, acc, m1, m2, sqd, epscol, gcast2, bcast2)
                E.wait("sync", "v", vr2m)
                ldx2 = E.dma("sync", grpn, lambda e, tt=tt, bo=bo: e.dma_start(
                    out=xin[:, bo:bo + D], in_=x1_h[tt * 128:(tt + 1) * 128, :]))
                E.wait("vector", grpn, ldx2)
                E.on("vector", lambda e, bo=bo: e.tensor_add(
                    acc[:, 0:D], acc[:, 0:D], xin[:, bo:bo + D]))
                E.milestone("vector", "v")
                fin = emit_ln_tile(E, acc, m1, m2, sqd, epscol)
                fin_ms[tt] = fin
                E.wait("sync", "v", fin)
                E.dma("sync", "out", lambda e, tt=tt: e.dma_start(
                    out=out_h[tt * 128:(tt + 1) * 128, :], in_=acc[:, 0:D]))
        E.wait("sync", "out", E.c["out"].val)
        if "db" in E.c:
            E.on("gpsimd", lambda e, v=E.c["db"].val, s=E.c["db"].sem: e.wait_ge(s, v))
        if not F:
            p.play(ctx.enter_context(nc.Block()))
    return nc





def _consts_stage3():
    T, HDm = S, 64
    half = np.arange(0, HDm, 2, dtype=np.float64)
    inv_freq = 1.0 / 10000.0 ** (half / HDm)
    t = np.arange(T, dtype=np.float64)
    freqs = np.repeat(t[:, None] * inv_freq[None, :], 2, axis=-1)
    scale_vec = (half + 0.4 * HDm) / (1.4 * HDm)
    power = (t - T // 2) / 512.0
    scale = np.repeat(scale_vec[None, :] ** power[:, None], 2, axis=-1)
    cos, sin = np.cos(freqs), np.sin(freqs)
    gamma = 1.0 - 2.0 ** (-5.0 - np.arange(16, dtype=np.float64))
    jj = np.arange(128, dtype=np.float64)
    vbar = gamma[None, :] ** (-jj[:, None])
    endsc = (gamma[None, :] ** jj[:, None]) * (HDm ** -0.5)
    E16 = np.zeros((16, D))
    for h in range(16):
        E16[h, h * 64:(h + 1) * 64] = 1.0
    return {
        "cCS": (cos * scale).astype(np.float32),
        "cSS": (sin * scale).astype(np.float32),
        "cCI": (cos / scale).astype(np.float32),
        "cSI": (sin / scale).astype(np.float32),
        "cVbar": vbar.astype(np.float32),
        "cEnd": endsc.astype(np.float32),
        "cE16": E16.astype(np.float32),
    }


_GAMMA128 = [float((1.0 - 2.0 ** (-5.0 - h)) ** 128) for h in range(16)]


def build_stage3(F=None):
    nc = F["nc"] if F else bass.Bass("TRN2", target_bir_lowering=False)
    x2_h = F["x2"] if F else nc.dram_tensor("x2", [S, D], F32, kind="ExternalInput").ap()
    wqkv_h = F["wqkv"] if F else nc.dram_tensor("wqkv", [3 * D, D], F32, kind="ExternalInput").ap()
    wg_h = F["wgated"] if F else nc.dram_tensor("wgated", [D, D], F32, kind="ExternalInput").ap()
    wp_h = F["wproj"] if F else nc.dram_tensor("wproj", [D, D], F32, kind="ExternalInput").ap()
    gnw_h = F["gnw"] if F else nc.dram_tensor("gnw", [1, D], F32, kind="ExternalInput").ap()
    gnb_h = F["gnb"] if F else nc.dram_tensor("gnb", [1, D], F32, kind="ExternalInput").ap()
    cCS_h = F["cCS"] if F else nc.dram_tensor("cCS", [S, 64], F32, kind="ExternalInput").ap()
    cSS_h = F["cSS"] if F else nc.dram_tensor("cSS", [S, 64], F32, kind="ExternalInput").ap()
    cCI_h = F["cCI"] if F else nc.dram_tensor("cCI", [S, 64], F32, kind="ExternalInput").ap()
    cSI_h = F["cSI"] if F else nc.dram_tensor("cSI", [S, 64], F32, kind="ExternalInput").ap()
    cVb_h = F["cVbar"] if F else nc.dram_tensor("cVbar", [128, 16], F32, kind="ExternalInput").ap()
    cEn_h = F["cEnd"] if F else nc.dram_tensor("cEnd", [128, 16], F32, kind="ExternalInput").ap()
    cE16_h = F["cE16"] if F else nc.dram_tensor("cE16", [16, D], F32, kind="ExternalInput").ap()
    cLTT_h = F["cLTT"] if F else nc.dram_tensor("cLTT", [128, 128], F32, kind="ExternalInput").ap()
    cId_h = F["cIdent"] if F else nc.dram_tensor("cIdent", [128, 128], F32, kind="ExternalInput").ap()
    cOR_h = F["cOnesRow"] if F else nc.dram_tensor("cOnesRow", [1, 128], F32, kind="ExternalInput").ap()
    out_h = F["out"] if F else nc.dram_tensor("out", [S, D], F32, kind="ExternalOutput").ap()
    ybs = nc.dram_tensor("ybs", [NPAIR, 128, S], BF16, kind="Internal").ap()

    NCH = S // 128
    NSTAT = 64 * S

    with ExitStack() as ctx:
        def sb(name, shape, dtype, c=None):
            return (c or ctx).enter_context(nc.sbuf_tensor("c" + name, shape, dtype))
        def psum(name, shape, dtype):
            return ctx.enter_context(nc.psum_tensor("c" + name, shape, dtype))
        def sem(name):
            return ctx.enter_context(nc.semaphore("c" + name))

        ident = sb("identB", [128, 128], BF16)
        identF = sb("identF3", [128, 128], F32)
        onesrF = sb("onesrF3", [1, 128], F32)
        epscol = sb("epscol3", [128, 1], F32)
        ropeCS = sb("ropeCS", [128, NCH * 64], BF16)
        ropeSS = sb("ropeSS", [128, NCH * 64], BF16)
        ropeCI = sb("ropeCI", [128, NCH * 64], BF16)
        ropeSI = sb("ropeSI", [128, NCH * 64], BF16)
        vbarC = sb("vbarC", [128, 16], F32)
        endC = sb("endC", [128, 16], F32)
        e16b = sb("e16b", [16, D], F32)
        Ypair = sb("Ypair", [128, NCH * 128], BF16)
        ytb = sb("ytb", [128, D], BF16)
        statsP = sb("statsP", [128, 2 * 16], F32)
        m1 = sb("m13", [128, 1], F32)
        m2 = sb("m23", [128, 1], F32)
        acc = sb("acc3", [128, D], F32)
        sqd = sb("sqd3", [128, D], F32)
        xin = sb("xin3", [128, 2 * D], F32)
        xinb = sb("xinb3", [128, 2 * D], BF16)
        qTp = sb("qTp", [128, S], BF16)
        kTp = sb("kTp", [128, S], BF16)
        kp = sb("kp", [128, NCH * 128], BF16)
        vp = sb("vp", [128, NCH * 128], BF16)
        vbarT = sb("vbarT", [128, 256], BF16)
        sbT = sb("sbT", [128, 1024], BF16)
        TmA = sb("TmA", [128, 64], F32)
        TmB = sb("TmB", [128, 64], BF16)
        qk3 = sb("qk3", [128, 384], F32)
        rp1 = sb("rp1", [128, 64], F32)
        rp2 = sb("rp2", [128, 64], F32)
        wqkvTp = sb("wqkvTp", [128, ND * 384], BF16)
        wrow = sb("wrow", [128, 3 * D], BF16)
        maskT = sb("maskT", [128, 128], F32)
        ys = sb("ys", [128, 64], F32)
        msum3 = sb("msum3", [128, 2], F32)
        msq3 = sb("msq3", [128, 2], F32)
        stat16 = sb("stat16", [16, 4], F32)
        rowsA = sb("rowsA", [1, 4 * D], F32)
        scaleB = sb("scaleB", [128, D], F32)
        biasB = sb("biasB", [128, D], F32)

        phA = ExitStack()
        xT = sb("xT3", [128, ND * S], BF16, phA)

        pA = psum("pA3", [128, 512], F32)
        pB = psum("pB3", [128, 512], F32)
        pC = psum("pC3", [128, 512], F32)
        pD = psum("pD3", [128, 512], F32)
        pT1 = psum("pT13", [128, 128], BF16)
        pT2 = psum("pT23", [128, 128], BF16)
        pY = psum("pY3", [128, 64], F32)
        pSm = psum("pSm3", [128, 128], F32)

        if F:
            E = F["E"]; p = F["p"]
        else:
            s_t0 = sem("s_t3"); s_v0 = sem("s_v3"); s_s0 = sem("s_s3"); s_g0 = sem("s_g3")
            p = Prog()
            E = Emit(p, {"t": s_t0, "v": s_v0, "s": s_s0, "g": s_g0}, nc=nc, ctx=ctx)
        s_t = E.c["t"].sem; s_v = E.c["v"].sem; s_s = E.c["s"].sem; s_g = E.c["g"].sem

        E.dma("gpsimd", "gC", lambda e: e.dma_start(out=ident[:], in_=cId_h[:]))
        E.dma("gpsimd", "gC", lambda e: e.dma_start(out=onesrF[:], in_=cOR_h[:]))
        E.dma("sync", "sC", lambda e: e.dma_start(out=identF[:], in_=cId_h[:]))
        E.dma("sync", "sC", lambda e: e.dma_start(out=maskT[:], in_=cLTT_h[:]))
        for dst, src_h in ((ropeCS, cCS_h), (ropeSS, cSS_h), (ropeCI, cCI_h),
                           (ropeSI, cSI_h)):
            E.dma("gpsimd", "gC", lambda e, dst=dst, src_h=src_h: e.dma_start(
                out=dst[:].rearrange("p (n m) -> p n m", n=NCH),
                in_=src_h.rearrange("(n p) m -> p n m", p=128)))
        E.dma("gpsimd", "gC", lambda e: e.dma_start(out=vbarC[:], in_=cVb_h[:]))
        E.dma("gpsimd", "gC", lambda e: e.dma_start(out=endC[:], in_=cEn_h[:]))
        E.dma("gpsimd", "gC", lambda e: e.dma_start(out=e16b[:], in_=cE16_h[:]))
        E.dma("sync", "sC", lambda e: e.dma_start(out=rowsA[0:1, 2 * D:3 * D], in_=gnw_h[:]))
        E.dma("sync", "sC", lambda e: e.dma_start(out=rowsA[0:1, 3 * D:4 * D], in_=gnb_h[:]))
        ld_g = E.c["gC"].val
        ld_s = E.c["sC"].val
        def vinit(e):
            e.memset(epscol[:], LN_EPS)
            e.memset(statsP[:, :], 0.0)
        E.on("vector", vinit)
        E.wait("vector", "gC", ld_g)
        E.wait("vector", "sC", ld_s)
        E.milestone("vector", "v")
        E.wait("tensor", "gC", ld_g)
        E.wait("tensor", "sC", ld_s)

        emit_x_to_xT(E, x2_h, NT, xT, xin, xinb, ident, pT1, pT2, S)
        xt_t = E.c["t"].val
        E.wait("vector", "t", xt_t)

        bev3 = {}
        for pr in range(NPAIR):
            ldw = E.dma("gpsimd", "wT", lambda e, pr=pr: e.dma_start(
                out=wrow[:].rearrange("p (n m) -> p n m", n=3),
                in_=wqkv_h.rearrange("(n p) m -> p n m", p=128)[:, pr:pr + 17:8, :]))
            E.wait("tensor", "wT", ldw)
            for wj in range(3):
                for dt in range(ND):
                    pt = (pT1, pT2)[dt % 2]
                    bt_wait(E, bev3, "pT%d" % (dt % 2))
                    def trw(e, wj=wj, dt=dt, pt=pt):
                        e.transpose(pt[:], wrow[:, wj * D + dt * 128:wj * D + (dt + 1) * 128],
                                    ident[:])
                    E.on("tensor", trw)
                    tms = E.milestone("tensor", "t")
                    E.wait("vector", "t", tms)
                    def evw(e, wj=wj, dt=dt, pt=pt):
                        e.tensor_copy(
                            wqkvTp[:, dt * 384 + wj * 128:dt * 384 + (wj + 1) * 128],
                            pt[:])
                    E.on("vector", evw)
                    bt_set(bev3, "pT%d" % (dt % 2), "v",
                           E.milestone("vector", "v"))
            E.wait("tensor", "v", E.c["v"].val)
            trqk_t = {}
            for ch in range(NCH):
                qps = (pA, pB)[ch % 2]
                bt_wait(E, bev3, ("pA", "pB")[ch % 2])
                def mmqkv(e, ch=ch, qps=qps):
                    for kt in range(ND):
                        e.matmul(qps[:, 0:384],
                                 xT[:, kt * S + ch * 128:kt * S + (ch + 1) * 128],
                                 wqkvTp[:, kt * 384:(kt + 1) * 384],
                                 start=(kt == 0), stop=(kt == ND - 1))
                E.on("tensor", mmqkv)
                tq = E.milestone("tensor", "t")
                E.wait("vector", "t", tq)
                def vevq(e, qps=qps):
                    e.tensor_copy(qk3[:, :], qps[:, 0:384])
                E.on("vector", vevq)
                bt_set(bev3, ("pA", "pB")[ch % 2], "v",
                       E.milestone("vector", "v"))
                def vrope(e, ch=ch):
                    for sub in range(2):
                        q0 = sub * 64
                        e.tensor_scalar_mul(rp1[:, 0:64:2], qk3[:, q0 + 1:q0 + 64:2], -1.0)
                        e.tensor_copy(rp1[:, 1:64:2], qk3[:, q0:q0 + 64:2])
                        e.tensor_scalar_mul(rp2[:, 0:64:2], qk3[:, 128 + q0 + 1:128 + q0 + 64:2], -1.0)
                        e.tensor_copy(rp2[:, 1:64:2], qk3[:, 128 + q0:128 + q0 + 64:2])
                        e.tensor_mul(qk3[:, q0:q0 + 64], qk3[:, q0:q0 + 64],
                                     ropeCS[:, ch * 64:(ch + 1) * 64])
                        e.tensor_mul(rp1[:, 0:64], rp1[:, 0:64],
                                     ropeSS[:, ch * 64:(ch + 1) * 64])
                        e.tensor_add(qk3[:, q0:q0 + 64], qk3[:, q0:q0 + 64], rp1[:, 0:64])
                        e.tensor_mul(qk3[:, 128 + q0:128 + q0 + 64],
                                     qk3[:, 128 + q0:128 + q0 + 64],
                                     ropeCI[:, ch * 64:(ch + 1) * 64])
                        e.tensor_mul(rp2[:, 0:64], rp2[:, 0:64],
                                     ropeSI[:, ch * 64:(ch + 1) * 64])
                        e.tensor_add(qk3[:, 128 + q0:128 + q0 + 64],
                                     qk3[:, 128 + q0:128 + q0 + 64], rp2[:, 0:64])
                    e.tensor_copy(kp[:, ch * 128:(ch + 1) * 128], qk3[:, 128:256])
                    e.tensor_copy(vp[:, ch * 128:(ch + 1) * 128], qk3[:, 256:384])
                    xo = (ch % 2) * 256
                    e.tensor_copy(xinb[:, xo:xo + 128], qk3[:, 0:128])
                    e.tensor_copy(xinb[:, xo + 128:xo + 256], qk3[:, 128:256])
                if ch >= 2:
                    E.wait("vector", "t", trqk_t[ch - 2])
                E.on("vector", vrope)
                vr = E.milestone("vector", "v")
                E.wait("tensor", "v", vr)
                xo = (ch % 2) * 256
                for which in range(2):
                    pt = (pT1, pT2)[which]
                    bt_wait(E, bev3, "pT%d" % which)
                    def trqk(e, which=which, pt=pt, xo=xo):
                        e.transpose(pt[:], xinb[:, xo + which * 128:xo + (which + 1) * 128],
                                    ident[:])
                    E.on("tensor", trqk)
                    tms = E.milestone("tensor", "t")
                    E.wait("vector", "t", tms)
                    def evqk(e, ch=ch, which=which, pt=pt):
                        dst = (qTp, kTp)[which]
                        e.tensor_copy(dst[:, ch * 128:(ch + 1) * 128], pt[:])
                    E.on("vector", evqk)
                    bt_set(bev3, "pT%d" % which, "v", E.milestone("vector", "v"))
                trqk_t[ch] = E.c["t"].val
            E.wait("tensor", "v", E.c["v"].val)

            def tminit(e):
                e.memset(TmA[:, :], 0.0)
                e.memset(TmB[:, :], 0.0)
            E.on("vector", tminit)
            tm0 = E.milestone("vector", "v")
            E.wait("tensor", "v", tm0)
            E.wait("scalar", "v", tm0)
            vvb_ms = {}
            evs_ms = {}
            evA_ms = {}
            mma_t = {}
            mmy_t = {}
            stat_ms = {}
            for m in range(NCH):
                for sub in range(2):
                    h = pr * 2 + sub
                    vsl = (m % 2) * 128 + sub * 64
                    ssl = (m % 2) * 512 + sub * 128
                    sck = ("pSm", "pB")[sub]
                    yk = ("pY", "pD")[sub]
                    ak = ("pC", "pA")[sub]
                    psc = (pSm, pB)[sub]
                    py_ = (pY, pD)[sub]
                    pa_ = (pC, pA)[sub]
                    if (m - 2, sub) in mma_t:
                        E.wait("scalar", "t", mma_t[(m - 2, sub)])
                    def vvb(e, m=m, sub=sub, h=h, vsl=vsl):
                        e.activation(vbarT[:, vsl:vsl + 64],
                                     vp[:, m * 128 + sub * 64:m * 128 + sub * 64 + 64],
                                     AF.Copy, scale=vbarC[:, h:h + 1])
                    E.on("scalar", vvb)
                    vvb_ms[(m, sub)] = E.milestone("scalar", "s")
                    bt_wait(E, bev3, sck)
                    def mms(e, m=m, sub=sub, psc=psc):
                        e.matmul(psc[:, 0:128],
                                 kTp[sub * 64:(sub + 1) * 64, m * 128:(m + 1) * 128],
                                 qTp[sub * 64:(sub + 1) * 64, m * 128:(m + 1) * 128],
                                 start=True, stop=True)
                    E.on("tensor", mms)
                    tsb = E.milestone("tensor", "t")
                    E.wait("vector", "t", tsb)
                    if (m - 2, sub) in mmy_t:
                        E.wait("vector", "t", mmy_t[(m - 2, sub)])
                    def evs(e, ssl=ssl, psc=psc):
                        e.scalar_tensor_tensor(sbT[:, ssl:ssl + 128], psc[:, 0:128],
                                               1.0, maskT[:, :], ALU.mult, ALU.mult)
                    E.on("vector", evs)
                    evs_ms[(m, sub)] = E.milestone("vector", "v")
                    bt_set(bev3, sck, "v", evs_ms[(m, sub)])
                    bt_wait(E, bev3, yk)
                    E.wait("tensor", "v", evs_ms[(m, sub)])
                    E.wait("tensor", "s", vvb_ms[(m, sub)])
                    if (m - 1, sub) in evA_ms:
                        E.wait("tensor", "v", evA_ms[(m - 1, sub)])
                    def mmy3(e, m=m, sub=sub, vsl=vsl, ssl=ssl, py_=py_):
                        e.matmul(py_[:, 0:64], sbT[:, ssl:ssl + 128],
                                 vbarT[:, vsl:vsl + 64],
                                 start=True, stop=(m == 0))
                        if m > 0:
                            e.matmul(py_[:, 0:64],
                                     qTp[sub * 64:(sub + 1) * 64, m * 128:(m + 1) * 128],
                                     TmB[sub * 64:(sub + 1) * 64, :],
                                     start=False, stop=True)
                    E.on("tensor", mmy3)
                    tym = E.milestone("tensor", "t")
                    mmy_t[(m, sub)] = tym
                    E.wait("scalar", "t", tym)
                    if (m - 1, sub) in stat_ms:
                        E.wait("scalar", "v", stat_ms[(m - 1, sub)])
                    def evy3a(e, m=m, sub=sub, h=h, py_=py_):
                        e.activation(
                            Ypair[:, m * 128 + sub * 64:m * 128 + (sub + 1) * 64],
                            py_[:, 0:64], AF.Copy, scale=endC[:, h:h + 1],
                            accum_out=msum3[:, sub:sub + 1])
                    E.on("scalar", evy3a)
                    bt_set(bev3, yk, "s", E.milestone("scalar", "s"))
                    def evy3b(e, m=m, sub=sub):
                        e.activation(
                            sqd[:, sub * 64:(sub + 1) * 64],
                            Ypair[:, m * 128 + sub * 64:m * 128 + (sub + 1) * 64],
                            AF.Square, accum_out=msq3[:, sub:sub + 1])
                    E.on("scalar", evy3b)
                    sqm = E.milestone("scalar", "s")
                    E.wait("vector", "s", sqm)
                    def vstat(e, sub=sub, h=h):
                        e.raw.tensor_add(statsP[:, h:h + 1], statsP[:, h:h + 1],
                                         msum3[:, sub:sub + 1])
                        e.tensor_add(statsP[:, 16 + h:16 + h + 1],
                                     statsP[:, 16 + h:16 + h + 1],
                                     msq3[:, sub:sub + 1])
                    E.on("vector", vstat)
                    stat_ms[(m, sub)] = E.milestone("vector", "v")
                    bt_wait(E, bev3, ak)
                    def mma(e, m=m, sub=sub, vsl=vsl, pa_=pa_):
                        e.matmul(pa_[sub * 64:(sub + 1) * 64, 0:64],
                                 kp[:, m * 128 + sub * 64:m * 128 + sub * 64 + 64],
                                 vbarT[:, vsl:vsl + 64], start=True, stop=True,
                                 tile_position=(0, sub * 64))
                    E.on("tensor", mma)
                    tam = E.milestone("tensor", "t")
                    mma_t[(m, sub)] = tam
                    E.wait("vector", "t", tam)
                    def evA(e, sub=sub, h=h, pa_=pa_):
                        sl = slice(sub * 64, (sub + 1) * 64)
                        e.tensor_add(TmA[sl, :], TmA[sl, :], pa_[sl, 0:64])
                        e.tensor_scalar_mul(TmA[sl, :], TmA[sl, :], _GAMMA128[h])
                        e.tensor_copy(TmB[sl, :], TmA[sl, :])
                    E.on("vector", evA)
                    evA_ms[(m, sub)] = E.milestone("vector", "v")
                    bt_set(bev3, ak, "v", evA_ms[(m, sub)])
            ypm = E.c["v"].val
            yps = E.c["s"].val
            E.on("gpsimd", lambda e, ypm=ypm: e.wait_ge(E.c["v"].sem, ypm))
            E.on("gpsimd", lambda e, yps=yps: e.wait_ge(E.c["s"].sem, yps))
            ybst = E.dma("gpsimd", "yb", lambda e, pr=pr: e.dma_start(
                out=ybs[pr], in_=Ypair[:, :]))
            E.wait("scalar", "yb", ybst)
            E.wait("vector", "yb", ybst)

        for k in ("pSm", "pA", "pB", "pC", "pD", "pY"):
            bt_wait(E, bev3, k)
        E.wait("tensor", "v", E.c["v"].val)
        for half in range(2):
            pt = (pSm, pSm)[0]
            def trst(e, half=half):
                e.transpose(pSm[0:16, :], statsP[:, half * 16:(half + 1) * 16],
                            identF[:])
            E.on("tensor", trst)
            tms = E.milestone("tensor", "t")
            E.wait("vector", "t", tms)
            def evst(e, half=half):
                e.reduce_sum(stat16[:, half:half + 1], pSm[0:16, :], axis=AX.X)
            E.on("vector", evst)
            vms = E.milestone("vector", "v")
            E.wait("tensor", "v", vms)
        def vstats(e):
            e.tensor_scalar_mul(stat16[:, 2:3], stat16[:, 0:1], 1.0 / NSTAT)
            e.tensor_scalar_mul(stat16[:, 1:2], stat16[:, 1:2], 1.0 / NSTAT)
            e.tensor_mul(stat16[:, 3:4], stat16[:, 2:3], stat16[:, 2:3])
            e.tensor_sub(stat16[:, 3:4], stat16[:, 1:2], stat16[:, 3:4])
            e.tensor_scalar_add(stat16[:, 3:4], stat16[:, 3:4], LN_EPS)
        E.on("vector", vstats)
        vsm = E.milestone("vector", "v")
        E.wait("scalar", "v", vsm)
        E.on("scalar", lambda e: e.activation(stat16[:, 3:4], stat16[:, 3:4], AF.Sqrt))
        ssm = E.milestone("scalar", "s")
        E.wait("vector", "s", ssm)
        E.on("vector", lambda e: e.reciprocal(stat16[:, 3:4], stat16[:, 3:4]))
        vrm = E.milestone("vector", "v")
        E.wait("tensor", "v", vrm)
        for which in range(2):
            col = 2 + which
            for chh in range(2):
                ps2 = (pA, pB)[chh]
                def mme(e, col=col, chh=chh, ps2=ps2):
                    e.matmul(ps2[0:1, :], stat16[0:16, col:col + 1],
                             e16b[:, chh * 512:(chh + 1) * 512],
                             start=True, stop=True)
                E.on("tensor", mme)
                tms = E.milestone("tensor", "t")
                E.wait("vector", "t", tms)
                def eve(e, which=which, chh=chh, ps2=ps2):
                    e.tensor_copy(rowsA[0:1, which * D + chh * 512:which * D + (chh + 1) * 512],
                                  ps2[0:1, :])
                E.on("vector", eve)
                vms = E.milestone("vector", "v")
                E.wait("tensor", "v", vms)
        def vrows(e):
            e.tensor_mul(rowsA[0:1, D:2 * D], rowsA[0:1, D:2 * D],
                         rowsA[0:1, 2 * D:3 * D])
            e.tensor_mul(rowsA[0:1, 0:D], rowsA[0:1, 0:D], rowsA[0:1, D:2 * D])
            e.tensor_sub(rowsA[0:1, 0:D], rowsA[0:1, 3 * D:4 * D], rowsA[0:1, 0:D])
        E.on("vector", vrows)
        vrwm = E.milestone("vector", "v")
        E.wait("tensor", "v", vrwm)
        emit_bcast_row(E, rowsA[0:1, D:2 * D], scaleB, onesrF, pC)
        emit_bcast_row(E, rowsA[0:1, 0:D], biasB, onesrF, pC)

        emit_barrier(E)
        phA.close()
        wgT = sb("wgT", [128, ND * D], BF16)
        wpT = sb("wpT", [128, ND * D], BF16)
        wtmp3 = sb("wtmp3", [128, ND * D], BF16)
        xTt = sb("xTt", [128, ND * 128], BF16)
        gt = sb("gt", [128, D], F32)
        emit_w_transpose(E, wg_h, D, wtmp3, wgT, D, ident, pT1, pT2, half_rows=8)
        emit_w_transpose(E, wp_h, D, wtmp3, wpT, D, ident, pT1, pT2, half_rows=8)

        res_ms = {}
        for tt in range(NT):
            bo = (tt % 2) * D
            grpn = ("xA", "xB")[tt % 2]
            if tt >= 2:
                E.wait("sync", "v", res_ms[tt - 2])
            else:
                E.wait("sync", "t", E.c["t"].val)
            ldx = E.dma("sync", grpn, lambda e, tt=tt, bo=bo: e.dma_start(
                out=xin[:, bo:bo + D], in_=x2_h[tt * 128:(tt + 1) * 128, :]))
            E.wait("vector", grpn, ldx)
            if tt < 2:
                E.wait("vector", "t", E.c["t"].val)
            E.on("vector", lambda e, bo=bo: e.tensor_copy(xinb[:, bo:bo + D],
                                                          xin[:, bo:bo + D]))
            cvm = E.milestone("vector", "v")
            E.wait("tensor", "v", cvm)
            for dt in range(ND):
                pt = (pT1, pT2)[dt % 2]
                bt_wait(E, bev3, "pT%d" % (dt % 2))
                def trx(e, bo=bo, dt=dt, pt=pt):
                    e.transpose(pt[:], xinb[:, bo + dt * 128:bo + (dt + 1) * 128],
                                ident[:])
                E.on("tensor", trx)
                tms = E.milestone("tensor", "t")
                E.wait("vector", "t", tms)
                def evx(e, dt=dt, pt=pt):
                    e.tensor_copy(xTt[:, dt * 128:(dt + 1) * 128], pt[:])
                E.on("vector", evx)
                bt_set(bev3, "pT%d" % (dt % 2), "v", E.milestone("vector", "v"))
            E.on("gpsimd", lambda e, v=E.c["v"].val: e.wait_ge(E.c["v"].sem, v))
            ldy = E.dma("gpsimd", "yt", lambda e, tt=tt: e.dma_start(
                out=ytb[:].rearrange("p (n m) -> p n m", n=NPAIR),
                in_=ybs[:, :, tt * 128:(tt + 1) * 128].rearrange("n p m -> p n m")))
            E.wait("vector", "yt", ldy)
            E.wait("tensor", "v", E.c["v"].val)
            bt_wait(E, bev3, "pA")
            bt_wait(E, bev3, "pB")
            for chh in range(2):
                ps2 = (pA, pB)[chh]
                def mmg3(e, chh=chh, ps2=ps2):
                    for kt in range(ND):
                        e.matmul(ps2[:, :], xTt[:, kt * 128:(kt + 1) * 128],
                                 wgT[:, kt * D + chh * 512:kt * D + (chh + 1) * 512],
                                 start=(kt == 0), stop=(kt == ND - 1))
                E.on("tensor", mmg3)
            tms = E.milestone("tensor", "t")
            E.wait("scalar", "t", tms)
            def evg3(e):
                e.activation(gt[:, 0:512], pA[:, :], AF.Silu)
                e.activation(gt[:, 512:1024], pB[:, :], AF.Silu)
            E.on("scalar", evg3)
            sms = E.milestone("scalar", "s")
            bt_set(bev3, "pA", "s", sms)
            bt_set(bev3, "pB", "s", sms)
            E.wait("vector", "s", sms)
            if "out" in E.c:
                E.wait("vector", "out", E.c["out"].val)
            def vz3(e, tt=tt):
                e.tensor_copy(acc[:, 0:D], ytb[:, 0:D])
                e.tensor_mul(acc[:, 0:D], acc[:, 0:D], scaleB[:, 0:D])
                e.tensor_add(acc[:, 0:D], acc[:, 0:D], biasB[:, 0:D])
                e.tensor_mul(acc[:, 0:D], acc[:, 0:D], gt[:, 0:D])
                e.tensor_copy(xinb[:, 0:D], acc[:, 0:D])
            E.on("vector", vz3)
            vzm = E.milestone("vector", "v")
            E.wait("tensor", "v", vzm)
            for dt in range(ND):
                pt = (pT1, pT2)[dt % 2]
                bt_wait(E, bev3, "pT%d" % (dt % 2))
                def trz(e, dt=dt, pt=pt):
                    e.transpose(pt[:], xinb[:, dt * 128:(dt + 1) * 128], ident[:])
                E.on("tensor", trz)
                tms = E.milestone("tensor", "t")
                E.wait("vector", "t", tms)
                def evz(e, dt=dt, pt=pt):
                    e.tensor_copy(xTt[:, dt * 128:(dt + 1) * 128], pt[:])
                E.on("vector", evz)
                bt_set(bev3, "pT%d" % (dt % 2), "v", E.milestone("vector", "v"))
            E.wait("tensor", "v", E.c["v"].val)
            bt_wait(E, bev3, "pC")
            bt_wait(E, bev3, "pD")
            for chh in range(2):
                ps2 = (pC, pD)[chh]
                def mmp3(e, chh=chh, ps2=ps2):
                    for kt in range(ND):
                        e.matmul(ps2[:, :], xTt[:, kt * 128:(kt + 1) * 128],
                                 wpT[:, kt * D + chh * 512:kt * D + (chh + 1) * 512],
                                 start=(kt == 0), stop=(kt == ND - 1))
                E.on("tensor", mmp3)
            tms = E.milestone("tensor", "t")
            E.wait("vector", "t", tms)
            if "out" in E.c:
                E.wait("vector", "out", E.c["out"].val)
            def vres3(e, bo=bo):
                e.scalar_tensor_tensor(acc[:, 0:512], xin[:, bo:bo + 512], 1.0,
                                       pC[:, :], ALU.mult, ALU.add)
                e.scalar_tensor_tensor(acc[:, 512:1024], xin[:, bo + 512:bo + D],
                                       1.0, pD[:, :], ALU.mult, ALU.add)
            E.on("vector", vres3)
            vrm3 = E.milestone("vector", "v")
            bt_set(bev3, "pC", "v", vrm3)
            bt_set(bev3, "pD", "v", vrm3)
            fin = emit_ln_tile(E, acc, m1, m2, sqd, epscol)
            res_ms[tt] = fin
            E.wait("sync", "v", fin)
            E.dma("sync", "out", lambda e, tt=tt: e.dma_start(
                out=out_h[tt * 128:(tt + 1) * 128, :], in_=acc[:, 0:D]))

        E.wait("sync", "out", E.c["out"].val)
        if not F:
            p.play(ctx.enter_context(nc.Block()))
    return nc





SLW = 4224
PEW = 8192
PEW2 = 6144
NTM4 = MY // 128


def _consts_stage4():
    pos = np.arange(S - 1, -S, -1, dtype=np.float64)
    inv_freq = 1.0 / 10000.0 ** (np.arange(0, D, 2, dtype=np.float64) / D)
    ang = pos[:, None] * inv_freq[None, :]
    pe = np.concatenate([np.sin(ang), np.cos(ang)], axis=-1)
    peT_rev = pe[::-1].T.copy()
    peT_pad = np.zeros((D, PEW), dtype=np.float64)
    peT_pad[:, :2 * S - 1] = peT_rev
    return {0: np.ascontiguousarray(peT_pad[:, 2048:2048 + PEW2]).astype(np.float32),
            1: np.ascontiguousarray(peT_pad[:, 0:PEW2]).astype(np.float32)}


def _consts_sel(half):
    ident = np.eye(128, dtype=np.float32)
    zero = np.zeros((128, 128), dtype=np.float32)
    return (ident, zero) if half == 0 else (zero, ident)


def build_stage4(F=None):
    nc = F["nc"] if F else bass.Bass("TRN2", target_bir_lowering=False)
    x3_h = F["x3"] if F else nc.dram_tensor("x3", [S, D], F32, kind="ExternalInput").ap()
    wq_h = F["wq4"] if F else nc.dram_tensor("wq4", [D, D], F32, kind="ExternalInput").ap()
    wk_h = F["wk4"] if F else nc.dram_tensor("wk4", [D, D], F32, kind="ExternalInput").ap()
    wv_h = F["wv4"] if F else nc.dram_tensor("wv4", [D, D], F32, kind="ExternalInput").ap()
    wp_h = F["wp4"] if F else nc.dram_tensor("wp4", [D, D], F32, kind="ExternalInput").ap()
    wo_h = F["wo4"] if F else nc.dram_tensor("wo4", [D, D], F32, kind="ExternalInput").ap()
    g1_h = F["ln1g4"] if F else nc.dram_tensor("ln1g4", [1, D], F32, kind="ExternalInput").ap()
    b1_h = F["ln1b4"] if F else nc.dram_tensor("ln1b4", [1, D], F32, kind="ExternalInput").ap()
    g2_h = F["ln2g4"] if F else nc.dram_tensor("ln2g4", [1, D], F32, kind="ExternalInput").ap()
    b2_h = F["ln2b4"] if F else nc.dram_tensor("ln2b4", [1, D], F32, kind="ExternalInput").ap()
    selw_h = F["selw"] if F else nc.dram_tensor("selw", [D, 32], F32, kind="ExternalInput").ap()
    keys_h = F["keys"] if F else nc.dram_tensor("keys", [32, D, 128], F32, kind="ExternalInput").ap()
    vals_h = F["vals"] if F else nc.dram_tensor("vals", [32, 128, D], F32, kind="ExternalInput").ap()
    pes_h = F["peslice"] if F else nc.dram_tensor("peslice", [D, PEW2], F32, kind="ExternalInput").ap()
    sel0_h = F["sel0"] if F else nc.dram_tensor("sel0", [128, 128], F32, kind="ExternalInput").ap()
    sel1_h = F["sel1"] if F else nc.dram_tensor("sel1", [128, 128], F32, kind="ExternalInput").ap()
    cId_h = F["cIdent"] if F else nc.dram_tensor("cIdent", [128, 128], F32, kind="ExternalInput").ap()
    cOR_h = F["cOnesRow"] if F else nc.dram_tensor("cOnesRow", [1, 128], F32, kind="ExternalInput").ap()
    out_h = F["out"] if F else nc.dram_tensor("out", [MY, D], F32, kind="ExternalOutput").ap()
    kts = nc.dram_tensor("kts4", [NPAIR, 128, S], BF16, kind="Internal").ap()
    qts = nc.dram_tensor("qts4", [NPAIR, 128, MY], BF16, kind="Internal").ap()
    vts = nc.dram_tensor("vts4", [NPAIR, S, 128], BF16, kind="Internal").ap()
    pts = nc.dram_tensor("pts4", [NPAIR, 128, PEW2], BF16, kind="Internal").ap()
    rts_t = nc.dram_tensor("rts4", [2, 128, SLW], BF16, kind="Internal")
    rts = rts_t.ap()
    us4 = nc.dram_tensor("us4", [MY, D], F32, kind="Internal").ap()
    x3ms = nc.dram_tensor("x3ms", [MY, D], F32, kind="Internal").ap()
    aoTs4 = nc.dram_tensor("aoTs4", [128, ND * MY], BF16, kind="Internal").ap()
    X2hTs = nc.dram_tensor("X2hTs", [128, ND * MY], BF16, kind="Internal").ap()

    with ExitStack() as ctx:
        def sb(name, shape, dtype, c=None):
            return (c or ctx).enter_context(nc.sbuf_tensor("d" + name, shape, dtype))
        def psum(name, shape, dtype):
            return ctx.enter_context(nc.psum_tensor("d" + name, shape, dtype))
        def sem(name):
            return ctx.enter_context(nc.semaphore("d" + name))

        ident = sb("ident4", [128, 128], BF16)
        onesrF = sb("onesrF4", [1, 128], F32)
        epscol = sb("epscol4", [128, 1], F32)
        m1 = sb("m14", [128, 1], F32)
        m2 = sb("m24", [128, 1], F32)
        acc = sb("acc4", [128, D], F32)
        sqd = sb("sqd4", [128, D], F32)
        xin = sb("xin4", [128, 2 * D], F32)
        xinb = sb("xinb4", [128, 2 * D], BF16)
        rows1 = sb("rows14", [1, D], F32)
        gcast1 = sb("gcast14", [128, D], F32)
        bcast1 = sb("bcast14", [128, D], F32)
        gcast2 = sb("gcast24", [128, D], F32)
        bcast2 = sb("bcast24", [128, D], F32)
        gates = sb("gates4", [128, NTM4 * 32], F32)
        selb = sb("selb", [128, 64], F32)
        selc = sb("selc", [128, 40], F32)
        sel0sb = sb("sel0sb", [128, 128], F32)
        sel1sb = sb("sel1sb", [128, 128], F32)
        xselb = sb("xselb", [128, D], F32)

        phA = ExitStack()
        XhT = sb("XhT", [128, ND * S], BF16, phA)
        XhMT = sb("XhMT", [128, ND * MY], BF16, phA)
        wsl = sb("wsl", [128, ND * 512], BF16, phA)
        peb = sb("peb", [128, 2 * ND * 512], BF16, phA)
        qTp4 = sb("qTp4", [128, MY], BF16, phA)
        kTp4 = sb("kTp4", [128, S], BF16, phA)
        vpb = sb("vpb", [128, NT * 128], BF16, phA)
        PThp = sb("PThp", [128, PEW2], BF16, phA)

        pA = psum("pA4", [128, 512], F32)
        pB = psum("pB4", [128, 512], F32)
        pO = psum("pO4", [64, 128], F32)
        pO2 = psum("pO24", [64, 128], F32)
        pQ1 = psum("pQ14", [128, 128], BF16)
        pQ2 = psum("pQ24", [128, 128], BF16)
        pQ3 = psum("pQ34", [128, 128], BF16)
        pQ4 = psum("pQ44", [128, 128], BF16)
        pT1, pT2 = pQ1, pQ2

        if F:
            E = F["E"]; p = F["p"]
        else:
            s_t0 = sem("s_t4"); s_v0 = sem("s_v4"); s_s0 = sem("s_s4"); s_g0 = sem("s_g4")
            p = Prog()
            E = Emit(p, {"t": s_t0, "v": s_v0, "s": s_s0, "g": s_g0}, nc=nc, ctx=ctx)
        s_t = E.c["t"].sem; s_v = E.c["v"].sem; s_s = E.c["s"].sem; s_g = E.c["g"].sem

        E.dma("gpsimd", "gC", lambda e: e.dma_start(out=ident[:], in_=cId_h[:]))
        E.dma("gpsimd", "gC", lambda e: e.dma_start(out=onesrF[:], in_=cOR_h[:]))
        E.dma("gpsimd", "gC", lambda e: e.dma_start(out=sel0sb[:], in_=sel0_h[:]))
        E.dma("gpsimd", "gC", lambda e: e.dma_start(out=sel1sb[:], in_=sel1_h[:]))
        ld_g = E.c["gC"].val
        E.on("vector", lambda e: e.memset(epscol[:], LN_EPS))
        E.wait("vector", "gC", ld_g)
        E.milestone("vector", "v")
        E.wait("tensor", "gC", ld_g)

        for row_h, dstc in ((g1_h, gcast1), (b1_h, bcast1), (g2_h, gcast2),
                            (b2_h, bcast2)):
            E.wait("sync", "t", E.c["t"].val)
            ldr = E.dma("sync", "sC", lambda e, row_h=row_h: e.dma_start(
                out=rows1[0:1, 0:D], in_=row_h[:]))
            E.wait("tensor", "sC", ldr)
            emit_bcast_row(E, rows1[0:1, 0:D], dstc, onesrF, pA)

        def emit_ln_transpose(src_h, ntiles, dstT, SW):
            xin_free = {}
            trans_done = {}
            bev = {}
            for tt in range(ntiles):
                bo = (tt % 2) * D
                grpn = ("xA", "xB")[tt % 2]
                if tt >= 2:
                    E.wait("sync", "v", xin_free[tt - 2])
                else:
                    E.wait("sync", "v", E.c["v"].val)
                    E.wait("sync", "t", E.c["t"].val)
                ldx = E.dma("sync", grpn, lambda e, tt=tt, bo=bo: e.dma_start(
                    out=xin[:, bo:bo + D], in_=src_h[tt * 128:(tt + 1) * 128, :]))
                E.wait("vector", grpn, ldx)
                if tt < 2:
                    E.wait("vector", "t", E.c["t"].val)
                def vld(e, bo=bo):
                    e.tensor_copy(acc[:, 0:D], xin[:, bo:bo + D])
                E.on("vector", vld)
                xin_free[tt] = E.milestone("vector", "v")
                if tt >= 2:
                    E.wait("vector", "t", trans_done[tt - 2])
                lnm = emit_ln_tile(E, acc, m1, m2, sqd, epscol, gcast1, bcast1,
                                   out_bf=xinb[:, bo:bo + D])
                E.wait("tensor", "v", lnm)
                for dt in range(ND):
                    pt = (pT1, pT2)[dt % 2]
                    bt_wait(E, bev, dt % 2)
                    def trx(e, bo=bo, dt=dt, pt=pt):
                        e.transpose(pt[:], xinb[:, bo + dt * 128:bo + (dt + 1) * 128],
                                    ident[:])
                    E.on("tensor", trx)
                    tms = E.milestone("tensor", "t")
                    E.wait("vector", "t", tms)
                    def evx(e, tt=tt, dt=dt, pt=pt, dstT=dstT, SW=SW):
                        e.tensor_copy(dstT[:, dt * SW + tt * 128:dt * SW + (tt + 1) * 128],
                                      pt[:])
                    E.on("vector", evx)
                    bt_set(bev, dt % 2, "v", E.milestone("vector", "v"))
                trans_done[tt] = E.c["t"].val
            E.wait("tensor", "v", E.c["v"].val)
        emit_ln_transpose(x3_h, NT, XhT, S)

        E.dma_group("x3m")
        bevA = {}
        selp_tr = {}
        x3m_st = 0
        trm_last = 0
        for tp in range(NTM4):
            if tp >= 1:
                E.wait("sync", "t", selp_tr[tp - 1])
            else:
                E.wait("sync", "v", E.c["v"].val)
                E.wait("sync", "t", E.c["t"].val)
            lda = E.dma("sync", "xA", lambda e, tp=tp: e.dma_start(
                out=xin[:, 0:D], in_=x3_h[tp * 128:(tp + 1) * 128, :]))
            ldb = E.dma("sync", "xB", lambda e, tp=tp: e.dma_start(
                out=xin[:, D:2 * D], in_=x3_h[(tp + 16) * 128:(tp + 17) * 128, :]))
            E.wait("tensor", "xA", lda)
            E.wait("tensor", "xB", ldb)
            if tp == 0:
                E.wait("tensor", "v", E.c["v"].val)
            for ch in range(2):
                ps = (pA, pB)[ch]
                def mmsel4(e, ch=ch, ps=ps):
                    e.matmul(ps[:, :], sel0sb[:], xin[:, ch * 512:(ch + 1) * 512],
                             start=True, stop=False)
                    e.matmul(ps[:, :], sel1sb[:],
                             xin[:, D + ch * 512:D + (ch + 1) * 512],
                             start=False, stop=True)
                E.on("tensor", mmsel4)
            selp_tr[tp] = E.milestone("tensor", "t")
            E.wait("vector", "t", selp_tr[tp])
            E.wait("vector", "x3m", x3m_st)
            def evsel(e):
                e.raw.tensor_copy(acc[:, 0:512], pA[:, :])
                e.tensor_copy(acc[:, 512:1024], pB[:, :])
            E.on("vector", evsel)
            evm = E.milestone("vector", "v")
            E.wait("tensor", "v", evm)
            E.wait("sync", "v", evm)
            x3m_st = E.dma("sync", "x3m", lambda e, tp=tp: e.dma_start(
                out=x3ms[tp * 128:(tp + 1) * 128, :], in_=acc[:, 0:D]))
            E.wait("vector", "x3m", x3m_st)
            E.wait("vector", "t", trm_last)
            lnm = emit_ln_tile(E, acc, m1, m2, sqd, epscol, gcast1, bcast1,
                               out_bf=xinb[:, 0:D])
            E.wait("tensor", "v", lnm)
            for dt in range(ND):
                pt = (pT1, pT2)[dt % 2]
                bt_wait(E, bevA, dt % 2)
                def trxm(e, dt=dt, pt=pt):
                    e.transpose(pt[:], xinb[:, dt * 128:(dt + 1) * 128], ident[:])
                E.on("tensor", trxm)
                tms = E.milestone("tensor", "t")
                E.wait("vector", "t", tms)
                def evxm(e, tp=tp, dt=dt, pt=pt):
                    e.tensor_copy(
                        XhMT[:, dt * MY + tp * 128:dt * MY + (tp + 1) * 128],
                        pt[:])
                E.on("vector", evxm)
                bt_set(bevA, dt % 2, "v", E.milestone("vector", "v"))
            trm_last = E.c["t"].val
        E.wait("tensor", "v", E.c["v"].val)

        for pr in range(NPAIR):
            bev = {}
            lw = 0
            for wi, w_h in enumerate((wq_h, wk_h, wv_h, wp_h)):
                lw = E.dma("gpsimd", "wpr", lambda e, wi=wi, w_h=w_h, pr=pr:
                           e.dma_start(
                    out=wsl[:].rearrange("p (n m) -> p n m", n=ND)[:, :, wi * 128:(wi + 1) * 128],
                    in_=w_h.rearrange("(n p) m -> p n m", p=128)[:, :, pr * 128:(pr + 1) * 128]))
            E.wait("tensor", "wpr", lw)
            for mc in range(MY // 512):
                ps = (pA, pB)[mc % 2]
                bt_wait(E, bev, mc % 2)
                def mmq4(e, mc=mc, ps=ps):
                    for kt in range(ND):
                        e.matmul(ps[:, :], wsl[:, kt * 512:kt * 512 + 128],
                                 XhMT[:, kt * MY + mc * 512:kt * MY + (mc + 1) * 512],
                                 start=(kt == 0), stop=(kt == ND - 1))
                E.on("tensor", mmq4)
                tms = E.milestone("tensor", "t")
                E.wait("vector", "t", tms)
                def evq4(e, mc=mc, ps=ps):
                    e.tensor_copy(qTp4[:, mc * 512:(mc + 1) * 512], ps[:, :])
                E.on("vector", evq4)
                bt_set(bev, mc % 2, "v", E.milestone("vector", "v"))
            vq = E.c["v"].val
            E.on("gpsimd", lambda e, vq=vq: e.wait_ge(E.c["v"].sem, vq))
            sq = E.dma("gpsimd", "kq", lambda e, pr=pr: e.dma_start(
                out=qts[pr], in_=qTp4[:, :]))
            for sc_ in range(S // 512):
                ps = (pA, pB)[sc_ % 2]
                bt_wait(E, bev, sc_ % 2)
                def mmk4(e, sc_=sc_, ps=ps):
                    for kt in range(ND):
                        e.matmul(ps[:, :], wsl[:, kt * 512 + 128:kt * 512 + 256],
                                 XhT[:, kt * S + sc_ * 512:kt * S + (sc_ + 1) * 512],
                                 start=(kt == 0), stop=(kt == ND - 1))
                E.on("tensor", mmk4)
                tms = E.milestone("tensor", "t")
                E.wait("vector", "t", tms)
                def evk4(e, sc_=sc_, ps=ps):
                    e.tensor_copy(kTp4[:, sc_ * 512:(sc_ + 1) * 512], ps[:, :])
                E.on("vector", evk4)
                bt_set(bev, sc_ % 2, "v", E.milestone("vector", "v"))
            vk = E.c["v"].val
            E.on("gpsimd", lambda e, vk=vk: e.wait_ge(E.c["v"].sem, vk))
            sk = E.dma("gpsimd", "kq", lambda e, pr=pr: e.dma_start(
                out=kts[pr], in_=kTp4[:, :]))
            for tt in range(NT):
                ps = (pA, pB)[tt % 2]
                po = (tt % 2) * 128
                bt_wait(E, bev, tt % 2)
                def mmv4(e, tt=tt, ps=ps):
                    for kt in range(ND):
                        e.matmul(ps[:, 0:128],
                                 XhT[:, kt * S + tt * 128:kt * S + (tt + 1) * 128],
                                 wsl[:, kt * 512 + 256:kt * 512 + 384],
                                 start=(kt == 0), stop=(kt == ND - 1))
                E.on("tensor", mmv4)
                tms = E.milestone("tensor", "t")
                E.wait("vector", "t", tms)
                def evv4(e, tt=tt, ps=ps):
                    e.tensor_copy(vpb[:, tt * 128:(tt + 1) * 128], ps[:, 0:128])
                E.on("vector", evv4)
                bt_set(bev, tt % 2, "v", E.milestone("vector", "v"))
            vv = E.c["v"].val
            E.on("gpsimd", lambda e, vv=vv: e.wait_ge(E.c["v"].sem, vv))
            sv = E.dma("gpsimd", "kq", lambda e, pr=pr: e.dma_start(
                out=vts[pr].rearrange("(n p) m -> p n m", p=128), in_=vpb[:].rearrange("p (n m) -> p n m", n=NT)))
            pe_t_hist = {}
            for pc in range(PEW2 // 512):
                slot = (pc % 2) * ND * 512
                pegrp = ("pe0", "pe1")[pc % 2]
                if pc >= 2:
                    E.on("gpsimd", lambda e, v=pe_t_hist[pc - 2]:
                         e.wait_ge(E.c["t"].sem, v))
                lpe = E.dma("gpsimd", pegrp, lambda e, pc=pc, slot=slot: e.dma_start(
                    out=peb[:, slot:slot + ND * 512].rearrange("p (n m) -> p n m", n=ND),
                    in_=pes_h.rearrange("(n p) m -> p n m", p=128)[:, :, pc * 512:(pc + 1) * 512]))
                E.wait("tensor", pegrp, lpe)
                ps = (pA, pB)[pc % 2]
                bt_wait(E, bev, pc % 2)
                def mmp4(e, pc=pc, ps=ps, slot=slot):
                    for kt in range(ND):
                        e.matmul(ps[:, :], wsl[:, kt * 512 + 384:kt * 512 + 512],
                                 peb[:, slot + kt * 512:slot + (kt + 1) * 512],
                                 start=(kt == 0), stop=(kt == ND - 1))
                E.on("tensor", mmp4)
                tms = E.milestone("tensor", "t")
                pe_t_hist[pc] = tms
                E.wait("vector", "t", tms)
                def evp4(e, pc=pc, ps=ps):
                    e.tensor_copy(PThp[:, pc * 512:(pc + 1) * 512], ps[:, :])
                E.on("vector", evp4)
                bt_set(bev, pc % 2, "v", E.milestone("vector", "v"))
            vpth = E.c["v"].val
            E.on("gpsimd", lambda e, vpth=vpth: e.wait_ge(E.c["v"].sem, vpth))
            sp = E.dma("gpsimd", "kq", lambda e, pr=pr: e.dma_start(
                out=pts[pr], in_=PThp[:, :]))
            E.wait("vector", "kq", E.c["kq"].val)
            E.wait("tensor", "kq", E.c["kq"].val)

        emit_barrier(E)
        phA.close()
        phB12 = ExitStack()
        wob = sb("wob", [128, ND * D], BF16, phB12)
        selwb = sb("selwb", [128, ND * 32], BF16, phB12)
        phB1 = ExitStack()
        kTh4 = sb("kTh4", [64, 2 * S], BF16, phB1)
        qTh4 = sb("qTh4", [64, 2 * MY], BF16, phB1)
        aosl4 = sb("aosl4", [64, 256], BF16, phB1)
        PTh4 = sb("PTh4", [64, 2 * PEW2], BF16, phB1)
        vh4 = sb("vh4", [128, 2 * NT * 64], BF16, phB1)
        Rbuf = sb("Rbuf", [128, SLW], BF16, phB1)
        scb = sb("scb", [128, 2 * S], F32, phB1)
        Eb = sb("Eb", [128, 2 * S], BF16, phB1)
        bdb = sb("bdb", [128, 2 * S], BF16, phB1)
        ETt = sb("ETt", [128, 256], BF16, phB1)
        rs = sb("rs4", [128, 2], F32, phB1)
        mx4 = sb("mx4", [128, 2], F32, phB1)

        head_guard = {}
        bev = {}
        pq_ev = {}
        wao_hist = {}
        rd_hist = {}
        bd_guard = {}
        ldh_val = {}
        st = {"last_wR": None, "scm_prev": 0, "trE_last": 0}

        def emit_head_loads(h):
            pr, sub = h // 2, h % 2
            hb = h % 2
            hgrp = ("hA", "hB")[hb]
            if h >= 2:
                E.wait("sync", "t", head_guard[h - 2])
            E.dma("sync", hgrp, lambda e, pr=pr, sub=sub, hb=hb: e.dma_start(
                out=kTh4[:, hb * S:(hb + 1) * S],
                in_=kts[pr, sub * 64:(sub + 1) * 64, :]))
            E.dma("sync", hgrp, lambda e, pr=pr, sub=sub, hb=hb: e.dma_start(
                out=qTh4[:, hb * MY:(hb + 1) * MY],
                in_=qts[pr, sub * 64:(sub + 1) * 64, :]))
            E.dma("sync", hgrp, lambda e, pr=pr, sub=sub, hb=hb: e.dma_start(
                out=PTh4[:, hb * PEW2:(hb + 1) * PEW2],
                in_=pts[pr, sub * 64:(sub + 1) * 64, :]))
            ldh_val[h] = E.dma("sync", hgrp, lambda e, pr=pr, sub=sub, hb=hb:
                               e.dma_start(
                out=vh4[:, hb * NT * 64:(hb + 1) * NT * 64].rearrange(
                    "p (n m) -> p n m", n=NT),
                in_=vts[pr].rearrange("(n p) m -> p n m", p=128)[:, :, sub * 64:(sub + 1) * 64]))

        def emit_R(h, qt):
            hb = h % 2
            qoff = hb * MY
            poff = hb * PEW2
            wbase = (15 - qt) * 128
            rslot = qt % 2
            rwgrp = ("rw0", "rw1")[rslot]
            bdgrp = ("bd0", "bd1")[rslot]
            hgrp = ("hA", "hB")[hb]
            if qt == 0:
                E.wait("tensor", hgrp, ldh_val[h])
            if st["last_wR"] is not None:
                E.wait("vector", st["last_wR"][0], st["last_wR"][1])
            for rc in range(9):
                n0 = rc * 512
                n1 = min(n0 + 512, SLW)
                ps = (pA, pB)[rc % 2]
                bt_wait(E, bev, rc % 2)
                def mmr(e, qt=qt, qoff=qoff, poff=poff, wbase=wbase,
                        n0=n0, n1=n1, ps=ps):
                    e.matmul(ps[:, 0:n1 - n0],
                             qTh4[:, qoff + qt * 128:qoff + (qt + 1) * 128],
                             PTh4[:, poff + wbase + n0:poff + wbase + n1],
                             start=True, stop=True)
                E.on("tensor", mmr)
                tms = E.milestone("tensor", "t")
                E.wait("vector", "t", tms)
                def evr(e, n0=n0, n1=n1, ps=ps):
                    e.tensor_copy(Rbuf[:, n0:n1], ps[:, 0:n1 - n0])
                E.on("vector", evr)
                bt_set(bev, rc % 2, "v", E.milestone("vector", "v"))
            vr = E.c["v"].val
            E.on("gpsimd", lambda e, vr=vr: e.wait_ge(E.c["v"].sem, vr))
            if rslot in bd_guard:
                E.on("gpsimd", lambda e, v=bd_guard[rslot]:
                     e.wait_ge(E.c["v"].sem, v))
            wR = E.dma("gpsimd", rwgrp, lambda e, rslot=rslot: e.dma_start(
                out=rts[rslot], in_=Rbuf[:, :]))
            st["last_wR"] = (rwgrp, wR)
            E.wait("sync", rwgrp, wR)
            rd_hist[rslot] = E.dma("sync", bdgrp, lambda e, rslot=rslot:
                                   e.dma_start(
                out=bdb[:, rslot * S:(rslot + 1) * S],
                in_=bass.AP(tensor=rts_t, offset=rslot * 128 * SLW + 127,
                            ap=[[SLW - 1, 128], [1, S]])))

        scm_hist = {0: 0, 1: 0}
        eb_guard = {0: 0, 1: 0}

        def emit_SC(h, qt):
            hb = h % 2
            qoff = hb * MY
            koff = hb * S
            rslot = qt % 2
            so = rslot * S
            bdgrp = ("bd0", "bd1")[rslot]
            E.wait("vector", "s", scm_hist[rslot])
            for ch in range(8):
                ps = (pA, pB)[ch % 2]
                bt_wait(E, bev, ch % 2)
                def mmac(e, qt=qt, qoff=qoff, koff=koff, ch=ch, ps=ps):
                    e.matmul(ps[:, :],
                             qTh4[:, qoff + qt * 128:qoff + (qt + 1) * 128],
                             kTh4[:, koff + ch * 512:koff + (ch + 1) * 512],
                             start=True, stop=True)
                E.on("tensor", mmac)
                tms = E.milestone("tensor", "t")
                E.wait("vector", "t", tms)
                if ch == 0:
                    E.wait("vector", bdgrp, rd_hist[rslot])
                def evac_(e, ch=ch, ps=ps, rslot=rslot, so=so):
                    e.scalar_tensor_tensor(scb[:, so + ch * 512:so + (ch + 1) * 512],
                                           ps[:, :], 1.0,
                                           bdb[:, rslot * S + ch * 512:rslot * S + (ch + 1) * 512],
                                           ALU.mult, ALU.add)
                E.on("vector", evac_)
                bt_set(bev, ch % 2, "v", E.milestone("vector", "v"))
            bd_guard[rslot] = E.c["v"].val
            def vmax4(e, so=so, rslot=rslot):
                e.reduce_max(mx4[:, rslot:rslot + 1], scb[:, so:so + 512],
                             axis=AX.X)
                e.tensor_scalar_mul(mx4[:, rslot:rslot + 1],
                                    mx4[:, rslot:rslot + 1], -0.125)
            E.on("vector", vmax4)
            vmx = E.milestone("vector", "v")
            E.wait("scalar", "v", vmx)
            E.wait("scalar", "t", eb_guard[rslot])
            def sexp4(e, so=so, rslot=rslot):
                e.activation(scb[:, so:so + S], scb[:, so:so + S], AF.Exp,
                             bias=mx4[:, rslot:rslot + 1], scale=0.125,
                             accum_out=rs[:, rslot:rslot + 1])
            E.on("scalar", sexp4)
            sms = E.milestone("scalar", "s")
            E.wait("vector", "s", sms)
            E.on("vector", lambda e, rslot=rslot: e.reciprocal(
                rs[:, rslot:rslot + 1], rs[:, rslot:rslot + 1]))
            vrc = E.milestone("vector", "v")
            E.wait("scalar", "v", vrc)
            def scast4(e, so=so, rslot=rslot):
                e.activation(Eb[:, so:so + S], scb[:, so:so + S], AF.Copy,
                             scale=rs[:, rslot:rslot + 1])
            E.on("scalar", scast4)
            scm_hist[rslot] = E.milestone("scalar", "s")

        def emit_pairs(h, qt):
            hb = h % 2
            voff = hb * NT * 64
            rslot = qt % 2
            so = rslot * S
            E.wait("tensor", "s", scm_hist[rslot])
            pol = (pO, pO2)[qt % 2]
            for jt in range(NT):
                pq = (pQ1, pQ2, pQ3, pQ4)[jt % 4]
                if (jt % 4) in pq_ev:
                    cn, v = pq_ev[jt % 4]
                    E.wait("tensor", cn, v)
                if jt >= 1:
                    cn, v = pq_ev[(jt - 1) % 4]
                    E.wait("tensor", cn, v)
                if jt == 0:
                    bt_wait(E, bev, "pO%d" % (qt % 2))
                def pair(e, jt=jt, pq=pq, voff=voff, pol=pol, so=so):
                    e.transpose(pq[:], Eb[:, so + jt * 128:so + (jt + 1) * 128],
                                ident[:])
                    if jt >= 1:
                        eslot = ((jt - 1) % 2) * 128
                        e.matmul(pol[:, :],
                                 vh4[:, voff + (jt - 1) * 64:voff + jt * 64],
                                 ETt[:, eslot:eslot + 128],
                                 start=(jt == 1), stop=False)
                E.on("tensor", pair)
                tms = E.milestone("tensor", "t")
                E.wait("vector", "t", tms)
                eslot = (jt % 2) * 128
                def evE(e, pq=pq, eslot=eslot):
                    e.tensor_copy(ETt[:, eslot:eslot + 128], pq[:])
                E.on("vector", evE)
                pq_ev[jt % 4] = ("v", E.milestone("vector", "v"))
            cn, v = pq_ev[(NT - 1) % 4]
            E.wait("tensor", cn, v)
            def mmov_last(e, voff=voff, pol=pol):
                eslot = ((NT - 1) % 2) * 128
                e.matmul(pol[:, :],
                         vh4[:, voff + (NT - 1) * 64:voff + NT * 64],
                         ETt[:, eslot:eslot + 128],
                         start=False, stop=True)
            E.on("tensor", mmov_last)
            tcomp = E.milestone("tensor", "t")
            eb_guard[rslot] = tcomp
            E.wait("vector", "t", tcomp)
            aslot = (qt % 2) * 128
            aogrp = ("ao40", "ao41")[qt % 2]
            if (qt % 2) in wao_hist:
                E.wait("vector", aogrp, wao_hist[qt % 2])
            def evo4(e, pol=pol, aslot=aslot):
                e.tensor_copy(aosl4[:, aslot:aslot + 128], pol[:, :])
            E.on("vector", evo4)
            vms = E.milestone("vector", "v")
            bt_set(bev, "pO%d" % (qt % 2), "v", vms)
            E.on("gpsimd", lambda e, vms=vms: e.wait_ge(E.c["v"].sem, vms))
            wao = E.dma("gpsimd", aogrp, lambda e, h=h, qt=qt, aslot=aslot:
                        e.dma_start(
                out=bass.AP(tensor=aoTs4.tensor,
                            offset=(h % 2) * 64 * (ND * MY) + (h // 2) * MY + qt * 128,
                            ap=[[ND * MY, 64], [1, 128]]),
                in_=aosl4[:, aslot:aslot + 128]))
            wao_hist[qt % 2] = wao

        pairs4 = [(h, qt) for h in range(NH) for qt in range(NTM4)]
        emit_head_loads(0)
        emit_head_loads(1)
        emit_R(*pairs4[0])
        emit_R(*pairs4[1])
        emit_SC(*pairs4[0])
        for i, (h, qt) in enumerate(pairs4):
            if i + 2 < len(pairs4):
                emit_R(*pairs4[i + 2])
            if i + 1 < len(pairs4):
                emit_SC(*pairs4[i + 1])
            emit_pairs(h, qt)
            if qt == NTM4 - 1:
                head_guard[h] = E.c["t"].val
                if h + 2 < NH:
                    emit_head_loads(h + 2)

        emit_barrier(E)
        phB1.close()
        aog4 = sb("aog4", [128, ND * 128], BF16, phB12)
        x2stg = sb("x2stg", [128, ND * 128], BF16, phB12)
        lwo = E.dma("gpsimd", "wo4g", lambda e: e.dma_start(
            out=wob[:].rearrange("p (n m) -> p n m", n=ND),
            in_=wo_h.rearrange("(n p) m -> p n m", p=128)))
        lsw = E.dma("gpsimd", "wo4g", lambda e: e.dma_start(
            out=selwb[:].rearrange("p (n m) -> p n m", n=ND),
            in_=selw_h.rearrange("(n p) m -> p n m", p=128)))
        E.wait("tensor", "wo4g", lsw)
        res_ms = {}
        mmop_ms = {}
        for tt in range(NTM4):
            bo = (tt % 2) * D
            grpn = ("xA", "xB")[tt % 2]
            if tt >= 2:
                E.wait("sync", "v", res_ms[tt - 2])
            else:
                E.wait("sync", "v", E.c["v"].val)
            ldx = E.dma("sync", grpn, lambda e, tt=tt, bo=bo: e.dma_start(
                out=xin[:, bo:bo + D], in_=x3ms[tt * 128:(tt + 1) * 128, :]))
            if (tt - 1) in mmop_ms:
                E.on("gpsimd", lambda e, v=mmop_ms[tt - 1]: e.wait_ge(E.c["t"].sem, v))
            lao = E.dma("gpsimd", "aol4", lambda e, tt=tt: e.dma_start(
                out=aog4[:].rearrange("p (n m) -> p n m", n=ND),
                in_=aoTs4.rearrange("p (n m) -> p n m", n=ND)[:, :, tt * 128:(tt + 1) * 128]))
            E.wait("tensor", "aol4", lao)
            for ch in range(2):
                ps = (pA, pB)[ch]
                def mmwo(e, tt=tt, ch=ch, ps=ps):
                    for kt in range(ND):
                        e.matmul(ps[:, :],
                                 aog4[:, kt * 128:(kt + 1) * 128],
                                 wob[:, kt * D + ch * 512:kt * D + (ch + 1) * 512],
                                 start=(kt == 0), stop=(kt == ND - 1))
                E.on("tensor", mmwo)
            tms = E.milestone("tensor", "t")
            mmop_ms[tt] = tms
            E.wait("vector", "t", tms)
            E.wait("vector", grpn, ldx)
            if "us" in E.c:
                E.wait("vector", "us", E.c["us"].val)
            def vsrc2(e, bo=bo):
                e.scalar_tensor_tensor(acc[:, 0:512], xin[:, bo:bo + 512], 1.0,
                                       pA[:, :], ALU.mult, ALU.add)
                e.scalar_tensor_tensor(acc[:, 512:1024], xin[:, bo + 512:bo + D],
                                       1.0, pB[:, :], ALU.mult, ALU.add)
            E.on("vector", vsrc2)
            vsm = E.milestone("vector", "v")
            E.wait("tensor", "v", vsm)
            E.wait("sync", "v", vsm)
            E.dma("sync", "us", lambda e, tt=tt: e.dma_start(
                out=us4[tt * 128:(tt + 1) * 128, :], in_=acc[:, 0:D]))
            E.wait("vector", "us", E.c["us"].val)
            lnm = emit_ln_tile(E, acc, m1, m2, sqd, epscol, gcast2, bcast2,
                               out_bf=xinb[:, 0:D])
            E.wait("tensor", "v", lnm)
            if "x2w" in E.c:
                E.wait("vector", "x2w", E.c["x2w"].val)
            bev2 = {}
            for dt in range(ND):
                pt = (pT1, pT2)[dt % 2]
                bt_wait(E, bev2, dt % 2)
                def trx2(e, dt=dt, pt=pt):
                    e.transpose(pt[:], xinb[:, dt * 128:(dt + 1) * 128], ident[:])
                E.on("tensor", trx2)
                tms = E.milestone("tensor", "t")
                E.wait("vector", "t", tms)
                def evx2(e, dt=dt, pt=pt):
                    e.tensor_copy(x2stg[:, dt * 128:(dt + 1) * 128], pt[:])
                E.on("vector", evx2)
                bt_set(bev2, dt % 2, "v", E.milestone("vector", "v"))
            x2v = E.c["v"].val
            E.wait("tensor", "v", x2v)
            E.on("gpsimd", lambda e, x2v=x2v: e.wait_ge(E.c["v"].sem, x2v))
            E.dma("gpsimd", "x2w", lambda e, tt=tt: e.dma_start(
                out=X2hTs.rearrange("p (n m) -> p n m", n=ND)[:, :, tt * 128:(tt + 1) * 128],
                in_=x2stg[:].rearrange("p (n m) -> p n m", n=ND)))
            def mmsel(e, tt=tt):
                for kt in range(ND):
                    e.matmul(pA[:, 0:32],
                             x2stg[:, kt * 128:(kt + 1) * 128],
                             selwb[:, kt * 32:(kt + 1) * 32],
                             start=(kt == 0), stop=(kt == ND - 1))
            E.on("tensor", mmsel)
            tms = E.milestone("tensor", "t")
            E.wait("scalar", "t", tms)
            sslot = (tt % 2) * 32
            if tt >= 2:
                E.wait("scalar", "v", res_ms[tt - 2])
            E.on("scalar", lambda e, sslot=sslot: e.activation(
                selb[:, sslot:sslot + 32], pA[:, 0:32], AF.Sigmoid))
            ssm = E.milestone("scalar", "s")
            E.wait("vector", "s", ssm)
            E.wait("tensor", "s", ssm)
            def vtop4(e, tt=tt, sslot=sslot):
                c = selc[:, 0:32]
                mx = selc[:, 32:33]
                e.tensor_copy(c, selb[:, sslot:sslot + 32])
                for it in range(3):
                    e.reduce_max(mx, c, axis=AX.X)
                    e.tensor_scalar(sqd[:, 0:32], c, mx, None, ALU.is_ge)
                    e.scalar_tensor_tensor(c, sqd[:, 0:32], -1e30, c,
                                           ALU.mult, ALU.add)
                e.reduce_max(mx, c, axis=AX.X)
                e.tensor_scalar(sqd[:, 0:32], selb[:, sslot:sslot + 32], mx,
                                None, ALU.is_ge)
                e.tensor_mul(gates[:, tt * 32:(tt + 1) * 32],
                             selb[:, sslot:sslot + 32], sqd[:, 0:32])
            E.on("vector", vtop4)
            res_ms[tt] = E.milestone("vector", "v")

        emit_barrier(E)
        phB12.close()
        kb = sb("kb4", [128, 2 * ND * 128], BF16)
        vb = sb("vb4", [128, 2 * D], BF16)
        hTe = sb("hTe", [128, 2 * 1024], BF16)
        yacc4 = sb("yacc4", [128, 8 * D], F32)
        X2hg = sb("X2hg", [128, ND * 1024], BF16)
        fin_ms = {}
        bevm = {}
        wk_t = {}
        for grp in range(MY // 1024):
            E.on("vector", lambda e: e.memset(yacc4[:, :], 0.0))
            ya0 = E.milestone("vector", "v")
            E.on("gpsimd", lambda e, v=E.c["t"].val: e.wait_ge(E.c["t"].sem, v))
            lx2 = E.dma("gpsimd", "x2l", lambda e, grp=grp: e.dma_start(
                out=X2hg[:].rearrange("p (n m) -> p n m", n=ND),
                in_=X2hTs.rearrange("p (n m) -> p n m", n=ND)[
                    :, :, grp * 1024:(grp + 1) * 1024]))
            E.wait("tensor", "x2l", lx2)
            for ei in range(32):
                es = ei % 2
                kgrp = ("wmk0", "wmk1")[es]
                vgrp = ("wmv0", "wmv1")[es]
                if es in wk_t:
                    E.on("gpsimd", lambda e, v=wk_t[es]: e.wait_ge(E.c["t"].sem, v))
                lk = E.dma("gpsimd", kgrp, lambda e, ei=ei, es=es: e.dma_start(
                    out=kb[:, es * ND * 128:(es + 1) * ND * 128].rearrange(
                        "p (n m) -> p n m", n=ND),
                    in_=keys_h[ei].rearrange("(n p) m -> p n m", p=128)))
                lv = E.dma("gpsimd", vgrp, lambda e, ei=ei, es=es: e.dma_start(
                    out=vb[:, es * D:(es + 1) * D], in_=vals_h[ei]))
                E.wait("tensor", kgrp, lk)
                E.wait("tensor", vgrp, lv)
                hs = es * 1024
                for mc in range(2):
                    ps = (pA, pB)[mc]
                    bt_wait(E, bevm, mc)
                    def mmh4(e, mc=mc, ps=ps, es=es):
                        for kt in range(ND):
                            e.matmul(ps[:, :],
                                     kb[:, es * ND * 128 + kt * 128:es * ND * 128 + (kt + 1) * 128],
                                     X2hg[:, kt * 1024 + mc * 512:kt * 1024 + (mc + 1) * 512],
                                     start=(kt == 0), stop=(kt == ND - 1))
                    E.on("tensor", mmh4)
                    tms = E.milestone("tensor", "t")
                    E.wait("scalar", "t", tms)
                    def evh4(e, mc=mc, ps=ps, hs=hs):
                        e.activation(hTe[:, hs + mc * 512:hs + (mc + 1) * 512],
                                     ps[:, :], AF.Relu)
                    E.on("scalar", evh4)
                    bt_set(bevm, mc, "s", E.milestone("scalar", "s"))
                E.wait("tensor", "s", E.c["s"].val)
                if ei == 0:
                    E.wait("vector", "v", ya0)
                for tt8 in range(8):
                    for ch in range(2):
                        ps = (pA, pB)[ch]
                        bt_wait(E, bevm, ch)
                        def mmy4(e, tt8=tt8, ch=ch, ps=ps, hs=hs, es=es):
                            e.matmul(ps[:, :],
                                     hTe[:, hs + tt8 * 128:hs + (tt8 + 1) * 128],
                                     vb[:, es * D + ch * 512:es * D + (ch + 1) * 512],
                                     start=True, stop=True)
                        E.on("tensor", mmy4)
                        tms = E.milestone("tensor", "t")
                        E.wait("vector", "t", tms)
                        def evy4(e, grp=grp, ei=ei, tt8=tt8, ch=ch, ps=ps):
                            col = gates[:, (grp * 8 + tt8) * 32 + ei:
                                        (grp * 8 + tt8) * 32 + ei + 1]
                            e.scalar_tensor_tensor(
                                yacc4[:, tt8 * D + ch * 512:tt8 * D + (ch + 1) * 512],
                                ps[:, :], col,
                                yacc4[:, tt8 * D + ch * 512:tt8 * D + (ch + 1) * 512],
                                ALU.mult, ALU.add)
                        E.on("vector", evy4)
                        bt_set(bevm, ch, "v", E.milestone("vector", "v"))
                wk_t[es] = E.c["t"].val
            for tt8 in range(8):
                tt = grp * 8 + tt8
                bo = (tt % 2) * D
                grpn = ("xA", "xB")[tt % 2]
                E.wait("sync", "us", E.c["us"].val)
                if tt >= 2 and (tt - 2) in fin_ms:
                    E.wait("sync", "v", fin_ms[tt - 2])
                else:
                    E.wait("sync", "v", E.c["v"].val)
                ldu = E.dma("sync", grpn, lambda e, tt=tt, bo=bo: e.dma_start(
                    out=xin[:, bo:bo + D], in_=us4[tt * 128:(tt + 1) * 128, :]))
                E.wait("vector", grpn, ldu)
                if "out" in E.c:
                    E.wait("vector", "out", E.c["out"].val)
                def vfin1(e, tt8=tt8, bo=bo):
                    e.tensor_add(acc[:, 0:D], xin[:, bo:bo + D],
                                 yacc4[:, tt8 * D:(tt8 + 1) * D])
                E.on("vector", vfin1)
                v1m = E.milestone("vector", "v")
                E.wait("sync", "v", v1m)
                ldx3 = E.dma("sync", grpn, lambda e, tt=tt, bo=bo: e.dma_start(
                    out=xin[:, bo:bo + D], in_=x3ms[tt * 128:(tt + 1) * 128, :]))
                E.wait("vector", grpn, ldx3)
                E.on("vector", lambda e, bo=bo: e.tensor_add(acc[:, 0:D], acc[:, 0:D],
                                                             xin[:, bo:bo + D]))
                E.milestone("vector", "v")
                fin = emit_ln_tile(E, acc, m1, m2, sqd, epscol)
                fin_ms[tt] = fin
                E.wait("sync", "v", fin)
                E.dma("sync", "out", lambda e, tt=tt: e.dma_start(
                    out=out_h[tt * 128:(tt + 1) * 128, :], in_=acc[:, 0:D]))
        E.wait("sync", "out", E.c["out"].val)
        if not F:
            p.play(ctx.enter_context(nc.Block()))
    return nc

_BUILD_CACHE = {}


def _get(name, fn):
    if name not in _BUILD_CACHE:
        _BUILD_CACHE[name] = fn()
    return _BUILD_CACHE[name]


def kernel_multi(**inp):
    c1 = _consts_stage1()
    c3 = _consts_stage3()
    pes = _consts_stage4()
    f32 = lambda a: np.ascontiguousarray(np.asarray(a, dtype=np.float32))

    nc1 = _get("s1", build_stage1)
    maps = []
    for c in range(8):
        b = c // 2
        m = {"x": f32(inp["L0_attn_x"][b]), "Wq": f32(inp["L0_attn_Wq"]),
             "Wk": f32(inp["L0_attn_Wk"]), "Wv": f32(inp["L0_attn_Wv"])}
        m.update(c1)
        maps.append(m)
    r1 = run_bass_kernel_spmd(nc1, maps, core_ids=list(range(8)))
    x1 = np.stack([r1.results[2 * b]["out"] for b in range(4)])

    bi = np.asarray(inp["L0_ffn_in_proj_b"], np.float32)
    bicols = np.zeros((128, 16), np.float32)
    for pr in range(8):
        bicols[:, pr] = bi[pr * 128:(pr + 1) * 128]
        bicols[:, 8 + pr] = bi[1024 + pr * 128:1024 + (pr + 1) * 128]
    b1 = np.asarray(inp["L0_ffn_b1"], np.float32)
    b1cols = np.zeros((128, 128), np.float32)
    for e in range(8):
        for ft in range(16):
            b1cols[:, e * 16 + ft] = b1[e, ft * 128:(ft + 1) * 128]
    nc2 = _get("s2", build_stage2)
    maps = []
    for c in range(8):
        b, half = c // 2, c % 2
        m = {
            "x1": f32(x1[b]),
            "wi": f32(inp["L0_ffn_in_proj_w"]),
            "bicols": bicols,
            "bivrow": f32(bi[2048:].reshape(1, 1024)),
            "wo": f32(inp["L0_ffn_out_proj_w"]),
            "borow": f32(np.asarray(inp["L0_ffn_out_proj_b"]).reshape(1, 1024)),
            "ln1g": f32(np.asarray(inp["L0_ffn_ln1_g"]).reshape(1, 1024)),
            "ln1b": f32(np.asarray(inp["L0_ffn_ln1_b"]).reshape(1, 1024)),
            "ln2g": f32(np.asarray(inp["L0_ffn_ln2_g"]).reshape(1, 1024)),
            "ln2b": f32(np.asarray(inp["L0_ffn_ln2_b"]).reshape(1, 1024)),
            "wg": f32(inp["L0_ffn_w_group"]),
            "bg": f32(np.asarray(inp["L0_ffn_b_group"]).reshape(1, 4)),
            "wef": f32(np.asarray(inp["L0_ffn_w_expert"]).transpose(1, 0, 2).reshape(1024, 8)),
            "bef": f32(np.asarray(inp["L0_ffn_b_expert"]).reshape(1, 8)),
            "w1": f32(inp["L0_ffn_w1"]),
            "b1cols": b1cols,
            "w2": f32(inp["L0_ffn_w2"]),
            "b2rows": f32(np.asarray(inp["L0_ffn_b2"]).reshape(1, 8192)),
            "cIdent": c1["cIdent"], "cOnesRow": c1["cOnesRow"],
        }
        maps.append(m)
    r2 = run_bass_kernel_spmd(nc2, maps, core_ids=list(range(8)))
    x2 = np.stack([r2.results[2 * b]["out"] for b in range(4)])

    nc3 = _get("s3", build_stage3)
    maps = []
    for c in range(8):
        b = c // 2
        m = {
            "x2": f32(x2[b]),
            "wqkv": f32(inp["L1_attn_w_qkv"]),
            "wgated": f32(inp["L1_attn_w_gated"]),
            "wproj": f32(inp["L1_attn_w_proj"]),
            "gnw": f32(np.asarray(inp["L1_attn_gn_weight"]).reshape(1, 1024)),
            "gnb": f32(np.asarray(inp["L1_attn_gn_bias"]).reshape(1, 1024)),
            "cIdent": c1["cIdent"], "cOnesRow": c1["cOnesRow"],
            "cLTT": c1["cLTT"],
        }
        m.update(c3)
        maps.append(m)
    r3 = run_bass_kernel_spmd(nc3, maps, core_ids=list(range(8)))
    x3 = np.stack([r3.results[2 * b]["out"] for b in range(4)])

    nc4 = _get("s4", build_stage4)
    maps = []
    for c in range(8):
        b, half = c // 2, c % 2
        m = {
            "x3": f32(x3[b]),
            "wq4": f32(inp["L1_ffn_wq"]), "wk4": f32(inp["L1_ffn_wk"]),
            "wv4": f32(inp["L1_ffn_wv"]), "wp4": f32(inp["L1_ffn_wp"]),
            "wo4": f32(inp["L1_ffn_wo"]),
            "ln1g4": f32(np.asarray(inp["L1_ffn_ln1_g"]).reshape(1, 1024)),
            "ln1b4": f32(np.asarray(inp["L1_ffn_ln1_b"]).reshape(1, 1024)),
            "ln2g4": f32(np.asarray(inp["L1_ffn_ln2_g"]).reshape(1, 1024)),
            "ln2b4": f32(np.asarray(inp["L1_ffn_ln2_b"]).reshape(1, 1024)),
            "selw": f32(inp["L1_ffn_sel_w"]),
            "keys": f32(inp["L1_ffn_keys"]),
            "vals": f32(inp["L1_ffn_values"]),
            "peslice": pes[half],
            "cIdent": c1["cIdent"], "cOnesRow": c1["cOnesRow"],
        }
        maps.append(m)
    r4 = run_bass_kernel_spmd(nc4, maps, core_ids=list(range(8)))
    out = np.stack([r4.results[2 * b]["out"] for b in range(4)]).astype(np.float32)
    return out



def build_fused():
    nc = bass.Bass("TRN2", target_bir_lowering=False)
    dt_in = [
        ("x", [S, D]), ("Wq", [D, D]), ("Wk", [D, D]), ("Wv", [D, D]),
        ("cF", [D, FPAD]), ("cG", [FPAD, D]), ("cLTT", [128, 128]),
        ("cIdent", [128, 128]), ("cOnesCol", [128, 1]), ("cOnesRow", [1, 128]),
        ("wi", [3 * D, D]), ("bicols", [128, 16]), ("bivrow", [1, D]),
        ("wo", [D, D]), ("borow", [1, D]),
        ("ln1g", [1, D]), ("ln1b", [1, D]), ("ln2g", [1, D]), ("ln2b", [1, D]),
        ("wg", [D, 4]), ("bg", [1, 4]), ("wef", [D, 8]), ("bef", [1, 8]),
        ("w1", [NE0, D, FF]), ("b1cols", [128, NE0 * 16]),
        ("w2", [NE0, FF, D]), ("b2rows", [1, NE0 * D]),
        ("wqkv", [3 * D, D]), ("wgated", [D, D]), ("wproj", [D, D]),
        ("gnw", [1, D]), ("gnb", [1, D]),
        ("cCS", [S, 64]), ("cSS", [S, 64]), ("cCI", [S, 64]), ("cSI", [S, 64]),
        ("cVbar", [128, 16]), ("cEnd", [128, 16]), ("cE16", [16, D]),
        ("wq4", [D, D]), ("wk4", [D, D]), ("wv4", [D, D]), ("wp4", [D, D]),
        ("wo4", [D, D]),
        ("ln1g4", [1, D]), ("ln1b4", [1, D]), ("ln2g4", [1, D]), ("ln2b4", [1, D]),
        ("selw", [D, 32]), ("keys", [32, D, 128]), ("vals", [32, 128, D]),
        ("peslice", [D, PEW2]), ("sel0", [128, 128]), ("sel1", [128, 128]),
    ]
    F = {"nc": nc}
    for nm, shp in dt_in:
        F[nm] = nc.dram_tensor(nm, shp, F32, kind="ExternalInput").ap()
    out_h = nc.dram_tensor("out", [MY, D], F32, kind="ExternalOutput").ap()
    x1s = nc.dram_tensor("x1s", [S, D], F32, kind="Internal").ap()
    x2s = nc.dram_tensor("x2s", [S, D], F32, kind="Internal").ap()
    x3s = nc.dram_tensor("x3s", [S, D], F32, kind="Internal").ap()

    with ExitStack() as mctx:
        s_t = mctx.enter_context(nc.semaphore("g_t"))
        s_v = mctx.enter_context(nc.semaphore("g_v"))
        s_s = mctx.enter_context(nc.semaphore("g_s"))
        s_g = mctx.enter_context(nc.semaphore("g_g"))
        p = Prog()
        E = Emit(p, {"t": s_t, "v": s_v, "s": s_s, "g": s_g}, nc=nc, ctx=mctx)
        F["E"] = E
        F["p"] = p

        F1 = dict(F); F1["out"] = x1s
        build_stage1(F=F1)
        emit_barrier(E)
        F2 = dict(F); F2["x1"] = x1s; F2["out"] = x2s
        build_stage2(F=F2)
        emit_barrier(E)
        F3 = dict(F); F3["x2"] = x2s; F3["out"] = x3s
        build_stage3(F=F3)
        emit_barrier(E)
        F4 = dict(F); F4["x3"] = x3s; F4["out"] = out_h
        build_stage4(F=F4)
        p.play(mctx.enter_context(nc.Block()))
    return nc


def _fused_in_maps(inp):
    c1 = _consts_stage1()
    c3 = _consts_stage3()
    pes = _consts_stage4()
    f32 = lambda a: np.ascontiguousarray(np.asarray(a, dtype=np.float32))
    bi = np.asarray(inp["L0_ffn_in_proj_b"], np.float32)
    bicols = np.zeros((128, 16), np.float32)
    for pr in range(8):
        bicols[:, pr] = bi[pr * 128:(pr + 1) * 128]
        bicols[:, 8 + pr] = bi[1024 + pr * 128:1024 + (pr + 1) * 128]
    b1 = np.asarray(inp["L0_ffn_b1"], np.float32)
    b1cols = np.zeros((128, 128), np.float32)
    for e in range(8):
        for ft in range(16):
            b1cols[:, e * 16 + ft] = b1[e, ft * 128:(ft + 1) * 128]
    base = {
        "Wq": f32(inp["L0_attn_Wq"]), "Wk": f32(inp["L0_attn_Wk"]),
        "Wv": f32(inp["L0_attn_Wv"]),
        "wi": f32(inp["L0_ffn_in_proj_w"]), "bicols": bicols,
        "bivrow": f32(bi[2048:].reshape(1, 1024)),
        "wo": f32(inp["L0_ffn_out_proj_w"]),
        "borow": f32(np.asarray(inp["L0_ffn_out_proj_b"]).reshape(1, 1024)),
        "ln1g": f32(np.asarray(inp["L0_ffn_ln1_g"]).reshape(1, 1024)),
        "ln1b": f32(np.asarray(inp["L0_ffn_ln1_b"]).reshape(1, 1024)),
        "ln2g": f32(np.asarray(inp["L0_ffn_ln2_g"]).reshape(1, 1024)),
        "ln2b": f32(np.asarray(inp["L0_ffn_ln2_b"]).reshape(1, 1024)),
        "wg": f32(inp["L0_ffn_w_group"]),
        "bg": f32(np.asarray(inp["L0_ffn_b_group"]).reshape(1, 4)),
        "wef": f32(np.asarray(inp["L0_ffn_w_expert"]).transpose(1, 0, 2).reshape(1024, 8)),
        "bef": f32(np.asarray(inp["L0_ffn_b_expert"]).reshape(1, 8)),
        "w1": f32(inp["L0_ffn_w1"]), "b1cols": b1cols,
        "w2": f32(inp["L0_ffn_w2"]),
        "b2rows": f32(np.asarray(inp["L0_ffn_b2"]).reshape(1, 8192)),
        "wqkv": f32(inp["L1_attn_w_qkv"]),
        "wgated": f32(inp["L1_attn_w_gated"]),
        "wproj": f32(inp["L1_attn_w_proj"]),
        "gnw": f32(np.asarray(inp["L1_attn_gn_weight"]).reshape(1, 1024)),
        "gnb": f32(np.asarray(inp["L1_attn_gn_bias"]).reshape(1, 1024)),
        "wq4": f32(inp["L1_ffn_wq"]), "wk4": f32(inp["L1_ffn_wk"]),
        "wv4": f32(inp["L1_ffn_wv"]), "wp4": f32(inp["L1_ffn_wp"]),
        "wo4": f32(inp["L1_ffn_wo"]),
        "ln1g4": f32(np.asarray(inp["L1_ffn_ln1_g"]).reshape(1, 1024)),
        "ln1b4": f32(np.asarray(inp["L1_ffn_ln1_b"]).reshape(1, 1024)),
        "ln2g4": f32(np.asarray(inp["L1_ffn_ln2_g"]).reshape(1, 1024)),
        "ln2b4": f32(np.asarray(inp["L1_ffn_ln2_b"]).reshape(1, 1024)),
        "selw": f32(inp["L1_ffn_sel_w"]),
        "keys": f32(inp["L1_ffn_keys"]),
        "vals": f32(inp["L1_ffn_values"]),
    }
    base.update(c1)
    base.update(c3)
    maps = []
    for c in range(8):
        b, half = c // 2, c % 2
        m = dict(base)
        m["x"] = np.ascontiguousarray(np.asarray(inp["L0_attn_x"][b], np.float32))
        m["peslice"] = pes[half]
        m["sel0"], m["sel1"] = _consts_sel(half)
        maps.append(m)
    return maps


def kernel(**inp):
    nc = _get("fused", build_fused)
    maps = _fused_in_maps(inp)
    res = run_bass_kernel_spmd(nc, maps, core_ids=list(range(8)))
    out = np.stack([np.concatenate([res.results[2 * b]["out"],
                                    res.results[2 * b + 1]["out"]], axis=0)
                    for b in range(4)]).astype(np.float32)
    return out
